# version 02
# baseline (speedup 1.0000x reference)
import numpy as np
from contextlib import ExitStack

import concourse.bass as bass
import concourse.mybir as mybir
from concourse.bass import ds
from concourse.expressions import RuntimeValue as RV
from concourse.bass_utils import run_bass_kernel_spmd

F32 = mybir.dt.float32
BF16 = mybir.dt.bfloat16
I32 = mybir.dt.int32
AF = mybir.ActivationFunctionType
ALU = mybir.AluOpType

NCORES = 8
T = 1024
D = 2048
NF = 16
NT = 8
EPS = 1e-5
NEXP = 64
EGRP = 8
CAP = 1024

_N_PHASES = 99
_DEBUG_TAP = None
_DEBUG_H = 0
_DEBUG_KV = "k"
_LAST_RES = None


class Sem:

    def __init__(self, ctx, nc, name):
        self.h = ctx.enter_context(nc.semaphore(name))
        self.n = 0

    def inc(self, inst, k=1):
        inst.then_inc(self.h, k)
        self.n += k
        return inst

    def dma(self, inst):
        return self.inc(inst, 16)

    def wait(self, eng, n=None):
        eng.wait_ge(self.h, self.n if n is None else n)


def build_program():
    nc = bass.Bass("TRN2", target_bir_lowering=False, num_devices=NCORES)

    u = nc.dram_tensor("u", [T, D], F32, kind="ExternalInput").ap()
    wbr = nc.dram_tensor("wbr", [D, 64], F32, kind="ExternalInput").ap()
    wbs = nc.dram_tensor("wbs", [64, 256], F32, kind="ExternalInput").ap()
    wcr = nc.dram_tensor("wcr", [256, 64], F32, kind="ExternalInput").ap()
    wcm = nc.dram_tensor("wcm", [64, D], F32, kind="ExternalInput").ap()
    afac = nc.dram_tensor("afac", [1, 32], F32, kind="ExternalInput").ap()
    dvec = nc.dram_tensor("dvec", [NF, 128], F32, kind="ExternalInput").ap()
    wg = nc.dram_tensor("wg", [D, 64], F32, kind="ExternalInput").ap()
    wsu = nc.dram_tensor("wsu", [D, 1024], F32, kind="ExternalInput").ap()
    wsd = nc.dram_tensor("wsd", [512, D], F32, kind="ExternalInput").ap()
    wup = nc.dram_tensor("wup", [NEXP, D, 512], F32, kind="ExternalInput").ap()
    wdn = nc.dram_tensor("wdn", [NEXP, 256, D], F32, kind="ExternalInput").ap()

    wq1 = nc.dram_tensor("wq1", [D, D], F32, kind="ExternalInput").ap()
    wk1 = nc.dram_tensor("wk1", [D, D], F32, kind="ExternalInput").ap()
    wv1 = nc.dram_tensor("wv1", [D, D], F32, kind="ExternalInput").ap()
    wo1 = nc.dram_tensor("wo1", [D, D], F32, kind="ExternalInput").ap()
    bq1 = nc.dram_tensor("bq1", [16, 128], F32, kind="ExternalInput").ap()
    bk1 = nc.dram_tensor("bk1", [16, 128], F32, kind="ExternalInput").ap()
    bv1 = nc.dram_tensor("bv1", [16, 128], F32, kind="ExternalInput").ap()
    bo1 = nc.dram_tensor("bo1", [16, 128], F32, kind="ExternalInput").ap()

    cnt_in = nc.dram_tensor("cnt_in", [1, 64], F32)
    cnt_out = nc.dram_tensor("cnt_out", [NCORES, 64], F32)
    offs_dram = nc.dram_tensor("offs_dram", [NCORES, 64], F32)
    bar_in = nc.dram_tensor("bar_in", [1, 2], F32)
    bar_out = nc.dram_tensor("bar_out", [NCORES, 2], F32)
    cinfo = nc.dram_tensor("cinfo", [1, 8], F32, kind="ExternalInput").ap()
    sh_q = nc.dram_tensor("sh_q", [D, D], BF16, addr_space="Shared")
    sh_k = nc.dram_tensor("sh_k", [D, D], BF16, addr_space="Shared")
    sh_v = nc.dram_tensor("sh_v", [D, D], BF16, addr_space="Shared")
    sh_ctx = nc.dram_tensor("sh_ctx", [D, D], BF16, addr_space="Shared")

    wqg = nc.dram_tensor("wqg", [5632, D], I32, kind="ExternalInput").ap()
    wqu = nc.dram_tensor("wqu", [5632, D], I32, kind="ExternalInput").ap()
    wqd = nc.dram_tensor("wqd", [D, 5632], I32, kind="ExternalInput").ap()
    sg_in = nc.dram_tensor("sg_in", [44, 128], F32, kind="ExternalInput").ap()
    su_in = nc.dram_tensor("su_in", [44, 128], F32, kind="ExternalInput").ap()
    sd_in = nc.dram_tensor("sd_in", [16, 128], F32, kind="ExternalInput").ap()

    out = nc.dram_tensor("out", [T, D], F32, kind="ExternalOutput").ap()
    dbg = (nc.dram_tensor("dbg", [128, NF * T], F32, kind="ExternalOutput").ap()
           if _DEBUG_TAP else None)

    ctx = ExitStack()
    with ctx:
        E = ctx.enter_context

        x_fm = E(nc.sbuf_tensor("x_fm", [128, NF * T], BF16))
        z_bf = E(nc.sbuf_tensor("z_bf", [128, NF * T], BF16))
        ones_bf = E(nc.sbuf_tensor("ones_bf", [128, 128], BF16))
        ones_f32 = E(nc.sbuf_tensor("ones_f32", [128, 128], F32))
        ident_bf = E(nc.sbuf_tensor("ident_bf", [128, 128], BF16))
        ident_f32 = E(nc.sbuf_tensor("ident_f32", [128, 128], F32))
        iota_jp = E(nc.sbuf_tensor("iota_jp", [128, 128], I32))
        ln_rs = E(nc.sbuf_tensor("ln_rs", [128, T], F32))
        ln_mr = E(nc.sbuf_tensor("ln_mr", [128, T], F32))
        ln_rs_bf = E(nc.sbuf_tensor("ln_rs_bf", [128, T], BF16))
        ln_mr_bf = E(nc.sbuf_tensor("ln_mr_bf", [128, T], BF16))
        stat = E(nc.sbuf_tensor("stat", [128, 1024], F32))
        eps_col = E(nc.sbuf_tensor("eps_col", [128, 1], F32))

        psA = [E(nc.psum_tensor(f"psA{i}", [128, 512], F32)) for i in range(4)]
        psB = [E(nc.psum_tensor(f"psB{i}", [128, 1024], BF16)) for i in range(2)]
        psS = [E(nc.psum_tensor(f"psS{i}", [128, 512], F32)) for i in range(2)]

        ld = Sem(ctx, nc, "s_ld")
        mm = Sem(ctx, nc, "s_mm")
        vx = Sem(ctx, nc, "s_vx")
        ac = Sem(ctx, nc, "s_ac")
        io = Sem(ctx, nc, "s_io")
        ini = Sem(ctx, nc, "s_ini")
        wul = Sem(ctx, nc, "s_wul")
        wdl = Sem(ctx, nc, "s_wdl")
        upc = Sem(ctx, nc, "s_upc")
        dnc = Sem(ctx, nc, "s_dnc")
        cc = Sem(ctx, nc, "s_cc")

        block = E(nc.Block())

        r_pid = nc.gpsimd.alloc_register("r_pid")
        r_tmp = nc.gpsimd.alloc_register("r_tmp")

        ln_seq = [0]

        def emit_layernorm():
            ln_seq[0] += 1
            with nc.sbuf_tensor(f"zsq_ln{ln_seq[0]}", [128, NF * T], BF16) as zsq:
                @block.scalar
                def _(s):
                    vx.wait(s)
                    last = None
                    for f in range(NF):
                        last = s.activation(zsq[:, f * T:(f + 1) * T],
                                            z_bf[:, f * T:(f + 1) * T], AF.Square)
                    ac.inc(last)

                @block.tensor
                def _(tb):
                    vx.wait(tb)
                    ac.wait(tb)
                    last = None
                    for g in range(2):
                        for f in range(NF):
                            last = tb.matmul(
                                psS[g][:, :], ones_bf[:],
                                z_bf[:, f * T + g * 512: f * T + (g + 1) * 512],
                                start=(f == 0), stop=(f == NF - 1))
                    for g in range(2):
                        for f in range(NF):
                            last = tb.matmul(
                                psA[2 + g][:, :], ones_bf[:],
                                zsq[:, f * T + g * 512: f * T + (g + 1) * 512],
                                start=(f == 0), stop=(f == NF - 1))
                    mm.inc(last)

                @block.vector
                def _(v):
                    mm.wait(v)
                    for g in range(2):
                        sl = slice(g * 512, (g + 1) * 512)
                        v.tensor_scalar_mul(ln_mr[:, sl], psS[g][:, :], 1.0 / D)
                        v.tensor_scalar_mul(stat[:, sl], psA[2 + g][:, :], 1.0 / D)
                        v.tensor_mul(ln_rs[:, sl], ln_mr[:, sl], ln_mr[:, sl])
                        v.tensor_sub(stat[:, sl], stat[:, sl], ln_rs[:, sl])
                    vx.inc(v.engine_nop())

                @block.scalar
                def _(s):
                    vx.wait(s)
                    ac.inc(s.activation(ln_rs[:, :], stat[:, :], AF.Sqrt,
                                        bias=eps_col[:, 0:1]))

                @block.vector
                def _(v):
                    ac.wait(v)
                    v.reciprocal(ln_rs[:, :], ln_rs[:, :])
                    v.tensor_mul(ln_mr[:, :], ln_mr[:, :], ln_rs[:, :])
                    v.tensor_copy(ln_rs_bf[:, :], ln_rs[:, :])
                    v.tensor_copy(ln_mr_bf[:, :], ln_mr[:, :])
                    for f in range(NF):
                        v.tensor_mul(x_fm[:, f * T:(f + 1) * T],
                                     z_bf[:, f * T:(f + 1) * T], ln_rs_bf[:, :])
                        v.tensor_sub(x_fm[:, f * T:(f + 1) * T],
                                     x_fm[:, f * T:(f + 1) * T], ln_mr_bf[:, :])
                    vx.inc(v.engine_nop())

        with nc.sbuf_tensor("x_tm", [128, NT * D], BF16) as x_tm:
            @block.gpsimd
            def _(g):
                pid = g.compute_val(g.partition_id())
                g.reg_mov(r_pid, pid)
                for t in range(NT):
                    ld.dma(g.dma_start(out=x_tm[:, t * D:(t + 1) * D],
                                       in_=u[t * 128:(t + 1) * 128, :]))
                ini.inc(g.iota(iota_jp[:], pattern=[[1, 128]], base=0,
                               channel_multiplier=-1))

            @block.vector
            def _(v):
                v.memset(ones_bf[:], 1.0)
                v.memset(ones_f32[:], 1.0)
                v.memset(eps_col[:], float(EPS))
                ini.wait(v)
                v.tensor_single_scalar(ident_bf[:], iota_jp[:], 0, ALU.is_equal)
                vx.inc(v.tensor_single_scalar(ident_f32[:], iota_jp[:], 0, ALU.is_equal))

            @block.tensor
            def _(tb):
                ld.wait(tb)
                vx.wait(tb)
                for f in range(NF):
                    if f >= 2:
                        vx.wait(tb, vx.n + f - 1)
                    pb = psB[f % 2]
                    last = None
                    for t in range(NT):
                        last = tb.transpose(pb[:, t * 128:(t + 1) * 128],
                                            x_tm[:, t * D + f * 128: t * D + (f + 1) * 128],
                                            ident_bf[:])
                    mm.inc(last)

            @block.vector
            def _(v):
                base = mm.n - NF
                for f in range(NF):
                    v.wait_ge(mm.h, base + f + 1)
                    vx.inc(v.tensor_copy(x_fm[:, f * T:(f + 1) * T], psB[f % 2][:, :]))

        if _N_PHASES >= 1:
            p1 = ExitStack()
            with p1:
                E1 = p1.enter_context
                wbr_sb = E1(nc.sbuf_tensor("wbr_sb", [128, NF * 64], BF16))
                wbs_sb = E1(nc.sbuf_tensor("wbs_sb", [64, 256], BF16))
                wcr_sb = E1(nc.sbuf_tensor("wcr_sb", [128, 2 * 64], BF16))
                wcm_sb = E1(nc.sbuf_tensor("wcm_sb", [64, D], BF16))
                afac_sb = E1(nc.sbuf_tensor("afac_sb", [1, 32], F32))
                abc = E1(nc.sbuf_tensor("abc", [128, 32], F32))
                dvecr = E1(nc.sbuf_tensor("dvecr", [16, 128], F32))
                dv_sb = E1(nc.sbuf_tensor("dv_sb", [128, 16], F32))
                t1_bf = E1(nc.sbuf_tensor("t1_bf", [64, T], BF16))
                st_a = E1(nc.sbuf_tensor("st_a", [128, NT * 256], F32))
                st_b = E1(nc.sbuf_tensor("st_b", [128, NT * 256], F32))
                stT_bf = E1(nc.sbuf_tensor("stT_bf", [128, 2 * T], BF16))
                y1_bf = E1(nc.sbuf_tensor("y1_bf", [64, T], BF16))

                gate_mm = mm.n

                @block.gpsimd
                def _(g):
                    g.wait_ge(mm.h, gate_mm)
                    for f in range(NF):
                        ld.dma(g.dma_start(out=wbr_sb[:, f * 64:(f + 1) * 64],
                                           in_=wbr[f * 128:(f + 1) * 128, :]))
                    ld.dma(g.dma_start(out=wbs_sb[:, :], in_=wbs[:, :]))
                    for j in range(2):
                        ld.dma(g.dma_start(out=wcr_sb[:, j * 64:(j + 1) * 64],
                                           in_=wcr[j * 128:(j + 1) * 128, :]))
                    ld.dma(g.dma_start(out=wcm_sb[:, :], in_=wcm[:, :]))
                    ld.dma(g.dma_start(out=afac_sb[:, :], in_=afac[:, :]))
                    ld.dma(g.dma_start(out=dvecr[:, :], in_=dvec[:, :]))

                @block.tensor
                def _(tb):
                    ld.wait(tb)
                    vx.wait(tb)
                    tb.matmul(psS[0][:, 0:32], ones_f32[0:1, :], afac_sb[:, :],
                              start=True, stop=True)
                    mm.inc(tb.transpose(psS[1][:, 0:16], dvecr[:, :],
                                        ident_f32[0:16, 0:16]))

                @block.vector
                def _(v):
                    mm.wait(v)
                    v.tensor_copy(abc[:, :], psS[0][:, 0:32])
                    vx.inc(v.tensor_scalar_add(dv_sb[:, :], psS[1][:, 0:16], 1.0))

                @block.tensor
                def _(tb):
                    vx.wait(tb)
                    for g in range(2):
                        if g == 1:
                            vx.wait(tb, vx.n + 1)
                        last = None
                        for f in range(NF):
                            last = tb.matmul(psA[0][0:64, :],
                                             wbr_sb[:, f * 64:(f + 1) * 64],
                                             x_fm[:, f * T + g * 512: f * T + (g + 1) * 512],
                                             start=(f == 0), stop=(f == NF - 1))
                        mm.inc(last)

                @block.vector
                def _(v):
                    base = mm.n - 2
                    for g in range(2):
                        v.wait_ge(mm.h, base + g + 1)
                        vx.inc(v.tensor_copy(t1_bf[0:64, g * 512:(g + 1) * 512],
                                             psA[0][0:64, :]))

                @block.tensor
                def _(tb):
                    vx.wait(tb)
                    for i in range(NT):
                        if i >= 2:
                            vx.wait(tb, vx.n + i - 1)
                        mm.inc(tb.matmul(psA[i % 2][:, 0:256],
                                         t1_bf[0:64, i * 128:(i + 1) * 128],
                                         wbs_sb[0:64, :], start=True, stop=True))

                @block.vector
                def _(v):
                    base = mm.n - NT
                    for i in range(NT):
                        v.wait_ge(mm.h, base + i + 1)
                        vx.inc(v.tensor_copy(st_a[:, i * 256:(i + 1) * 256],
                                             psA[i % 2][:, 0:256]))

                @block.vector
                def _(v):
                    vx.wait(v)
                    src, dst = st_a, st_b
                    for lvl in range(8):
                        s = 1 << (7 - lvl)
                        a00 = abc[:, lvl * 4 + 0: lvl * 4 + 1]
                        a01 = abc[:, lvl * 4 + 1: lvl * 4 + 2]
                        a10 = abc[:, lvl * 4 + 2: lvl * 4 + 3]
                        a11 = abc[:, lvl * 4 + 3: lvl * 4 + 4]

                        def view(buf, bit, s=s):
                            ap = buf[:, :].rearrange("p (t g i) -> p t g i",
                                                     t=NT, g=256 // (2 * s), i=2 * s)
                            return ap[:, :, :, bit * s:(bit + 1) * s]

                        x0s, x1s = view(src, 0), view(src, 1)
                        y0s, y1s = view(dst, 0), view(dst, 1)
                        v.tensor_scalar_mul(y0s, x1s, a10)
                        v.scalar_tensor_tensor(y0s, x0s, a00, y0s, op0=ALU.mult, op1=ALU.add)
                        v.tensor_scalar_mul(y1s, x1s, a11)
                        v.scalar_tensor_tensor(y1s, x0s, a01, y1s, op0=ALU.mult, op1=ALU.add)
                        src, dst = dst, src
                    vx.inc(v.engine_nop())

                @block.tensor
                def _(tb):
                    vx.wait(tb)
                    k = 0
                    for j in range(2):
                        for q in range(2):
                            if k >= 2:
                                vx.wait(tb, vx.n + k - 1)
                            last = None
                            for t4 in range(4):
                                i = q * 4 + t4
                                last = tb.transpose(
                                    psA[q][:, t4 * 128:(t4 + 1) * 128],
                                    st_a[:, i * 256 + j * 128: i * 256 + (j + 1) * 128],
                                    ident_f32[:])
                            mm.inc(last)
                            k += 1

                @block.vector
                def _(v):
                    base = mm.n - 4
                    k = 0
                    for j in range(2):
                        for q in range(2):
                            k += 1
                            v.wait_ge(mm.h, base + k)
                            vx.inc(v.tensor_copy(
                                stT_bf[:, j * T + q * 512: j * T + (q + 1) * 512],
                                psA[q][:, :]))

                @block.tensor
                def _(tb):
                    vx.wait(tb)
                    for g in range(2):
                        if g == 1:
                            vx.wait(tb, vx.n + 1)
                        last = None
                        for j in range(2):
                            last = tb.matmul(psA[2][0:64, :],
                                             wcr_sb[:, j * 64:(j + 1) * 64],
                                             stT_bf[:, j * T + g * 512: j * T + (g + 1) * 512],
                                             start=(j == 0), stop=(j == 1))
                        mm.inc(last)

                @block.vector
                def _(v):
                    base = mm.n - 2
                    for g in range(2):
                        v.wait_ge(mm.h, base + g + 1)
                        vx.inc(v.tensor_copy(y1_bf[0:64, g * 512:(g + 1) * 512],
                                             psA[2][0:64, :]))

                @block.tensor
                def _(tb):
                    vx.wait(tb)
                    for f in range(NF):
                        for g in range(2):
                            i = 2 * f + g
                            if i >= 2:
                                vx.wait(tb, vx.n + i - 1)
                            mm.inc(tb.matmul(psA[i % 2][:, :],
                                             wcm_sb[0:64, f * 128:(f + 1) * 128],
                                             y1_bf[0:64, g * 512:(g + 1) * 512],
                                             start=True, stop=True))

                @block.vector
                def _(v):
                    base = mm.n - 2 * NF
                    for f in range(NF):
                        for g in range(2):
                            i = 2 * f + g
                            v.wait_ge(mm.h, base + i + 1)
                            vx.inc(v.scalar_tensor_tensor(
                                z_bf[:, f * T + g * 512: f * T + (g + 1) * 512],
                                x_fm[:, f * T + g * 512: f * T + (g + 1) * 512],
                                dv_sb[:, f:f + 1],
                                psA[i % 2][:, :],
                                op0=ALU.mult, op1=ALU.add))

            emit_layernorm()

        if _N_PHASES >= 2:
            coeff_bf = E(nc.sbuf_tensor("coeff_bf", [64, T], BF16))
            p2_gate_mm = mm.n
            p2_gate_vx = vx.n
            p2_gate_ac = ac.n

            p2a = ExitStack()
            with p2a:
                E2 = p2a.enter_context
                wg_sb = E2(nc.sbuf_tensor("wg_sb", [128, NF * 64], BF16))
                wsu_sb = E2(nc.sbuf_tensor("wsu_sb", [128, NF * 1024], BF16))
                wsd_sb = E2(nc.sbuf_tensor("wsd_sb", [128, 4 * D], BF16))
                h_s = E2(nc.sbuf_tensor("h_s", [128, 4 * T], BF16))
                sgb = E2(nc.sbuf_tensor("sgb", [128, 2 * 512], BF16))
                sc_fm = E2(nc.sbuf_tensor("sc_fm", [64, T], F32))
                sc_tm = E2(nc.sbuf_tensor("sc_tm", [128, NT * 64], F32))
                g_tm = E2(nc.sbuf_tensor("g_tm", [128, NT * 64], F32))
                t8 = E2(nc.sbuf_tensor("t8", [128, 8], F32))
                ts_col = E2(nc.sbuf_tensor("ts_col", [128, NT], F32))
                coeff_fm = E2(nc.sbuf_tensor("coeff_fm", [64, T], F32))
                m_fm = E2(nc.sbuf_tensor("m_fm", [64, T], F32))
                ltm = E2(nc.sbuf_tensor("ltm", [8, 8], F32))
                stat2 = E2(nc.sbuf_tensor("stat2", [128, 1024], F32))
                cnts_sb = E2(nc.sbuf_tensor("cnts_sb", [8, 64], F32))
                offs_sb = E2(nc.sbuf_tensor("offs_sb", [8, 64], F32))
                offc = E2(nc.sbuf_tensor("offc", [64, 1], F32))

                @block.gpsimd
                def _(g):
                    g.wait_ge(mm.h, p2_gate_mm)
                    g.wait_ge(vx.h, p2_gate_vx)
                    for f in range(NF):
                        ld.dma(g.dma_start(out=wg_sb[:, f * 64:(f + 1) * 64],
                                           in_=wg[f * 128:(f + 1) * 128, :]))
                    for f in range(NF):
                        ld.dma(g.dma_start(out=wsu_sb[:, f * 1024:(f + 1) * 1024],
                                           in_=wsu[f * 128:(f + 1) * 128, :]))
                    for j in range(4):
                        ld.dma(g.dma_start(out=wsd_sb[:, j * D:(j + 1) * D],
                                           in_=wsd[j * 128:(j + 1) * 128, :]))

                @block.tensor
                def _(tb):
                    ld.wait(tb, ld.n - 20 * 16)
                    vx.wait(tb)
                    for g2 in range(2):
                        if g2 == 1:
                            ac.wait(tb, ac.n + 1)
                        last = None
                        for f in range(NF):
                            last = tb.matmul(psS[0][0:64, :],
                                             wg_sb[:, f * 64:(f + 1) * 64],
                                             x_fm[:, f * T + g2 * 512: f * T + (g2 + 1) * 512],
                                             start=(f == 0), stop=(f == NF - 1))
                        mm.inc(last)

                @block.scalar
                def _(s):
                    base = mm.n - 2
                    for g2 in range(2):
                        s.wait_ge(mm.h, base + g2 + 1)
                        ac.inc(s.activation(sc_fm[0:64, g2 * 512:(g2 + 1) * 512],
                                            psS[0][0:64, :], AF.Sigmoid))

                @block.tensor
                def _(tb):
                    ac.wait(tb)
                    last = None
                    for t in range(NT):
                        last = tb.transpose(psS[1][:, t * 64:(t + 1) * 64],
                                            sc_fm[0:64, t * 128:(t + 1) * 128],
                                            ident_f32[0:64, 0:64])
                    mm.inc(last)

                @block.vector
                def _(v):
                    mm.wait(v)
                    v.tensor_copy(sc_tm[:, :], psS[1][:, :])
                    for t in range(NT):
                        scs = sc_tm[:, t * 64:(t + 1) * 64]
                        gts = g_tm[:, t * 64:(t + 1) * 64]
                        v.max(out=t8[:], in_=scs)
                        v.memset(t8[:, 6:8], -1e30)
                        v.tensor_copy(gts, scs)
                        v.match_replace(out=gts, in_to_replace=t8[:, 0:8],
                                        in_values=gts, imm_value=0.0)
                        v.tensor_sub(gts, scs, gts)
                    vx.inc(v.engine_nop())

                @block.tensor
                def _(tb):
                    vx.wait(tb)
                    last = None
                    for t in range(NT):
                        last = tb.transpose(psA[t // 4][0:64, (t % 4) * 128:(t % 4 + 1) * 128],
                                            g_tm[:, t * 64:(t + 1) * 64],
                                            ident_f32[:])
                    mm.inc(last)

                @block.vector
                def _(v):
                    mm.wait(v)
                    for q in range(2):
                        v.tensor_copy(coeff_fm[0:64, q * 512:(q + 1) * 512],
                                      psA[q][0:64, :])
                    v.tensor_single_scalar(m_fm[0:64, :], coeff_fm[0:64, :], 0.0, ALU.is_gt)
                    v.tensor_tensor_scan(stat[0:64, :], m_fm[0:64, :], m_fm[0:64, :],
                                         0.0, ALU.add, ALU.bypass)
                    vx.inc(v.engine_nop())

                @block.tensor
                def _(tb):
                    vx.wait(tb)
                    last = None
                    for q in range(2):
                        last = tb.matmul(psA[2 + q][:, :], ones_f32[0:64, :],
                                         coeff_fm[0:64, q * 512:(q + 1) * 512],
                                         start=True, stop=True)
                    mm.inc(last)

                @block.vector
                def _(v):
                    mm.wait(v)
                    for q in range(2):
                        sl = slice(q * 512, (q + 1) * 512)
                        v.reciprocal(stat2[:, sl], psA[2 + q][:, :])
                    v.tensor_mul(coeff_fm[0:64, :], coeff_fm[0:64, :], stat2[0:64, :])
                    vx.inc(v.engine_nop())

                @block.sync
                def _(s):
                    vx.wait(s)
                    io.dma(s.dma_start(out=cnt_in.ap()[0:1, :],
                                       in_=stat[0:64, (T - 1):T]))

                @block.gpsimd
                def _(g):
                    io.wait(g)
                    cc.inc(g.collective_compute(
                        "AllGather", ALU.bypass,
                        replica_groups=[list(range(NCORES))],
                        ins=[cnt_in.ap()], outs=[cnt_out.ap()]))
                    cc.wait(g)
                    ld.dma(g.dma_start(out=cnts_sb[:, :], in_=cnt_out.ap()[:, :]))

                shared_ld = ld.n - 16

                @block.tensor
                def _(tb):
                    ld.wait(tb, shared_ld)
                    vx.wait(tb)
                    k = 0
                    for j in range(4):
                        for g2 in range(2):
                            py = psA[0 + (k % 2)]
                            pg = psA[2 + (k % 2)]
                            if k >= 2:
                                vx.wait(tb, vx.n + k - 1)
                            last = None
                            for f in range(NF):
                                tb.matmul(py[:, :],
                                          wsu_sb[:, f * 1024 + j * 128: f * 1024 + (j + 1) * 128],
                                          x_fm[:, f * T + g2 * 512: f * T + (g2 + 1) * 512],
                                          start=(f == 0), stop=(f == NF - 1))
                                last = tb.matmul(pg[:, :],
                                                 wsu_sb[:, f * 1024 + 512 + j * 128: f * 1024 + 512 + (j + 1) * 128],
                                                 x_fm[:, f * T + g2 * 512: f * T + (g2 + 1) * 512],
                                                 start=(f == 0), stop=(f == NF - 1))
                            mm.inc(last)
                            k += 1

                @block.scalar
                def _(s):
                    base = mm.n - 8
                    base_v = vx.n
                    for k in range(8):
                        s.wait_ge(mm.h, base + k + 1)
                        if k >= 2:
                            s.wait_ge(vx.h, base_v + k - 1)
                        ac.inc(s.activation(sgb[:, (k % 2) * 512:(k % 2 + 1) * 512],
                                            psA[2 + (k % 2)][:, :], AF.Silu))

                @block.vector
                def _(v):
                    base_a = ac.n - 8
                    k = 0
                    for j in range(4):
                        for g2 in range(2):
                            v.wait_ge(ac.h, base_a + k + 1)
                            vx.inc(v.tensor_mul(
                                h_s[:, j * T + g2 * 512: j * T + (g2 + 1) * 512],
                                psA[0 + (k % 2)][:, :],
                                sgb[:, (k % 2) * 512:(k % 2 + 1) * 512]))
                            k += 1

                @block.tensor
                def _(tb):
                    vx.wait(tb)
                    for f in range(NF):
                        for g2 in range(2):
                            i = 2 * f + g2
                            if i >= 2:
                                vx.wait(tb, vx.n + i - 1)
                            last = None
                            for j in range(4):
                                last = tb.matmul(
                                    psA[i % 2][:, :],
                                    wsd_sb[:, j * D + f * 128: j * D + (f + 1) * 128],
                                    h_s[:, j * T + g2 * 512: j * T + (g2 + 1) * 512],
                                    start=(j == 0), stop=(j == 3))
                            mm.inc(last)

                @block.vector
                def _(v):
                    base = mm.n - 2 * NF
                    for f in range(NF):
                        for g2 in range(2):
                            i = 2 * f + g2
                            v.wait_ge(mm.h, base + i + 1)
                            vx.inc(v.tensor_add(
                                z_bf[:, f * T + g2 * 512: f * T + (g2 + 1) * 512],
                                x_fm[:, f * T + g2 * 512: f * T + (g2 + 1) * 512],
                                psA[i % 2][:, :]))

                @block.vector
                def _(v):
                    ld.wait(v)
                    vx.inc(v.tensor_single_scalar(ltm[:, :], iota_jp[0:8, 0:8],
                                                  0, ALU.is_gt))

                @block.tensor
                def _(tb):
                    vx.wait(tb)
                    ac.wait(tb)
                    mm.inc(tb.matmul(psS[0][0:8, 0:64], ltm[:, :], cnts_sb[:, :],
                                     start=True, stop=True))

                @block.vector
                def _(v):
                    mm.wait(v)
                    vx.inc(v.tensor_copy(offs_sb[:, :], psS[0][0:8, 0:64]))

                @block.sync
                def _(s):
                    vx.wait(s)
                    io.dma(s.dma_start(out=offs_dram.ap()[:, :], in_=offs_sb[:, :]))

                @block.gpsimd
                def _(g):
                    io.wait(g)
                    ld.dma(g.dma_start(out=offc[:, :],
                                       in_=offs_dram.ap()[ds(RV(r_pid, 0, 7), 1), :]))

                @block.vector
                def _(v):
                    ld.wait(v)
                    v.tensor_scalar_add(stat[0:64, :], stat[0:64, :], offc[0:64, 0:1])
                    v.tensor_single_scalar(m_fm[0:64, :], stat[0:64, :], float(CAP),
                                           ALU.is_le)
                    v.tensor_mul(coeff_fm[0:64, :], coeff_fm[0:64, :], m_fm[0:64, :])
                    vx.inc(v.tensor_copy(coeff_bf[0:64, :], coeff_fm[0:64, :]))

                if _DEBUG_TAP == "coeff":
                    @block.sync
                    def _(s):
                        vx.wait(s)
                        io.dma(s.dma_start(out=dbg[0:64, 0:T], in_=coeff_fm[0:64, :]))
                        io.dma(s.dma_start(out=dbg[64:128, 0:T], in_=stat[0:64, :]))
                        io.dma(s.dma_start(out=dbg[0:8, T:T + 64], in_=cnts_sb[:, :]))
                        io.dma(s.dma_start(out=dbg[0:64, T + 64:T + 65], in_=offc[:, :], allow_slow_non_contiguous=True))
                        io.dma(s.dma_start(out=dbg[0:64, T + 65:T + 65 + 512],
                                           in_=sc_fm[0:64, 0:512]))
                        io.wait(s)

            p2b = ExitStack()
            with p2b:
                E2 = p2b.enter_context
                wup_sb = E2(nc.sbuf_tensor("wup_sb", [128, 2 * NF * 512], BF16))
                wdnf = E2(nc.sbuf_tensor("wdnf", [128, 2 * 16 * 128], BF16))
                h_grp = E2(nc.sbuf_tensor("h_grp", [128, 2 * EGRP * T], BF16))
                cbc = E2(nc.sbuf_tensor("cbc", [128, EGRP * T], BF16))
                sgb2 = E2(nc.sbuf_tensor("sgb2", [128, 2 * 512], BF16))
                tvb = E2(nc.sbuf_tensor("tvb", [128, 2 * 512], BF16))
                oh_i32 = E2(nc.sbuf_tensor("oh_i32", [64, EGRP * 128], I32))
                oh_bf = E2(nc.sbuf_tensor("oh_bf", [64, EGRP * 128], BF16))

                gate2_mm = mm.n
                gate2_vx = vx.n
                cmp_vx = []
                stageA_ac = []
                stageB_vx = []

                for grp in range(NEXP // EGRP):
                    @block.gpsimd
                    def _(g, grp=grp):
                        if grp == 0:
                            g.wait_ge(mm.h, gate2_mm)
                            g.wait_ge(vx.h, gate2_vx)
                        else:
                            g.wait_ge(vx.h, cmp_vx[grp - 1])
                        ini.inc(g.iota(oh_i32[:, :], pattern=[[-1, EGRP], [0, 128]],
                                       base=-(grp * EGRP), channel_multiplier=1))
                        stageA_ac.append(ac.n)
                        for e4 in range(EGRP):
                            e = grp * EGRP + e4
                            if e >= 2:
                                if e4 >= 2:
                                    g.wait_ge(ac.h, ac.n + 4 * (e4 - 1))
                                else:
                                    g.wait_ge(ac.h, stageA_ac[grp - 1] + 4 * (e4 + 7))
                            src = wup[e].rearrange("(f p) c -> p f c", p=128)
                            wul.dma(g.dma_start(
                                out=wup_sb[:, (e % 2) * NF * 512:(e % 2 + 1) * NF * 512],
                                in_=src))

                    @block.vector
                    def _(v, grp=grp):
                        ini.wait(v)
                        vx.inc(v.tensor_single_scalar(oh_bf[:, :], oh_i32[:, :],
                                                      0, ALU.is_equal))
                        cmp_vx.append(vx.n)
                    @block.tensor
                    def _(tb, grp=grp):
                        vx.wait(tb)
                        last = None
                        for e4 in range(EGRP):
                            for g2 in range(2):
                                last = tb.matmul(
                                    psS[g2][:, :],
                                    oh_bf[:, e4 * 128:(e4 + 1) * 128],
                                    coeff_bf[0:64, g2 * 512:(g2 + 1) * 512],
                                    start=True, stop=True)
                            mm.inc(last)
                            vx.wait(tb, vx.n + e4 + 1)

                    @block.vector
                    def _(v, grp=grp):
                        base = mm.n - EGRP
                        for e4 in range(EGRP):
                            v.wait_ge(mm.h, base + e4 + 1)
                            v.tensor_copy(cbc[:, e4 * T: e4 * T + 512], psS[0][:, :])
                            vx.inc(v.tensor_copy(cbc[:, e4 * T + 512:(e4 + 1) * T],
                                                 psS[1][:, :]))

                    for e4 in range(EGRP):
                        e = grp * EGRP + e4

                        @block.tensor
                        def _(tb, e=e, e4=e4):
                            wul.wait(tb, (e + 1) * 16)
                            k = 0
                            for j in range(2):
                                for g2 in range(2):
                                    pg = psA[0 + (k % 2)]
                                    pv = psA[2 + (k % 2)]
                                    ac.wait(tb, ac.n + k - 1)
                                    vx.wait(tb, vx.n + k - 1)
                                    last = None
                                    for f in range(NF):
                                        wslice = wup_sb[:, (e % 2) * NF * 512 + f * 512:
                                                        (e % 2) * NF * 512 + f * 512 + 512]
                                        tb.matmul(pg[:, :], wslice[:, j * 128:(j + 1) * 128],
                                                  x_fm[:, f * T + g2 * 512: f * T + (g2 + 1) * 512],
                                                  start=(f == 0), stop=(f == NF - 1))
                                        last = tb.matmul(pv[:, :],
                                                         wslice[:, 256 + j * 128: 256 + (j + 1) * 128],
                                                         x_fm[:, f * T + g2 * 512: f * T + (g2 + 1) * 512],
                                                         start=(f == 0), stop=(f == NF - 1))
                                    mm.inc(last)
                                    k += 1

                        @block.scalar
                        def _(s, e4=e4):
                            base = mm.n - 4
                            base_v = vx.n
                            for k in range(4):
                                s.wait_ge(mm.h, base + k + 1)
                                s.wait_ge(vx.h, base_v + k - 1)
                                ac.inc(s.activation(sgb2[:, (k % 2) * 512:(k % 2 + 1) * 512],
                                                    psA[0 + (k % 2)][:, :], AF.Silu))

                        @block.vector
                        def _(v, e4=e4):
                            base_a = ac.n - 4
                            k = 0
                            for j in range(2):
                                for g2 in range(2):
                                    v.wait_ge(ac.h, base_a + k + 1)
                                    v.tensor_mul(tvb[:, (k % 2) * 512:(k % 2 + 1) * 512],
                                                 sgb2[:, (k % 2) * 512:(k % 2 + 1) * 512],
                                                 cbc[:, e4 * T + g2 * 512: e4 * T + (g2 + 1) * 512])
                                    vx.inc(v.scalar_tensor_tensor(
                                        h_grp[:, (2 * e4 + j) * T + g2 * 512:
                                              (2 * e4 + j) * T + (g2 + 1) * 512],
                                        tvb[:, (k % 2) * 512:(k % 2 + 1) * 512],
                                        ones_f32[:, 0:1],
                                        psA[2 + (k % 2)][:, :],
                                        op0=ALU.mult, op1=ALU.mult))
                                    k += 1

                    @block.gpsimd
                    def _(g, grp=grp):
                        stageB_vx.append(vx.n)
                        for f in range(NF):
                            if grp * NF + f >= 2:
                                if f >= 2:
                                    g.wait_ge(vx.h, vx.n + 2 * (f - 1))
                                else:
                                    g.wait_ge(vx.h, stageB_vx[grp - 1] + 30 + 2 * f)
                            src = wdn[grp * EGRP:(grp + 1) * EGRP,
                                      :, f * 128:(f + 1) * 128]
                            src = src.rearrange("e (j p) c -> p e j c", p=128)
                            wdl.dma(g.dma_start(
                                out=wdnf[:, (f % 2) * 2048:(f % 2 + 1) * 2048],
                                in_=src))

                    @block.tensor
                    def _(tb, grp=grp):
                        vx.wait(tb)
                        for f in range(NF):
                            wdl.wait(tb, (grp * NF + f + 1) * 16)
                            for g2 in range(2):
                                i = 2 * f + g2
                                if i >= 2:
                                    vx.wait(tb, vx.n + i - 1)
                                last = None
                                kk = 0
                                for e4 in range(EGRP):
                                    for j in range(2):
                                        last = tb.matmul(
                                            psS[i % 2][:, :],
                                            wdnf[:, (f % 2) * 2048 + (2 * e4 + j) * 128:
                                                 (f % 2) * 2048 + (2 * e4 + j + 1) * 128],
                                            h_grp[:, (2 * e4 + j) * T + g2 * 512:
                                                  (2 * e4 + j) * T + (g2 + 1) * 512],
                                            start=(kk == 0), stop=(kk == 2 * EGRP - 1))
                                        kk += 1
                                mm.inc(last)

                    @block.vector
                    def _(v, grp=grp):
                        base = mm.n - 2 * NF
                        for f in range(NF):
                            for g2 in range(2):
                                i = 2 * f + g2
                                v.wait_ge(mm.h, base + i + 1)
                                vx.inc(v.tensor_add(
                                    z_bf[:, f * T + g2 * 512: f * T + (g2 + 1) * 512],
                                    z_bf[:, f * T + g2 * 512: f * T + (g2 + 1) * 512],
                                    psS[i % 2][:, :]))

            emit_layernorm()

        if _N_PHASES >= 3:
            SLOPES = [2.0 ** (-(h + 1) / 2.0) for h in range(16)]
            CEXP = 20.0
            INVSQ = 1.0 / float(np.sqrt(128.0))
            p3 = ExitStack()
            with p3:
                E3 = p3.enter_context
                bias4 = E3(nc.sbuf_tensor("bias4", [128, 4 * 16], F32))
                kcol = E3(nc.sbuf_tensor("kcol", [128, 16 * 16], F32))
                kval_f = E3(nc.sbuf_tensor("kval_f", [128, 16], F32))
                ij_f = E3(nc.sbuf_tensor("ij_f", [128, 512], F32))
                masks8 = E3(nc.sbuf_tensor("masks8", [128, 8 * 512], BF16))
                thr8 = E3(nc.sbuf_tensor("thr8", [128, 8], F32))
                thr_row = E3(nc.sbuf_tensor("thr_row", [1, 8], F32))
                thr_row2 = E3(nc.sbuf_tensor("thr_row2", [1, 16], F32))
                ci_row = E3(nc.sbuf_tensor("ci_row", [1, 8], F32))
                par_bc = E3(nc.sbuf_tensor("par_bc", [128, 2], F32))
                pen_col = E3(nc.sbuf_tensor("pen_col", [128, 4], F32))
                inv_col = E3(nc.sbuf_tensor("inv_col", [128, 2], F32))
                jb = E3(nc.sbuf_tensor("jb", [128, 512], F32))
                jbx = E3(nc.sbuf_tensor("jbx", [128, 512], F32))
                ctx_loc = E3(nc.sbuf_tensor("ctx_loc", [128, NF * T], BF16))

                p3_gate_mm = mm.n
                p3_gate_vx = vx.n
                p3_gate_ac = ac.n

                with nc.sbuf_tensor("w_st1", [128, 2 * NF * 128], BF16) as w_st, \
                     nc.sbuf_tensor("kv_stage", [128, NF * T], BF16) as kv_stage, \
                     nc.sbuf_tensor("vfm", [128, 2 * T], BF16) as vfm, \
                     nc.sbuf_tensor("biasr", [16, 4 * 128], F32) as biasr:

                    @block.gpsimd
                    def _(g):
                        g.wait_ge(mm.h, p3_gate_mm)
                        g.wait_ge(vx.h, p3_gate_vx)
                        g.wait_ge(ac.h, p3_gate_ac)
                        for bi, bb in enumerate((bq1, bk1, bv1, bo1)):
                            ld.dma(g.dma_start(out=biasr[:, bi * 128:(bi + 1) * 128],
                                               in_=bb[:, :]))

                    @block.tensor
                    def _(tb):
                        ld.wait(tb)
                        vx.wait(tb)
                        ac.wait(tb)
                        last = None
                        for bi in range(4):
                            last = tb.transpose(psS[1][:, bi * 16:(bi + 1) * 16],
                                                biasr[:, bi * 128:(bi + 1) * 128],
                                                ident_f32[0:16, 0:16])
                        mm.inc(last)

                    @block.vector
                    def _(v):
                        mm.wait(v)
                        vx.inc(v.tensor_copy(bias4[:, :], psS[1][:, 0:64]))

                    if _DEBUG_TAP == "attn":
                        @block.sync
                        def _(s):
                            vx.wait(s)
                            io.dma(s.dma_start(out=dbg[:, 14336:14400], in_=bias4[:, :]))
                            io.wait(s)

                    def emit_proj3(wsrc, bcol_i, kind, counter=[0]):
                        for m in range(NF):
                            idx = counter[0]
                            counter[0] += 1
                            slot = idx % 2

                            @block.gpsimd
                            def _(g, m=m, idx=idx, slot=slot, wsrc=wsrc):
                                if idx >= 2:
                                    g.wait_ge(ac.h, ac.n - 2)
                                srcw = wsrc[:, m * 128:(m + 1) * 128]
                                srcw = srcw.rearrange("(f p) c -> p f c", p=128)
                                wul.dma(g.dma_start(
                                    out=w_st[:, slot * NF * 128:(slot + 1) * NF * 128],
                                    in_=srcw))

                            @block.tensor
                            def _(tb, m=m, idx=idx, slot=slot):
                                wul.wait(tb)
                                if idx == 0:
                                    vx.wait(tb)
                                    ac.wait(tb)
                                last = None
                                for g2 in range(2):
                                    if 2 * idx + g2 >= 2:
                                        ac.wait(tb, ac.n + g2 - 1)
                                    for f in range(NF):
                                        last = tb.matmul(
                                            psA[(2 * idx + g2) % 2][:, :],
                                            w_st[:, slot * NF * 128 + f * 128:
                                                 slot * NF * 128 + (f + 1) * 128],
                                            x_fm[:, f * T + g2 * 512: f * T + (g2 + 1) * 512],
                                            start=(f == 0), stop=(f == NF - 1))
                                    mm.inc(last)

                            @block.scalar
                            def _(s, m=m, idx=idx, slot=slot, bcol_i=bcol_i, kind=kind):
                                base = mm.n - 2
                                if idx == 0:
                                    s.wait_ge(vx.h, vx.n)
                                for g2 in range(2):
                                    s.wait_ge(mm.h, base + g2 + 1)
                                    if kind == "k":
                                        dstap = kv_stage[:, m * T + g2 * 512:
                                                         m * T + (g2 + 1) * 512]
                                    else:
                                        dstap = vfm[:, slot * T + g2 * 512:
                                                    slot * T + (g2 + 1) * 512]
                                    ac.inc(s.add(dstap,
                                                 psA[(2 * idx + g2) % 2][:, :],
                                                 bias4[:, bcol_i * 16 + m:
                                                       bcol_i * 16 + m + 1]))

                            if kind == "v" and _DEBUG_TAP == "vfm" and m == _DEBUG_H:
                                @block.gpsimd
                                def _(g, slot=slot):
                                    g.wait_ge(ac.h, ac.n)
                                    io.dma(g.dma_start(out=dbg[:, 0:1024],
                                                       in_=vfm[:, slot * T:(slot + 1) * T]))
                                    io.wait(g)

                            if kind == "v":
                                @block.tensor
                                def _(tb, m=m, idx=idx, slot=slot):
                                    ac.wait(tb)
                                    if m >= 2:
                                        vx.wait(tb, vx.n - 1)
                                    last = None
                                    for t in range(NT):
                                        last = tb.transpose(
                                            psB[slot][:, t * 128:(t + 1) * 128],
                                            vfm[:, slot * T + t * 128:
                                                slot * T + (t + 1) * 128],
                                            ident_bf[:])
                                    mm.inc(last)

                                @block.vector
                                def _(v, m=m, slot=slot):
                                    mm.wait(v)
                                    if m == 0:
                                        v.wait_ge(io.h, io.n)
                                    last = None
                                    for t in range(NT):
                                        last = v.tensor_copy(
                                            kv_stage[:, t * 2048 + m * 128:
                                                     t * 2048 + (m + 1) * 128],
                                            psB[slot][:, t * 128:(t + 1) * 128])
                                    vx.inc(last)

                                if _DEBUG_TAP == "vfm" and m == _DEBUG_H:
                                    @block.gpsimd
                                    def _(g, m=m):
                                        g.wait_ge(vx.h, vx.n)
                                        src = kv_stage[:, :].rearrange(
                                            "p (t c) -> p t c", t=NT)
                                        io.dma(g.dma_start(
                                            out=dbg[:, 1024:2048].rearrange(
                                                "p (t c) -> p t c", t=NT),
                                            in_=src[:, :, m * 128:(m + 1) * 128]))
                                        io.wait(g)

                    emit_proj3(wk1, 1, "k")

                    @block.gpsimd
                    def _(g):
                        g.wait_ge(ac.h, ac.n)
                        par = g.partition_id() % 2
                        dstk = sh_k.ap().rearrange("(m p) (h t) -> h p m t", h=2, p=128)
                        io.dma(g.dma_start(out=dstk[ds(par, 1), :, :, :],
                                           in_=kv_stage[:, :]))

                    emit_proj3(wv1, 2, "v")

                    if _DEBUG_TAP == "vstage":
                        @block.gpsimd
                        def _(g):
                            g.wait_ge(vx.h, vx.n)
                            io.dma(g.dma_start(out=dbg[:, :], in_=kv_stage[:, :]))
                            io.wait(g)

                    @block.gpsimd
                    def _(g):
                        g.wait_ge(vx.h, vx.n)
                        par = g.partition_id() % 2
                        dstv = sh_v.ap().rearrange("(tt tp p) c -> tt p tp c", tt=2, p=128)
                        io.dma(g.dma_start(out=dstv[ds(par, 1), :, :, :],
                                           in_=kv_stage[:, :]))

                    @block.gpsimd
                    def _(g):
                        io.wait(g)
                        cc.inc(g.collective_compute(
                            "AllGather", ALU.bypass,
                            replica_groups=[list(range(NCORES))],
                            ins=[bar_in.ap()], outs=[bar_out.ap()]))

                    with nc.sbuf_tensor("kval_i", [128, 16], I32) as kval_i, \
                         nc.sbuf_tensor("ij_i", [128, 512], I32) as ij_i, \
                         nc.sbuf_tensor("ijp_i", [128, 512], I32) as ijp_i, \
                         nc.sbuf_tensor("ijp_f", [128, 512], F32) as ijp_f:

                        @block.gpsimd
                        def _(g):
                            cc.wait(g)
                            ini.inc(g.iota(kval_i[:, :], pattern=[[128, 16]], base=0,
                                           channel_multiplier=1))
                            ini.inc(g.iota(ij_i[:, :], pattern=[[1, 512]], base=0,
                                           channel_multiplier=0))
                            ini.inc(g.iota(ijp_i[:, :], pattern=[[1, 512]], base=0,
                                           channel_multiplier=-1))
                            ini.dma(g.dma_start(out=ci_row[0:1, :],
                                                in_=cinfo[0:1, :]))

                        @block.vector
                        def _(v):
                            ini.wait(v)
                            v.tensor_copy(kval_f[:, :], kval_i[:, :])
                            v.tensor_copy(ij_f[:, :], ij_i[:, :])
                            v.tensor_copy(ijp_f[:, :], ijp_i[:, :])
                            v.memset(inv_col[:, :], INVSQ)
                            for i, mi in enumerate([0, 1, 2, 3, 8, 9, 10, 11]):
                                v.memset(thr_row[0:1, i:i + 1], float(mi * 128))
                            for h in range(16):
                                v.tensor_scalar_mul(kcol[:, h * 16:(h + 1) * 16],
                                                    kval_f[:, :], SLOPES[h])
                            vx.inc(v.engine_nop())

                        @block.vector
                        def _(v):
                            vx.wait(v)
                            v.tensor_single_scalar(thr_row2[0:1, 0:8], thr_row[0:1, :],
                                                   ci_row[0:1, 2:3], ALU.subtract)
                            vx.inc(v.engine_nop())

                        @block.tensor
                        def _(tb):
                            vx.wait(tb)
                            tb.matmul(psS[0][:, 0:8], ones_f32[0:1, :],
                                      thr_row2[0:1, 0:8], start=True, stop=True)
                            mm.inc(tb.matmul(psS[0][:, 8:16], ones_f32[0:1, :],
                                             ci_row[0:1, 0:8], start=True, stop=True))

                        @block.vector
                        def _(v):
                            mm.wait(v)
                            v.tensor_copy(thr8[:, :], psS[0][:, 0:8])
                            v.tensor_copy(par_bc[:, 0:2], psS[0][:, 8:10])
                            v.tensor_copy(pen_col[:, 0:2], psS[0][:, 12:14])
                            for i in range(8):
                                v.tensor_single_scalar(masks8[:, i * 512:(i + 1) * 512],
                                                       ijp_f[:, :], thr8[:, i:i + 1],
                                                       ALU.is_ge)
                            vx.inc(v.engine_nop())

                BOUND = set([0, 1, 2, 3, 8, 9, 10, 11])
                MIDX = {0: 0, 1: 1, 2: 2, 3: 3, 8: 4, 9: 5, 10: 6, 11: 7}
                cc_bar = cc.n
                qev_ac = []

                with nc.sbuf_tensor("w_st2", [128, 2 * NF * 128], BF16) as w_st2, \
                     nc.sbuf_tensor("q_h", [128, 2 * T], BF16) as q_h, \
                     nc.sbuf_tensor("k_h", [128, 2 * 2048], BF16) as k_h, \
                     nc.sbuf_tensor("v_h", [128, 2 * 2048], BF16) as v_h, \
                     nc.sbuf_tensor("pt_sb", [128, 16 * 512], BF16) as pt_sb, \
                     nc.sbuf_tensor("argb", [128, 2 * 512], F32) as argb, \
                     nc.sbuf_tensor("sum_row", [1, 512], F32) as sum_row, \
                     nc.sbuf_tensor("rb_sb", [128, 512], F32) as rb_sb:

                    for h in range(16):
                        hs2 = h % 2

                        @block.gpsimd
                        def _(g, h=h, hs2=hs2):
                            if h >= 2:
                                g.wait_ge(ac.h, qev_ac[h - 2])
                            srcw = wq1[:, h * 128:(h + 1) * 128]
                            srcw = srcw.rearrange("(f p) c -> p f c", p=128)
                            wul.dma(g.dma_start(
                                out=w_st2[:, hs2 * NF * 128:(hs2 + 1) * NF * 128],
                                in_=srcw))

                        @block.sync
                        def _(s, h=h, hs2=hs2):
                            if h >= 2:
                                s.wait_ge(dnc.h, dnc.n - 1)
                            else:
                                s.wait_ge(cc.h, cc_bar)
                            wdl.dma(s.dma_start(out=k_h[:, hs2 * 2048:(hs2 + 1) * 2048],
                                                in_=sh_k.ap()[h * 128:(h + 1) * 128, :]))
                            srcv = sh_v.ap()[:, h * 128:(h + 1) * 128]
                            srcv = srcv.rearrange("(kt p) c -> p kt c", p=128)
                            wdl.dma(s.dma_start(out=v_h[:, hs2 * 2048:(hs2 + 1) * 2048],
                                                in_=srcv))

                        @block.tensor
                        def _(tb, h=h, hs2=hs2):
                            wul.wait(tb)
                            vx.wait(tb)
                            ac.wait(tb)
                            last = None
                            for g2 in range(2):
                                for f in range(NF):
                                    last = tb.matmul(
                                        psA[g2][:, :],
                                        w_st2[:, hs2 * NF * 128 + f * 128:
                                              hs2 * NF * 128 + (f + 1) * 128],
                                        x_fm[:, f * T + g2 * 512: f * T + (g2 + 1) * 512],
                                        start=(f == 0), stop=(f == NF - 1))
                                mm.inc(last)

                        @block.scalar
                        def _(s, h=h, hs2=hs2):
                            base = mm.n - 2
                            for g2 in range(2):
                                s.wait_ge(mm.h, base + g2 + 1)
                                if h >= 2:
                                    s.wait_ge(dnc.h, dnc.n - 1)
                                ac.inc(s.add(
                                    q_h[:, hs2 * T + g2 * 512: hs2 * T + (g2 + 1) * 512],
                                    psA[g2][:, :],
                                    bias4[:, 0 * 16 + h: 0 * 16 + h + 1]))
                            qev_ac.append(ac.n)

                        if _DEBUG_TAP == "attn" and h == _DEBUG_H:
                            @block.gpsimd
                            def _(g, hs2=hs2):
                                g.wait_ge(ac.h, ac.n)
                                g.wait_ge(wdl.h, wdl.n)
                                io.dma(g.dma_start(out=dbg[:, 11264:11264 + 1024],
                                                   in_=q_h[:, hs2 * T:hs2 * T + 1024]))
                                kvsrc = k_h if _DEBUG_KV == "k" else v_h
                                io.dma(g.dma_start(out=dbg[:, 12288:12288 + 2048],
                                                   in_=kvsrc[:, hs2 * 2048:hs2 * 2048 + 2048]))
                                io.wait(g)

                        for qg in range(2):
                            q0 = qg * 512
                            nkc = 12 if qg == 0 else 16

                            @block.vector
                            def _(v, h=h, qg=qg, q0=q0):
                                v.tensor_scalar(jb[:, :], ij_f[:, :], SLOPES[h],
                                                SLOPES[h] * q0 + CEXP,
                                                op0=ALU.mult, op1=ALU.add)
                                v.scalar_tensor_tensor(
                                    jb[:, :], par_bc[:, 0:1].to_broadcast([128, 512]),
                                    SLOPES[h] * 1024.0, jb[:, :],
                                    op0=ALU.mult, op1=ALU.add)
                                v.tensor_scalar_add(jbx[:, :], jb[:, :], pen_col[:, 0:1])
                                vx.inc(v.engine_nop())

                            stt_cnt = {}
                            exp_cnt = {}

                            @block.tensor
                            def _(tb, h=h, hs2=hs2, qg=qg, q0=q0, nkc=nkc,
                                  stt_cnt=stt_cnt):
                                wdl.wait(tb, 16 * 2 * (h + 1))
                                ac.wait(tb)
                                vx.wait(tb)
                                base_stt = vx.n
                                for kc in range(nkc):
                                    if kc >= 2:
                                        tb.wait_ge(vx.h, base_stt + kc - 1)
                                    mm.inc(tb.matmul(
                                        psA[kc % 2][:, :],
                                        k_h[:, hs2 * 2048 + kc * 128:
                                            hs2 * 2048 + (kc + 1) * 128],
                                        q_h[:, hs2 * T + q0: hs2 * T + q0 + 512],
                                        start=True, stop=True))

                            @block.vector
                            def _(v, h=h, qg=qg, nkc=nkc, stt_cnt=stt_cnt):
                                base = mm.n - nkc
                                base_ac = ac.n
                                for kc in range(nkc):
                                    mi = kc - 4 * qg
                                    v.wait_ge(mm.h, base + kc + 1)
                                    v.wait_ge(ac.h, base_ac + max(0, kc - 1))
                                    jsel = jbx if mi >= 4 else jb
                                    v.scalar_tensor_tensor(
                                        argb[:, (kc % 2) * 512:(kc % 2 + 1) * 512],
                                        psA[kc % 2][:, :], inv_col[:, 0:1], jsel[:, :],
                                        op0=ALU.mult, op1=ALU.subtract)
                                    if mi in BOUND:
                                        v.tensor_scalar(
                                            argb[:, (kc % 2) * 512:(kc % 2 + 1) * 512],
                                            argb[:, (kc % 2) * 512:(kc % 2 + 1) * 512],
                                            kcol[:, h * 16 + kc: h * 16 + kc + 1], 0.0,
                                            op0=ALU.add, op1=ALU.min)
                                    vx.inc(v.engine_nop())
                                    stt_cnt[kc] = vx.n

                            @block.scalar
                            def _(s, h=h, qg=qg, nkc=nkc, stt_cnt=stt_cnt,
                                  exp_cnt=exp_cnt):
                                for kc in range(nkc):
                                    mi = kc - 4 * qg
                                    s.wait_ge(vx.h, stt_cnt[kc])
                                    if mi in BOUND:
                                        ins = s.activation(
                                            pt_sb[:, kc * 512:(kc + 1) * 512],
                                            argb[:, (kc % 2) * 512:(kc % 2 + 1) * 512],
                                            AF.Exp)
                                    else:
                                        ins = s.activation(
                                            pt_sb[:, kc * 512:(kc + 1) * 512],
                                            argb[:, (kc % 2) * 512:(kc % 2 + 1) * 512],
                                            AF.Exp,
                                            bias=kcol[:, h * 16 + kc: h * 16 + kc + 1])
                                    ac.inc(ins)
                                    exp_cnt[kc] = ac.n

                            @block.vector
                            def _(v, qg=qg, nkc=nkc, exp_cnt=exp_cnt):
                                for kc in range(nkc):
                                    mi = kc - 4 * qg
                                    if mi in BOUND:
                                        v.wait_ge(ac.h, exp_cnt[kc])
                                        v.tensor_mul(
                                            pt_sb[:, kc * 512:(kc + 1) * 512],
                                            pt_sb[:, kc * 512:(kc + 1) * 512],
                                            masks8[:, MIDX[mi] * 512:(MIDX[mi] + 1) * 512])
                                vx.inc(v.engine_nop())

                            if _DEBUG_TAP == "attn" and h == _DEBUG_H and qg == 1:
                                @block.gpsimd
                                def _(g, nkc=nkc):
                                    ac.wait(g)
                                    g.wait_ge(vx.h, vx.n)
                                    io.dma(g.dma_start(out=dbg[:, 0:8192], in_=pt_sb[:, 0:8192]))
                                    io.dma(g.dma_start(out=dbg[:, 8192:8704], in_=jb[:, :]))
                                    io.dma(g.dma_start(out=dbg[:, 8704:9216], in_=jbx[:, :]))
                                    io.dma(g.dma_start(out=dbg[:, 9216:10240], in_=argb[:, :]))
                                    io.dma(g.dma_start(out=dbg[:, 10240:10248], in_=thr8[:, :]))
                                    io.dma(g.dma_start(out=dbg[:, 10248:10250], in_=par_bc[:, 0:2]))
                                    io.dma(g.dma_start(out=dbg[:, 10250:10252], in_=pen_col[:, 0:2]))
                                    io.dma(g.dma_start(out=dbg[:, 10252:10508], in_=kcol[:, 0:256]))
                                    io.wait(g)

                            @block.tensor
                            def _(tb, h=h, hs2=hs2, qg=qg, nkc=nkc):
                                vx.wait(tb)
                                ac.wait(tb)
                                for kc in range(nkc):
                                    tb.matmul(psS[0][0:1, :], ones_bf[:, 0:1],
                                              pt_sb[:, kc * 512:(kc + 1) * 512],
                                              start=(kc == 0), stop=(kc == nkc - 1))
                                last = None
                                for kc in range(nkc):
                                    last = tb.matmul(
                                        psA[2][:, :],
                                        v_h[:, hs2 * 2048 + kc * 128:
                                            hs2 * 2048 + (kc + 1) * 128],
                                        pt_sb[:, kc * 512:(kc + 1) * 512],
                                        start=(kc == 0), stop=(kc == nkc - 1))
                                mm.inc(last)

                            @block.vector
                            def _(v):
                                mm.wait(v)
                                vx.inc(v.tensor_copy(sum_row[0:1, :], psS[0][0:1, :]))
                                v.reciprocal(sum_row[0:1, :], sum_row[0:1, :])
                                vx.inc(v.engine_nop())

                            @block.tensor
                            def _(tb):
                                vx.wait(tb)
                                mm.inc(tb.matmul(psS[1][:, :], ones_f32[0:1, :],
                                                 sum_row[0:1, :], start=True, stop=True))

                            @block.vector
                            def _(v, h=h, qg=qg):
                                mm.wait(v)
                                v.tensor_copy(rb_sb[:, :], psS[1][:, :])
                                vx.inc(v.tensor_mul(
                                    ctx_loc[:, h * T + qg * 512: h * T + (qg + 1) * 512],
                                    psA[2][:, :], rb_sb[:, :]))

                            if _DEBUG_TAP == "attn" and h == _DEBUG_H and qg == 1:
                                @block.gpsimd
                                def _(g):
                                    g.wait_ge(vx.h, vx.n)
                                    io.dma(g.dma_start(out=dbg[:, 14400:14912],
                                                       in_=rb_sb[:, :]))
                                    io.wait(g)

                        @block.vector
                        def _(v):
                            dnc.inc(v.engine_nop())

                if _DEBUG_TAP == "ctx":
                    @block.gpsimd
                    def _(g):
                        g.wait_ge(vx.h, vx.n)
                        io.dma(g.dma_start(out=dbg[:, 0:NF * T], in_=ctx_loc[:, :]))
                        io.wait(g)

                with nc.sbuf_tensor("w_st3", [128, 2 * NF * 128], BF16) as w_st3, \
                     nc.sbuf_tensor("oproj_st", [128, 2 * T], BF16) as oproj_st:

                    o_gate_mm = mm.n
                    o_gate_vx = vx.n
                    for m in range(NF):
                        slot = m % 2

                        @block.gpsimd
                        def _(g, m=m, slot=slot):
                            if m < 2:
                                g.wait_ge(mm.h, o_gate_mm)
                                g.wait_ge(vx.h, o_gate_vx)
                            else:
                                g.wait_ge(ac.h, ac.n - 2)
                            srcw = wo1[:, m * 128:(m + 1) * 128]
                            srcw = srcw.rearrange("(f p) c -> p f c", p=128)
                            wul.dma(g.dma_start(
                                out=w_st3[:, slot * NF * 128:(slot + 1) * NF * 128],
                                in_=srcw))

                        @block.tensor
                        def _(tb, m=m, slot=slot):
                            wul.wait(tb)
                            if m == 0:
                                ac.wait(tb)
                                vx.wait(tb)
                            last = None
                            for g2 in range(2):
                                if 2 * m + g2 >= 2:
                                    ac.wait(tb, ac.n + g2 - 1)
                                for f in range(NF):
                                    last = tb.matmul(
                                        psA[(2 * m + g2) % 2][:, :],
                                        w_st3[:, slot * NF * 128 + f * 128:
                                              slot * NF * 128 + (f + 1) * 128],
                                        ctx_loc[:, f * T + g2 * 512: f * T + (g2 + 1) * 512],
                                        start=(f == 0), stop=(f == NF - 1))
                                mm.inc(last)

                        @block.scalar
                        def _(s, m=m, slot=slot):
                            base = mm.n - 2
                            base_v = vx.n
                            for g2 in range(2):
                                s.wait_ge(mm.h, base + g2 + 1)
                                if m >= 2:
                                    s.wait_ge(vx.h, base_v + g2 - 3)
                                ac.inc(s.add(
                                    oproj_st[:, slot * T + g2 * 512:
                                             slot * T + (g2 + 1) * 512],
                                    psA[(2 * m + g2) % 2][:, :],
                                    bias4[:, 3 * 16 + m: 3 * 16 + m + 1]))

                        @block.vector
                        def _(v, m=m, slot=slot):
                            base = ac.n - 2
                            for g2 in range(2):
                                v.wait_ge(ac.h, base + g2 + 1)
                                vx.inc(v.tensor_add(
                                    z_bf[:, m * T + g2 * 512: m * T + (g2 + 1) * 512],
                                    x_fm[:, m * T + g2 * 512: m * T + (g2 + 1) * 512],
                                    oproj_st[:, slot * T + g2 * 512:
                                             slot * T + (g2 + 1) * 512]))

            emit_layernorm()

        if _N_PHASES >= 4:
            NH = 44
            p4 = ExitStack()
            with p4:
                E4 = p4.enter_context
                h_act = E4(nc.sbuf_tensor("h_act", [128, NH * T], BF16))
                scl_sb = E4(nc.sbuf_tensor("scl_sb", [128, 144], F32))
                m128 = E4(nc.sbuf_tensor("m128", [1, 128], F32))
                sr_row = E4(nc.sbuf_tensor("sr_row", [1, T], F32))

                p4_mm = mm.n
                p4_vx = vx.n
                p4_ac = ac.n

                with nc.sbuf_tensor("sgr", [44, 256], F32) as sgr, \
                     nc.sbuf_tensor("sdr", [16, 128], F32) as sdr, \
                     nc.sbuf_tensor("xneg", [128, T], F32) as xneg:

                    @block.gpsimd
                    def _(g):
                        g.wait_ge(mm.h, p4_mm)
                        g.wait_ge(vx.h, p4_vx)
                        g.wait_ge(ac.h, p4_ac)
                        ld.dma(g.dma_start(out=sgr[:, 0:128], in_=sg_in[:, :]))
                        ld.dma(g.dma_start(out=sgr[:, 128:256], in_=su_in[:, :]))
                        ld.dma(g.dma_start(out=sdr[:, :], in_=sd_in[:, :]))

                    @block.vector
                    def _(v):
                        v.wait_ge(mm.h, p4_mm)
                        vx.inc(v.memset(m128[0:1, :], -128.0))

                    @block.tensor
                    def _(tb):
                        ld.wait(tb)
                        vx.wait(tb)
                        ac.wait(tb)
                        tb.transpose(psS[0][:, 0:44], sgr[:, 0:128],
                                     ident_f32[0:44, 0:44])
                        tb.transpose(psS[0][:, 64:108], sgr[:, 128:256],
                                     ident_f32[0:44, 0:44])
                        tb.transpose(psS[0][:, 128:144], sdr[:, :],
                                     ident_f32[0:16, 0:16])
                        last = None
                        for f in range(NF):
                            tb.matmul(psS[1][0:1, :], ones_bf[:, 0:1],
                                      x_fm[:, f * T: f * T + 512],
                                      start=(f == 0), stop=(f == NF - 1))
                            last = tb.matmul(psA[0][0:1, :], ones_bf[:, 0:1],
                                             x_fm[:, f * T + 512:(f + 1) * T],
                                             start=(f == 0), stop=(f == NF - 1))
                        mm.inc(last)

                    @block.vector
                    def _(v):
                        mm.wait(v)
                        v.tensor_copy(scl_sb[:, :], psS[0][:, 0:144])
                        v.tensor_copy(sr_row[0:1, 0:512], psS[1][0:1, :])
                        vx.inc(v.tensor_copy(sr_row[0:1, 512:1024],
                                             psA[0][0:1, :]))

                    @block.tensor
                    def _(tb):
                        vx.wait(tb)
                        tb.matmul(psS[0][:, :], m128[0:1, :], sr_row[0:1, 0:512],
                                  start=True, stop=True)
                        mm.inc(tb.matmul(psS[1][:, :], m128[0:1, :],
                                         sr_row[0:1, 512:1024],
                                         start=True, stop=True))

                    @block.vector
                    def _(v):
                        mm.wait(v)
                        v.tensor_copy(xneg[:, 0:512], psS[0][:, :])
                        vx.inc(v.tensor_copy(xneg[:, 512:1024], psS[1][:, :]))

                    with nc.sbuf_tensor("wrow4", [128, D], BF16) as wrow4, \
                         nc.sbuf_tensor("wT4", [128, D], BF16) as wT4, \
                         nc.sbuf_tensor("sgb4", [128, T], BF16) as sgb4, \
                         nc.sbuf_tensor("t1g", [128, T], F32) as t1g:

                        ldg_wul = []
                        trg_mm = []
                        cpg_vx = []
                        gm_mm = {}
                        um_mm = {}
                        evg_vx = {}
                        evu_vx = {}
                        sil_ac = {}

                        for hc in range(NH):
                            for mi, wsrc in ((0, wqg), (1, wqu)):
                                seq = 2 * hc + mi

                                @block.gpsimd
                                def _(g, hc=hc, mi=mi, seq=seq, wsrc=wsrc):
                                    if seq == 0:
                                        g.wait_ge(mm.h, mm.n)
                                        g.wait_ge(vx.h, vx.n)
                                        g.wait_ge(ac.h, ac.n)
                                    else:
                                        g.wait_ge(mm.h, trg_mm[seq - 1])
                                    wul.dma(g.dma_start(
                                        out=wrow4[:, :],
                                        in_=wsrc[hc * 128:(hc + 1) * 128, :]))
                                    ldg_wul.append(wul.n)

                                @block.tensor
                                def _(tb, hc=hc, mi=mi, seq=seq):
                                    tb.wait_ge(wul.h, ldg_wul[seq])
                                    if seq >= 2:
                                        tb.wait_ge(vx.h, cpg_vx[seq - 2])
                                    else:
                                        vx.wait(tb)
                                    last = None
                                    for i in range(NF):
                                        last = tb.transpose(
                                            psB[i // 8][:, (i % 8) * 128:
                                                        (i % 8 + 1) * 128],
                                            wrow4[:, i * 128:(i + 1) * 128],
                                            ident_bf[:])
                                    mm.inc(last)
                                    trg_mm.append(mm.n)

                                @block.vector
                                def _(v, seq=seq):
                                    v.wait_ge(mm.h, trg_mm[seq])
                                    v.tensor_copy(wT4[:, 0:1024], psB[0][:, :])
                                    vx.inc(v.tensor_copy(wT4[:, 1024:2048],
                                                         psB[1][:, :]))
                                    cpg_vx.append(vx.n)

                                @block.tensor
                                def _(tb, hc=hc, mi=mi, seq=seq):
                                    tb.wait_ge(vx.h, cpg_vx[seq])
                                    pbase = 2 * mi
                                    for g2 in range(2):
                                        if hc > 0:
                                            ev = evg_vx if mi == 0 else evu_vx
                                            tb.wait_ge(vx.h, ev[(hc - 1, g2)])
                                        last = None
                                        for f in range(NF):
                                            last = tb.matmul(
                                                psA[pbase + g2][:, :],
                                                wT4[:, f * 128:(f + 1) * 128],
                                                x_fm[:, f * T + g2 * 512:
                                                     f * T + (g2 + 1) * 512],
                                                start=(f == 0), stop=(f == NF - 1))
                                        mm.inc(last)
                                        if mi == 0:
                                            gm_mm[(hc, g2)] = mm.n
                                        else:
                                            um_mm[(hc, g2)] = mm.n

                            @block.vector
                            def _(v, hc=hc):
                                for g2 in range(2):
                                    v.wait_ge(mm.h, gm_mm[(hc, g2)])
                                    vx.inc(v.tensor_add(
                                        t1g[:, g2 * 512:(g2 + 1) * 512],
                                        psA[g2][:, :],
                                        xneg[:, g2 * 512:(g2 + 1) * 512]))
                                    evg_vx[(hc, g2)] = vx.n

                            @block.scalar
                            def _(s, hc=hc):
                                for g2 in range(2):
                                    s.wait_ge(vx.h, evg_vx[(hc, g2)])
                                    ac.inc(s.activation(
                                        sgb4[:, g2 * 512:(g2 + 1) * 512],
                                        t1g[:, g2 * 512:(g2 + 1) * 512],
                                        AF.Silu, scale=scl_sb[:, hc:hc + 1]))
                                    sil_ac[(hc, g2)] = ac.n

                            @block.vector
                            def _(v, hc=hc):
                                for g2 in range(2):
                                    v.wait_ge(ac.h, sil_ac[(hc, g2)])
                                    v.wait_ge(mm.h, um_mm[(hc, g2)])
                                    vx.inc(v.tensor_add(
                                        t1g[:, g2 * 512:(g2 + 1) * 512],
                                        psA[2 + g2][:, :],
                                        xneg[:, g2 * 512:(g2 + 1) * 512]))
                                    evu_vx[(hc, g2)] = vx.n
                                    vx.inc(v.scalar_tensor_tensor(
                                        h_act[:, hc * T + g2 * 512:
                                              hc * T + (g2 + 1) * 512],
                                        t1g[:, g2 * 512:(g2 + 1) * 512],
                                        scl_sb[:, 64 + hc: 64 + hc + 1],
                                        sgb4[:, g2 * 512:(g2 + 1) * 512],
                                        op0=ALU.mult, op1=ALU.mult))

                with nc.sbuf_tensor("hneg", [128, T], F32) as hneg, \
                     nc.sbuf_tensor("wrow_d", [128, 1024], BF16) as wrow_d, \
                     nc.sbuf_tensor("wdT", [128, 5632], BF16) as wdT, \
                     nc.sbuf_tensor("t1d", [128, 512], F32) as t1d:

                    @block.tensor
                    def _(tb):
                        vx.wait(tb)
                        last = None
                        for j in range(NH):
                            tb.matmul(psS[0][0:1, :], ones_bf[:, 0:1],
                                      h_act[:, j * T: j * T + 512],
                                      start=(j == 0), stop=(j == NH - 1))
                            last = tb.matmul(psS[1][0:1, :], ones_bf[:, 0:1],
                                             h_act[:, j * T + 512:(j + 1) * T],
                                             start=(j == 0), stop=(j == NH - 1))
                        mm.inc(last)

                    @block.vector
                    def _(v):
                        mm.wait(v)
                        v.tensor_copy(sr_row[0:1, 0:512], psS[0][0:1, :])
                        vx.inc(v.tensor_copy(sr_row[0:1, 512:1024],
                                             psS[1][0:1, :]))

                    @block.tensor
                    def _(tb):
                        vx.wait(tb)
                        tb.matmul(psS[0][:, :], m128[0:1, :], sr_row[0:1, 0:512],
                                  start=True, stop=True)
                        mm.inc(tb.matmul(psS[1][:, :], m128[0:1, :],
                                         sr_row[0:1, 512:1024],
                                         start=True, stop=True))

                    @block.vector
                    def _(v):
                        mm.wait(v)
                        v.tensor_copy(hneg[:, 0:512], psS[0][:, :])
                        vx.inc(v.tensor_copy(hneg[:, 512:1024], psS[1][:, :]))

                    WB = [1024] * 5 + [512]
                    ldd_wul = []
                    trd_mm = []
                    cpd_vx = []
                    dnm_mm = {}
                    evd_vx = {}

                    for dc in range(NF):
                        for b in range(6):
                            seq = 6 * dc + b
                            c0 = b * 1024
                            w = WB[b]

                            @block.gpsimd
                            def _(g, dc=dc, b=b, seq=seq, c0=c0, w=w):
                                if seq < 2:
                                    g.wait_ge(vx.h, vx.n)
                                else:
                                    g.wait_ge(mm.h, trd_mm[seq - 1])
                                wul.dma(g.dma_start(
                                    out=wrow_d[:, 0:w],
                                    in_=wqd[dc * 128:(dc + 1) * 128, c0:c0 + w]))
                                ldd_wul.append(wul.n)

                            @block.tensor
                            def _(tb, seq=seq, b=b, w=w):
                                tb.wait_ge(wul.h, ldd_wul[seq])
                                if seq >= 2:
                                    tb.wait_ge(vx.h, cpd_vx[seq - 2])
                                elif seq == 0:
                                    vx.wait(tb)
                                last = None
                                for i in range(w // 128):
                                    last = tb.transpose(
                                        psB[seq % 2][:, i * 128:(i + 1) * 128],
                                        wrow_d[:, i * 128:(i + 1) * 128],
                                        ident_bf[:])
                                mm.inc(last)
                                trd_mm.append(mm.n)

                            @block.vector
                            def _(v, seq=seq, b=b, c0=c0, w=w):
                                v.wait_ge(mm.h, trd_mm[seq])
                                vx.inc(v.tensor_copy(wdT[:, c0:c0 + w],
                                                     psB[seq % 2][:, 0:w]))
                                cpd_vx.append(vx.n)

                        @block.tensor
                        def _(tb, dc=dc):
                            tb.wait_ge(vx.h, cpd_vx[6 * dc + 5])
                            for g2 in range(2):
                                if dc > 0:
                                    tb.wait_ge(vx.h, evd_vx[(dc - 1, g2)])
                                last = None
                                for j in range(NH):
                                    last = tb.matmul(
                                        psA[g2][:, :],
                                        wdT[:, j * 128:(j + 1) * 128],
                                        h_act[:, j * T + g2 * 512:
                                              j * T + (g2 + 1) * 512],
                                        start=(j == 0), stop=(j == NH - 1))
                                mm.inc(last)
                                dnm_mm[(dc, g2)] = mm.n

                        @block.vector
                        def _(v, dc=dc):
                            for g2 in range(2):
                                v.wait_ge(mm.h, dnm_mm[(dc, g2)])
                                v.tensor_add(t1d[:, :], psA[g2][:, :],
                                             hneg[:, g2 * 512:(g2 + 1) * 512])
                                vx.inc(v.scalar_tensor_tensor(
                                    z_bf[:, dc * T + g2 * 512:
                                         dc * T + (g2 + 1) * 512],
                                    t1d[:, :],
                                    scl_sb[:, 128 + dc: 128 + dc + 1],
                                    x_fm[:, dc * T + g2 * 512:
                                         dc * T + (g2 + 1) * 512],
                                    op0=ALU.mult, op1=ALU.add))
                                evd_vx[(dc, g2)] = vx.n

            emit_layernorm()

        with nc.sbuf_tensor("out_sb", [128, 2 * D], F32) as out_sb:
            @block.tensor
            def _(tb):
                vx.wait(tb)
                for t in range(NT):
                    for half in range(2):
                        i = 2 * t + half
                        if i >= 2:
                            vx.wait(tb, vx.n + i - 1)
                        pb = psB[half]
                        last = None
                        for f4 in range(4):
                            f = half * 8 + 2 * f4
                            last = tb.transpose(pb[:, f4 * 256:f4 * 256 + 128],
                                                x_fm[:, f * T + t * 128: f * T + (t + 1) * 128],
                                                ident_bf[:])
                            last = tb.transpose(pb[:, f4 * 256 + 128:(f4 + 1) * 256],
                                                x_fm[:, (f + 1) * T + t * 128:(f + 1) * T + (t + 1) * 128],
                                                ident_bf[:])
                        mm.inc(last)

            @block.vector
            def _(v):
                base = mm.n - 2 * NT
                io_base = io.n
                for t in range(NT):
                    if t >= 2:
                        v.wait_ge(io.h, io_base + (t - 1) * 16)
                    for half in range(2):
                        i = 2 * t + half
                        v.wait_ge(mm.h, base + i + 1)
                        vx.inc(v.tensor_copy(
                            out_sb[:, (t % 2) * D + half * 1024:(t % 2) * D + (half + 1) * 1024],
                            psB[half][:, :]))

            @block.sync
            def _(s):
                base = vx.n - 2 * NT
                for t in range(NT):
                    s.wait_ge(vx.h, base + 2 * (t + 1))
                    io.dma(s.dma_start(out=out[t * 128:(t + 1) * 128, :],
                                       in_=out_sb[:, (t % 2) * D:(t % 2 + 1) * D]))
                io.wait(s)

    return nc


def kernel(**inputs):
    u_full = np.ascontiguousarray(inputs["L0_attn_u"].reshape(8192, D))
    afac = np.ascontiguousarray(inputs["L0_attn_A_factors"].reshape(1, 32))
    dvec = np.ascontiguousarray(inputs["L0_attn_D_vec"].reshape(NF, 128))

    nc = build_program()
    in_maps = []
    for c in range(NCORES):
        m = {
            "u": u_full[c * T:(c + 1) * T],
            "wbr": inputs["L0_attn_W_b_rank"],
            "wbs": inputs["L0_attn_W_b_state"],
            "wcr": inputs["L0_attn_W_c_rank"],
            "wcm": inputs["L0_attn_W_c_model"],
            "afac": afac,
            "dvec": dvec,
        }
        m.update({
            "wg": inputs["L0_ffn_w_gate"],
            "wsu": inputs["L0_ffn_w_shared_up"],
            "wsd": inputs["L0_ffn_w_shared_down"],
            "wup": inputs["L0_ffn_w_up"],
            "wdn": inputs["L0_ffn_w_down"],
            "wq1": inputs["L1_attn_Wq"],
            "wk1": inputs["L1_attn_Wk"],
            "wv1": inputs["L1_attn_Wv"],
            "wo1": inputs["L1_attn_Wo"],
            "bq1": np.ascontiguousarray(inputs["L1_attn_bq"].reshape(16, 128)),
            "bk1": np.ascontiguousarray(inputs["L1_attn_bk"].reshape(16, 128)),
            "bv1": np.ascontiguousarray(inputs["L1_attn_bv"].reshape(16, 128)),
            "bo1": np.ascontiguousarray(inputs["L1_attn_bo"].reshape(16, 128)),
        })
        m.update({
            "wqg": inputs["L1_ffn_wq_gate"],
            "wqu": inputs["L1_ffn_wq_up"],
            "wqd": inputs["L1_ffn_wq_down"],
            "sg_in": np.ascontiguousarray(
                inputs["L1_ffn_s_gate"].reshape(44, 128)),
            "su_in": np.ascontiguousarray(
                inputs["L1_ffn_s_up"].reshape(44, 128)),
            "sd_in": np.ascontiguousarray(
                inputs["L1_ffn_s_down"].reshape(16, 128)),
        })
        par = c % 2
        pen = (1 - par) * 1.0e4
        m["cinfo"] = np.array([[par, par, 1024.0 * par, 1024.0 * par,
                                pen, pen, float(c), float(c)]], dtype=np.float32)
        in_maps.append(m)
    res = run_bass_kernel_spmd(nc, in_maps, core_ids=list(range(NCORES)))
    global _LAST_RES
    _LAST_RES = res
    out = np.concatenate([res.results[c]["out"] for c in range(NCORES)], axis=0)
    return out.reshape(4, 2048, 2048)


# version 04
# speedup vs baseline: 1.0624x; 1.0624x over previous
import numpy as np
from contextlib import ExitStack

import concourse.bass as bass
import concourse.mybir as mybir
from concourse.bass import ds
from concourse.expressions import RuntimeValue as RV
from concourse.bass_utils import run_bass_kernel_spmd

F32 = mybir.dt.float32
BF16 = mybir.dt.bfloat16
I32 = mybir.dt.int32
AF = mybir.ActivationFunctionType
ALU = mybir.AluOpType

NCORES = 8
T = 1024
D = 2048
NF = 16
NT = 8
EPS = 1e-5
NEXP = 64
EGRP = 8
CAP = 1024

_N_PHASES = 99
_DEBUG_TAP = None
_DEBUG_H = 0
_DEBUG_KV = "k"
_LAST_RES = None


class Sem:

    def __init__(self, ctx, nc, name):
        self.h = ctx.enter_context(nc.semaphore(name))
        self.n = 0

    def inc(self, inst, k=1):
        inst.then_inc(self.h, k)
        self.n += k
        return inst

    def dma(self, inst):
        return self.inc(inst, 16)

    def wait(self, eng, n=None):
        eng.wait_ge(self.h, self.n if n is None else n)


def build_program():
    nc = bass.Bass("TRN2", target_bir_lowering=False, num_devices=NCORES)

    u = nc.dram_tensor("u", [T, D], F32, kind="ExternalInput").ap()
    wbr = nc.dram_tensor("wbr", [D, 64], F32, kind="ExternalInput").ap()
    wbs = nc.dram_tensor("wbs", [64, 256], F32, kind="ExternalInput").ap()
    wcr = nc.dram_tensor("wcr", [256, 64], F32, kind="ExternalInput").ap()
    wcm = nc.dram_tensor("wcm", [64, D], F32, kind="ExternalInput").ap()
    afac = nc.dram_tensor("afac", [1, 32], F32, kind="ExternalInput").ap()
    dvec = nc.dram_tensor("dvec", [NF, 128], F32, kind="ExternalInput").ap()
    wg = nc.dram_tensor("wg", [D, 64], F32, kind="ExternalInput").ap()
    wsu = nc.dram_tensor("wsu", [D, 1024], F32, kind="ExternalInput").ap()
    wsd = nc.dram_tensor("wsd", [512, D], F32, kind="ExternalInput").ap()
    wup = nc.dram_tensor("wup", [NEXP, D, 512], F32, kind="ExternalInput").ap()
    wdn = nc.dram_tensor("wdn", [NEXP, 256, D], F32, kind="ExternalInput").ap()

    wq1 = nc.dram_tensor("wq1", [D, D], F32, kind="ExternalInput").ap()
    wk1 = nc.dram_tensor("wk1", [D, D], F32, kind="ExternalInput").ap()
    wv1 = nc.dram_tensor("wv1", [D, D], F32, kind="ExternalInput").ap()
    wo1 = nc.dram_tensor("wo1", [D, D], F32, kind="ExternalInput").ap()
    bq1 = nc.dram_tensor("bq1", [16, 128], F32, kind="ExternalInput").ap()
    bk1 = nc.dram_tensor("bk1", [16, 128], F32, kind="ExternalInput").ap()
    bv1 = nc.dram_tensor("bv1", [16, 128], F32, kind="ExternalInput").ap()
    bo1 = nc.dram_tensor("bo1", [16, 128], F32, kind="ExternalInput").ap()

    cnt_in = nc.dram_tensor("cnt_in", [1, 64], F32)
    cnt_out = nc.dram_tensor("cnt_out", [NCORES, 64], F32)
    offs_dram = nc.dram_tensor("offs_dram", [NCORES, 64], F32)
    bar_in = nc.dram_tensor("bar_in", [1, 2], F32)
    bar_out = nc.dram_tensor("bar_out", [NCORES, 2], F32)
    cinfo = nc.dram_tensor("cinfo", [1, 8], F32, kind="ExternalInput").ap()
    sh_q = nc.dram_tensor("sh_q", [D, D], BF16, addr_space="Shared")
    sh_k = nc.dram_tensor("sh_k", [D, D], BF16, addr_space="Shared")
    sh_v = nc.dram_tensor("sh_v", [D, D], BF16, addr_space="Shared")
    sh_ctx = nc.dram_tensor("sh_ctx", [D, D], BF16, addr_space="Shared")

    wqgT = nc.dram_tensor("wqgT", [D, 5632], I32, kind="ExternalInput").ap()
    wquT = nc.dram_tensor("wquT", [D, 5632], I32, kind="ExternalInput").ap()
    wqdT = nc.dram_tensor("wqdT", [5632, D], I32, kind="ExternalInput").ap()
    sg_in = nc.dram_tensor("sg_in", [44, 128], F32, kind="ExternalInput").ap()
    su_in = nc.dram_tensor("su_in", [44, 128], F32, kind="ExternalInput").ap()
    sd_in = nc.dram_tensor("sd_in", [16, 128], F32, kind="ExternalInput").ap()

    out = nc.dram_tensor("out", [T, D], F32, kind="ExternalOutput").ap()
    dbg = (nc.dram_tensor("dbg", [128, NF * T], F32, kind="ExternalOutput").ap()
           if _DEBUG_TAP else None)

    ctx = ExitStack()
    with ctx:
        E = ctx.enter_context

        x_fm = E(nc.sbuf_tensor("x_fm", [128, NF * T], BF16))
        z_bf = E(nc.sbuf_tensor("z_bf", [128, NF * T], BF16))
        ones_bf = E(nc.sbuf_tensor("ones_bf", [128, 128], BF16))
        ones_f32 = E(nc.sbuf_tensor("ones_f32", [128, 128], F32))
        ident_bf = E(nc.sbuf_tensor("ident_bf", [128, 128], BF16))
        ident_f32 = E(nc.sbuf_tensor("ident_f32", [128, 128], F32))
        iota_jp = E(nc.sbuf_tensor("iota_jp", [128, 128], I32))
        ln_rs = E(nc.sbuf_tensor("ln_rs", [128, T], F32))
        ln_mr = E(nc.sbuf_tensor("ln_mr", [128, T], F32))
        ln_rs_bf = E(nc.sbuf_tensor("ln_rs_bf", [128, T], BF16))
        ln_mr_bf = E(nc.sbuf_tensor("ln_mr_bf", [128, T], BF16))
        stat = E(nc.sbuf_tensor("stat", [128, 1024], F32))
        eps_col = E(nc.sbuf_tensor("eps_col", [128, 1], F32))

        psA = [E(nc.psum_tensor(f"psA{i}", [128, 512], F32)) for i in range(4)]
        psB = [E(nc.psum_tensor(f"psB{i}", [128, 1024], BF16)) for i in range(2)]
        psS = [E(nc.psum_tensor(f"psS{i}", [128, 512], F32)) for i in range(2)]

        ld = Sem(ctx, nc, "s_ld")
        mm = Sem(ctx, nc, "s_mm")
        vx = Sem(ctx, nc, "s_vx")
        ac = Sem(ctx, nc, "s_ac")
        io = Sem(ctx, nc, "s_io")
        ini = Sem(ctx, nc, "s_ini")
        wul = Sem(ctx, nc, "s_wul")
        wdl = Sem(ctx, nc, "s_wdl")
        upc = Sem(ctx, nc, "s_upc")
        dnc = Sem(ctx, nc, "s_dnc")
        cc = Sem(ctx, nc, "s_cc")

        block = E(nc.Block())

        r_pid = nc.gpsimd.alloc_register("r_pid")
        r_tmp = nc.gpsimd.alloc_register("r_tmp")

        ln_seq = [0]

        def emit_layernorm():
            ln_seq[0] += 1
            with nc.sbuf_tensor(f"zsq_ln{ln_seq[0]}", [128, NF * T], BF16) as zsq:
                @block.scalar
                def _(s):
                    vx.wait(s)
                    last = None
                    for f in range(NF):
                        last = s.activation(zsq[:, f * T:(f + 1) * T],
                                            z_bf[:, f * T:(f + 1) * T], AF.Square)
                    ac.inc(last)

                @block.tensor
                def _(tb):
                    vx.wait(tb)
                    ac.wait(tb)
                    last = None
                    for g in range(2):
                        for f in range(NF):
                            last = tb.matmul(
                                psS[g][:, :], ones_bf[:],
                                z_bf[:, f * T + g * 512: f * T + (g + 1) * 512],
                                start=(f == 0), stop=(f == NF - 1))
                    for g in range(2):
                        for f in range(NF):
                            last = tb.matmul(
                                psA[2 + g][:, :], ones_bf[:],
                                zsq[:, f * T + g * 512: f * T + (g + 1) * 512],
                                start=(f == 0), stop=(f == NF - 1))
                    mm.inc(last)

                @block.vector
                def _(v):
                    mm.wait(v)
                    for g in range(2):
                        sl = slice(g * 512, (g + 1) * 512)
                        v.tensor_scalar_mul(ln_mr[:, sl], psS[g][:, :], 1.0 / D)
                        v.tensor_scalar_mul(stat[:, sl], psA[2 + g][:, :], 1.0 / D)
                        v.tensor_mul(ln_rs[:, sl], ln_mr[:, sl], ln_mr[:, sl])
                        v.tensor_sub(stat[:, sl], stat[:, sl], ln_rs[:, sl])
                    vx.inc(v.engine_nop())

                @block.scalar
                def _(s):
                    vx.wait(s)
                    ac.inc(s.activation(ln_rs[:, :], stat[:, :], AF.Sqrt,
                                        bias=eps_col[:, 0:1]))

                @block.vector
                def _(v):
                    ac.wait(v)
                    v.reciprocal(ln_rs[:, :], ln_rs[:, :])
                    v.tensor_mul(ln_mr[:, :], ln_mr[:, :], ln_rs[:, :])
                    v.tensor_copy(ln_rs_bf[:, :], ln_rs[:, :])
                    v.tensor_copy(ln_mr_bf[:, :], ln_mr[:, :])
                    for f in range(NF):
                        v.tensor_mul(x_fm[:, f * T:(f + 1) * T],
                                     z_bf[:, f * T:(f + 1) * T], ln_rs_bf[:, :])
                        v.tensor_sub(x_fm[:, f * T:(f + 1) * T],
                                     x_fm[:, f * T:(f + 1) * T], ln_mr_bf[:, :])
                    vx.inc(v.engine_nop())

        with nc.sbuf_tensor("x_tm", [128, NT * D], BF16) as x_tm:
            @block.gpsimd
            def _(g):
                pid = g.compute_val(g.partition_id())
                g.reg_mov(r_pid, pid)
                for t in range(NT):
                    ld.dma(g.dma_start(out=x_tm[:, t * D:(t + 1) * D],
                                       in_=u[t * 128:(t + 1) * 128, :]))
                ini.inc(g.iota(iota_jp[:], pattern=[[1, 128]], base=0,
                               channel_multiplier=-1))

            @block.vector
            def _(v):
                v.memset(ones_bf[:], 1.0)
                v.memset(ones_f32[:], 1.0)
                v.memset(eps_col[:], float(EPS))
                ini.wait(v)
                v.tensor_single_scalar(ident_bf[:], iota_jp[:], 0, ALU.is_equal)
                vx.inc(v.tensor_single_scalar(ident_f32[:], iota_jp[:], 0, ALU.is_equal))

            @block.tensor
            def _(tb):
                ld.wait(tb)
                vx.wait(tb)
                for f in range(NF):
                    if f >= 2:
                        vx.wait(tb, vx.n + f - 1)
                    pb = psB[f % 2]
                    last = None
                    for t in range(NT):
                        last = tb.transpose(pb[:, t * 128:(t + 1) * 128],
                                            x_tm[:, t * D + f * 128: t * D + (f + 1) * 128],
                                            ident_bf[:])
                    mm.inc(last)

            @block.vector
            def _(v):
                base = mm.n - NF
                for f in range(NF):
                    v.wait_ge(mm.h, base + f + 1)
                    vx.inc(v.tensor_copy(x_fm[:, f * T:(f + 1) * T], psB[f % 2][:, :]))

        if _N_PHASES >= 1:
            p1 = ExitStack()
            with p1:
                E1 = p1.enter_context
                wbr_sb = E1(nc.sbuf_tensor("wbr_sb", [128, NF * 64], BF16))
                wbs_sb = E1(nc.sbuf_tensor("wbs_sb", [64, 256], BF16))
                wcr_sb = E1(nc.sbuf_tensor("wcr_sb", [128, 2 * 64], BF16))
                wcm_sb = E1(nc.sbuf_tensor("wcm_sb", [64, D], BF16))
                afac_sb = E1(nc.sbuf_tensor("afac_sb", [1, 32], F32))
                abc = E1(nc.sbuf_tensor("abc", [128, 32], F32))
                dvecr = E1(nc.sbuf_tensor("dvecr", [16, 128], F32))
                dv_sb = E1(nc.sbuf_tensor("dv_sb", [128, 16], F32))
                t1_bf = E1(nc.sbuf_tensor("t1_bf", [64, T], BF16))
                st_a = E1(nc.sbuf_tensor("st_a", [128, NT * 256], F32))
                st_b = E1(nc.sbuf_tensor("st_b", [128, NT * 256], F32))
                stT_bf = E1(nc.sbuf_tensor("stT_bf", [128, 2 * T], BF16))
                y1_bf = E1(nc.sbuf_tensor("y1_bf", [64, T], BF16))

                gate_mm = mm.n

                @block.gpsimd
                def _(g):
                    g.wait_ge(mm.h, gate_mm)
                    for f in range(NF):
                        ld.dma(g.dma_start(out=wbr_sb[:, f * 64:(f + 1) * 64],
                                           in_=wbr[f * 128:(f + 1) * 128, :]))
                    ld.dma(g.dma_start(out=wbs_sb[:, :], in_=wbs[:, :]))
                    for j in range(2):
                        ld.dma(g.dma_start(out=wcr_sb[:, j * 64:(j + 1) * 64],
                                           in_=wcr[j * 128:(j + 1) * 128, :]))
                    ld.dma(g.dma_start(out=wcm_sb[:, :], in_=wcm[:, :]))
                    ld.dma(g.dma_start(out=afac_sb[:, :], in_=afac[:, :]))
                    ld.dma(g.dma_start(out=dvecr[:, :], in_=dvec[:, :]))

                @block.tensor
                def _(tb):
                    ld.wait(tb)
                    vx.wait(tb)
                    tb.matmul(psS[0][:, 0:32], ones_f32[0:1, :], afac_sb[:, :],
                              start=True, stop=True)
                    mm.inc(tb.transpose(psS[1][:, 0:16], dvecr[:, :],
                                        ident_f32[0:16, 0:16]))

                @block.vector
                def _(v):
                    mm.wait(v)
                    v.tensor_copy(abc[:, :], psS[0][:, 0:32])
                    vx.inc(v.tensor_scalar_add(dv_sb[:, :], psS[1][:, 0:16], 1.0))

                @block.tensor
                def _(tb):
                    vx.wait(tb)
                    for g in range(2):
                        if g == 1:
                            vx.wait(tb, vx.n + 1)
                        last = None
                        for f in range(NF):
                            last = tb.matmul(psA[0][0:64, :],
                                             wbr_sb[:, f * 64:(f + 1) * 64],
                                             x_fm[:, f * T + g * 512: f * T + (g + 1) * 512],
                                             start=(f == 0), stop=(f == NF - 1))
                        mm.inc(last)

                @block.vector
                def _(v):
                    base = mm.n - 2
                    for g in range(2):
                        v.wait_ge(mm.h, base + g + 1)
                        vx.inc(v.tensor_copy(t1_bf[0:64, g * 512:(g + 1) * 512],
                                             psA[0][0:64, :]))

                @block.tensor
                def _(tb):
                    vx.wait(tb)
                    for i in range(NT):
                        if i >= 2:
                            vx.wait(tb, vx.n + i - 1)
                        mm.inc(tb.matmul(psA[i % 2][:, 0:256],
                                         t1_bf[0:64, i * 128:(i + 1) * 128],
                                         wbs_sb[0:64, :], start=True, stop=True))

                @block.vector
                def _(v):
                    base = mm.n - NT
                    for i in range(NT):
                        v.wait_ge(mm.h, base + i + 1)
                        vx.inc(v.tensor_copy(st_a[:, i * 256:(i + 1) * 256],
                                             psA[i % 2][:, 0:256]))

                @block.vector
                def _(v):
                    vx.wait(v)
                    src, dst = st_a, st_b
                    for lvl in range(8):
                        s = 1 << (7 - lvl)
                        a00 = abc[:, lvl * 4 + 0: lvl * 4 + 1]
                        a01 = abc[:, lvl * 4 + 1: lvl * 4 + 2]
                        a10 = abc[:, lvl * 4 + 2: lvl * 4 + 3]
                        a11 = abc[:, lvl * 4 + 3: lvl * 4 + 4]

                        def view(buf, bit, s=s):
                            ap = buf[:, :].rearrange("p (t g i) -> p t g i",
                                                     t=NT, g=256 // (2 * s), i=2 * s)
                            return ap[:, :, :, bit * s:(bit + 1) * s]

                        x0s, x1s = view(src, 0), view(src, 1)
                        y0s, y1s = view(dst, 0), view(dst, 1)
                        v.tensor_scalar_mul(y0s, x1s, a10)
                        v.scalar_tensor_tensor(y0s, x0s, a00, y0s, op0=ALU.mult, op1=ALU.add)
                        v.tensor_scalar_mul(y1s, x1s, a11)
                        v.scalar_tensor_tensor(y1s, x0s, a01, y1s, op0=ALU.mult, op1=ALU.add)
                        src, dst = dst, src
                    vx.inc(v.engine_nop())

                @block.tensor
                def _(tb):
                    vx.wait(tb)
                    k = 0
                    for j in range(2):
                        for q in range(2):
                            if k >= 2:
                                vx.wait(tb, vx.n + k - 1)
                            last = None
                            for t4 in range(4):
                                i = q * 4 + t4
                                last = tb.transpose(
                                    psA[q][:, t4 * 128:(t4 + 1) * 128],
                                    st_a[:, i * 256 + j * 128: i * 256 + (j + 1) * 128],
                                    ident_f32[:])
                            mm.inc(last)
                            k += 1

                @block.vector
                def _(v):
                    base = mm.n - 4
                    k = 0
                    for j in range(2):
                        for q in range(2):
                            k += 1
                            v.wait_ge(mm.h, base + k)
                            vx.inc(v.tensor_copy(
                                stT_bf[:, j * T + q * 512: j * T + (q + 1) * 512],
                                psA[q][:, :]))

                @block.tensor
                def _(tb):
                    vx.wait(tb)
                    for g in range(2):
                        if g == 1:
                            vx.wait(tb, vx.n + 1)
                        last = None
                        for j in range(2):
                            last = tb.matmul(psA[2][0:64, :],
                                             wcr_sb[:, j * 64:(j + 1) * 64],
                                             stT_bf[:, j * T + g * 512: j * T + (g + 1) * 512],
                                             start=(j == 0), stop=(j == 1))
                        mm.inc(last)

                @block.vector
                def _(v):
                    base = mm.n - 2
                    for g in range(2):
                        v.wait_ge(mm.h, base + g + 1)
                        vx.inc(v.tensor_copy(y1_bf[0:64, g * 512:(g + 1) * 512],
                                             psA[2][0:64, :]))

                @block.tensor
                def _(tb):
                    vx.wait(tb)
                    for f in range(NF):
                        for g in range(2):
                            i = 2 * f + g
                            if i >= 2:
                                vx.wait(tb, vx.n + i - 1)
                            mm.inc(tb.matmul(psA[i % 2][:, :],
                                             wcm_sb[0:64, f * 128:(f + 1) * 128],
                                             y1_bf[0:64, g * 512:(g + 1) * 512],
                                             start=True, stop=True))

                @block.vector
                def _(v):
                    base = mm.n - 2 * NF
                    for f in range(NF):
                        for g in range(2):
                            i = 2 * f + g
                            v.wait_ge(mm.h, base + i + 1)
                            vx.inc(v.scalar_tensor_tensor(
                                z_bf[:, f * T + g * 512: f * T + (g + 1) * 512],
                                x_fm[:, f * T + g * 512: f * T + (g + 1) * 512],
                                dv_sb[:, f:f + 1],
                                psA[i % 2][:, :],
                                op0=ALU.mult, op1=ALU.add))

            emit_layernorm()

        if _N_PHASES >= 2:
            coeff_bf = E(nc.sbuf_tensor("coeff_bf", [64, T], BF16))
            p2_gate_mm = mm.n
            p2_gate_vx = vx.n
            p2_gate_ac = ac.n

            p2a = ExitStack()
            with p2a:
                E2 = p2a.enter_context
                wg_sb = E2(nc.sbuf_tensor("wg_sb", [128, NF * 64], BF16))
                wsu_sb = E2(nc.sbuf_tensor("wsu_sb", [128, NF * 1024], BF16))
                wsd_sb = E2(nc.sbuf_tensor("wsd_sb", [128, 4 * D], BF16))
                h_s = E2(nc.sbuf_tensor("h_s", [128, 4 * T], BF16))
                sgb = E2(nc.sbuf_tensor("sgb", [128, 2 * 512], BF16))
                sc_fm = E2(nc.sbuf_tensor("sc_fm", [64, T], F32))
                sc_tm = E2(nc.sbuf_tensor("sc_tm", [128, NT * 64], F32))
                g_tm = E2(nc.sbuf_tensor("g_tm", [128, NT * 64], F32))
                t8 = E2(nc.sbuf_tensor("t8", [128, 8], F32))
                ts_col = E2(nc.sbuf_tensor("ts_col", [128, NT], F32))
                coeff_fm = E2(nc.sbuf_tensor("coeff_fm", [64, T], F32))
                m_fm = E2(nc.sbuf_tensor("m_fm", [64, T], F32))
                ltm = E2(nc.sbuf_tensor("ltm", [8, 8], F32))
                stat2 = E2(nc.sbuf_tensor("stat2", [128, 1024], F32))
                cnts_sb = E2(nc.sbuf_tensor("cnts_sb", [8, 64], F32))
                offs_sb = E2(nc.sbuf_tensor("offs_sb", [8, 64], F32))
                offc = E2(nc.sbuf_tensor("offc", [64, 1], F32))

                @block.gpsimd
                def _(g):
                    g.wait_ge(mm.h, p2_gate_mm)
                    g.wait_ge(vx.h, p2_gate_vx)
                    for f in range(NF):
                        ld.dma(g.dma_start(out=wg_sb[:, f * 64:(f + 1) * 64],
                                           in_=wg[f * 128:(f + 1) * 128, :]))
                    for f in range(NF):
                        ld.dma(g.dma_start(out=wsu_sb[:, f * 1024:(f + 1) * 1024],
                                           in_=wsu[f * 128:(f + 1) * 128, :]))
                    for j in range(4):
                        ld.dma(g.dma_start(out=wsd_sb[:, j * D:(j + 1) * D],
                                           in_=wsd[j * 128:(j + 1) * 128, :]))

                @block.tensor
                def _(tb):
                    ld.wait(tb, ld.n - 20 * 16)
                    vx.wait(tb)
                    for g2 in range(2):
                        if g2 == 1:
                            ac.wait(tb, ac.n + 1)
                        last = None
                        for f in range(NF):
                            last = tb.matmul(psS[0][0:64, :],
                                             wg_sb[:, f * 64:(f + 1) * 64],
                                             x_fm[:, f * T + g2 * 512: f * T + (g2 + 1) * 512],
                                             start=(f == 0), stop=(f == NF - 1))
                        mm.inc(last)

                @block.scalar
                def _(s):
                    base = mm.n - 2
                    for g2 in range(2):
                        s.wait_ge(mm.h, base + g2 + 1)
                        ac.inc(s.activation(sc_fm[0:64, g2 * 512:(g2 + 1) * 512],
                                            psS[0][0:64, :], AF.Sigmoid))

                @block.tensor
                def _(tb):
                    ac.wait(tb)
                    last = None
                    for t in range(NT):
                        last = tb.transpose(psS[1][:, t * 64:(t + 1) * 64],
                                            sc_fm[0:64, t * 128:(t + 1) * 128],
                                            ident_f32[0:64, 0:64])
                    mm.inc(last)

                @block.vector
                def _(v):
                    mm.wait(v)
                    v.tensor_copy(sc_tm[:, :], psS[1][:, :])
                    for t in range(NT):
                        scs = sc_tm[:, t * 64:(t + 1) * 64]
                        gts = g_tm[:, t * 64:(t + 1) * 64]
                        v.max(out=t8[:], in_=scs)
                        v.memset(t8[:, 6:8], -1e30)
                        v.tensor_copy(gts, scs)
                        v.match_replace(out=gts, in_to_replace=t8[:, 0:8],
                                        in_values=gts, imm_value=0.0)
                        v.tensor_sub(gts, scs, gts)
                    vx.inc(v.engine_nop())

                @block.tensor
                def _(tb):
                    vx.wait(tb)
                    last = None
                    for t in range(NT):
                        last = tb.transpose(psA[t // 4][0:64, (t % 4) * 128:(t % 4 + 1) * 128],
                                            g_tm[:, t * 64:(t + 1) * 64],
                                            ident_f32[:])
                    mm.inc(last)

                @block.vector
                def _(v):
                    mm.wait(v)
                    for q in range(2):
                        v.tensor_copy(coeff_fm[0:64, q * 512:(q + 1) * 512],
                                      psA[q][0:64, :])
                    v.tensor_single_scalar(m_fm[0:64, :], coeff_fm[0:64, :], 0.0, ALU.is_gt)
                    v.tensor_tensor_scan(stat[0:64, :], m_fm[0:64, :], m_fm[0:64, :],
                                         0.0, ALU.add, ALU.bypass)
                    vx.inc(v.engine_nop())

                @block.tensor
                def _(tb):
                    vx.wait(tb)
                    last = None
                    for q in range(2):
                        last = tb.matmul(psA[2 + q][:, :], ones_f32[0:64, :],
                                         coeff_fm[0:64, q * 512:(q + 1) * 512],
                                         start=True, stop=True)
                    mm.inc(last)

                @block.vector
                def _(v):
                    mm.wait(v)
                    for q in range(2):
                        sl = slice(q * 512, (q + 1) * 512)
                        v.reciprocal(stat2[:, sl], psA[2 + q][:, :])
                    v.tensor_mul(coeff_fm[0:64, :], coeff_fm[0:64, :], stat2[0:64, :])
                    vx.inc(v.engine_nop())

                @block.sync
                def _(s):
                    vx.wait(s)
                    io.dma(s.dma_start(out=cnt_in.ap()[0:1, :],
                                       in_=stat[0:64, (T - 1):T]))

                @block.gpsimd
                def _(g):
                    io.wait(g)
                    cc.inc(g.collective_compute(
                        "AllGather", ALU.bypass,
                        replica_groups=[list(range(NCORES))],
                        ins=[cnt_in.ap()], outs=[cnt_out.ap()]))
                    cc.wait(g)
                    ld.dma(g.dma_start(out=cnts_sb[:, :], in_=cnt_out.ap()[:, :]))

                shared_ld = ld.n - 16

                @block.tensor
                def _(tb):
                    ld.wait(tb, shared_ld)
                    vx.wait(tb)
                    k = 0
                    for j in range(4):
                        for g2 in range(2):
                            py = psA[0 + (k % 2)]
                            pg = psA[2 + (k % 2)]
                            if k >= 2:
                                vx.wait(tb, vx.n + k - 1)
                            last = None
                            for f in range(NF):
                                tb.matmul(py[:, :],
                                          wsu_sb[:, f * 1024 + j * 128: f * 1024 + (j + 1) * 128],
                                          x_fm[:, f * T + g2 * 512: f * T + (g2 + 1) * 512],
                                          start=(f == 0), stop=(f == NF - 1))
                                last = tb.matmul(pg[:, :],
                                                 wsu_sb[:, f * 1024 + 512 + j * 128: f * 1024 + 512 + (j + 1) * 128],
                                                 x_fm[:, f * T + g2 * 512: f * T + (g2 + 1) * 512],
                                                 start=(f == 0), stop=(f == NF - 1))
                            mm.inc(last)
                            k += 1

                @block.scalar
                def _(s):
                    base = mm.n - 8
                    base_v = vx.n
                    for k in range(8):
                        s.wait_ge(mm.h, base + k + 1)
                        if k >= 2:
                            s.wait_ge(vx.h, base_v + k - 1)
                        ac.inc(s.activation(sgb[:, (k % 2) * 512:(k % 2 + 1) * 512],
                                            psA[2 + (k % 2)][:, :], AF.Silu))

                @block.vector
                def _(v):
                    base_a = ac.n - 8
                    k = 0
                    for j in range(4):
                        for g2 in range(2):
                            v.wait_ge(ac.h, base_a + k + 1)
                            vx.inc(v.tensor_mul(
                                h_s[:, j * T + g2 * 512: j * T + (g2 + 1) * 512],
                                psA[0 + (k % 2)][:, :],
                                sgb[:, (k % 2) * 512:(k % 2 + 1) * 512]))
                            k += 1

                @block.tensor
                def _(tb):
                    vx.wait(tb)
                    for f in range(NF):
                        for g2 in range(2):
                            i = 2 * f + g2
                            if i >= 2:
                                vx.wait(tb, vx.n + i - 1)
                            last = None
                            for j in range(4):
                                last = tb.matmul(
                                    psA[i % 2][:, :],
                                    wsd_sb[:, j * D + f * 128: j * D + (f + 1) * 128],
                                    h_s[:, j * T + g2 * 512: j * T + (g2 + 1) * 512],
                                    start=(j == 0), stop=(j == 3))
                            mm.inc(last)

                @block.vector
                def _(v):
                    base = mm.n - 2 * NF
                    for f in range(NF):
                        for g2 in range(2):
                            i = 2 * f + g2
                            v.wait_ge(mm.h, base + i + 1)
                            vx.inc(v.tensor_add(
                                z_bf[:, f * T + g2 * 512: f * T + (g2 + 1) * 512],
                                x_fm[:, f * T + g2 * 512: f * T + (g2 + 1) * 512],
                                psA[i % 2][:, :]))

                @block.vector
                def _(v):
                    ld.wait(v)
                    vx.inc(v.tensor_single_scalar(ltm[:, :], iota_jp[0:8, 0:8],
                                                  0, ALU.is_gt))

                @block.tensor
                def _(tb):
                    vx.wait(tb)
                    ac.wait(tb)
                    mm.inc(tb.matmul(psS[0][0:8, 0:64], ltm[:, :], cnts_sb[:, :],
                                     start=True, stop=True))

                @block.vector
                def _(v):
                    mm.wait(v)
                    vx.inc(v.tensor_copy(offs_sb[:, :], psS[0][0:8, 0:64]))

                @block.sync
                def _(s):
                    vx.wait(s)
                    io.dma(s.dma_start(out=offs_dram.ap()[:, :], in_=offs_sb[:, :]))

                @block.gpsimd
                def _(g):
                    io.wait(g)
                    ld.dma(g.dma_start(out=offc[:, :],
                                       in_=offs_dram.ap()[ds(RV(r_pid, 0, 7), 1), :]))

                @block.vector
                def _(v):
                    ld.wait(v)
                    v.tensor_scalar_add(stat[0:64, :], stat[0:64, :], offc[0:64, 0:1])
                    v.tensor_single_scalar(m_fm[0:64, :], stat[0:64, :], float(CAP),
                                           ALU.is_le)
                    v.tensor_mul(coeff_fm[0:64, :], coeff_fm[0:64, :], m_fm[0:64, :])
                    vx.inc(v.tensor_copy(coeff_bf[0:64, :], coeff_fm[0:64, :]))

                if _DEBUG_TAP == "coeff":
                    @block.sync
                    def _(s):
                        vx.wait(s)
                        io.dma(s.dma_start(out=dbg[0:64, 0:T], in_=coeff_fm[0:64, :]))
                        io.dma(s.dma_start(out=dbg[64:128, 0:T], in_=stat[0:64, :]))
                        io.dma(s.dma_start(out=dbg[0:8, T:T + 64], in_=cnts_sb[:, :]))
                        io.dma(s.dma_start(out=dbg[0:64, T + 64:T + 65], in_=offc[:, :], allow_slow_non_contiguous=True))
                        io.dma(s.dma_start(out=dbg[0:64, T + 65:T + 65 + 512],
                                           in_=sc_fm[0:64, 0:512]))
                        io.wait(s)

            p2b = ExitStack()
            with p2b:
                E2 = p2b.enter_context
                wup_sb = E2(nc.sbuf_tensor("wup_sb", [128, 2 * NF * 512], BF16))
                wdnf = E2(nc.sbuf_tensor("wdnf", [128, 2 * 16 * 128], BF16))
                h_grp = E2(nc.sbuf_tensor("h_grp", [128, 2 * EGRP * T], BF16))
                cbc = E2(nc.sbuf_tensor("cbc", [128, EGRP * T], BF16))
                sgb2 = E2(nc.sbuf_tensor("sgb2", [128, 2 * 512], BF16))
                tvb = E2(nc.sbuf_tensor("tvb", [128, 2 * 512], BF16))
                oh_i32 = E2(nc.sbuf_tensor("oh_i32", [64, EGRP * 128], I32))
                oh_bf = E2(nc.sbuf_tensor("oh_bf", [64, EGRP * 128], BF16))

                gate2_mm = mm.n
                gate2_vx = vx.n
                cmp_vx = []
                stageA_ac = []
                stageB_vx = []

                for grp in range(NEXP // EGRP):
                    @block.gpsimd
                    def _(g, grp=grp):
                        if grp == 0:
                            g.wait_ge(mm.h, gate2_mm)
                            g.wait_ge(vx.h, gate2_vx)
                        else:
                            g.wait_ge(vx.h, cmp_vx[grp - 1])
                        ini.inc(g.iota(oh_i32[:, :], pattern=[[-1, EGRP], [0, 128]],
                                       base=-(grp * EGRP), channel_multiplier=1))
                        stageA_ac.append(ac.n)
                        for e4 in range(EGRP):
                            e = grp * EGRP + e4
                            if e >= 2:
                                if e4 >= 2:
                                    g.wait_ge(ac.h, ac.n + 4 * (e4 - 1))
                                else:
                                    g.wait_ge(ac.h, stageA_ac[grp - 1] + 4 * (e4 + 7))
                            src = wup[e].rearrange("(f p) c -> p f c", p=128)
                            wul.dma(g.dma_start(
                                out=wup_sb[:, (e % 2) * NF * 512:(e % 2 + 1) * NF * 512],
                                in_=src))

                    @block.vector
                    def _(v, grp=grp):
                        ini.wait(v)
                        vx.inc(v.tensor_single_scalar(oh_bf[:, :], oh_i32[:, :],
                                                      0, ALU.is_equal))
                        cmp_vx.append(vx.n)
                    @block.tensor
                    def _(tb, grp=grp):
                        vx.wait(tb)
                        last = None
                        for e4 in range(EGRP):
                            for g2 in range(2):
                                last = tb.matmul(
                                    psS[g2][:, :],
                                    oh_bf[:, e4 * 128:(e4 + 1) * 128],
                                    coeff_bf[0:64, g2 * 512:(g2 + 1) * 512],
                                    start=True, stop=True)
                            mm.inc(last)
                            vx.wait(tb, vx.n + e4 + 1)

                    @block.vector
                    def _(v, grp=grp):
                        base = mm.n - EGRP
                        for e4 in range(EGRP):
                            v.wait_ge(mm.h, base + e4 + 1)
                            v.tensor_copy(cbc[:, e4 * T: e4 * T + 512], psS[0][:, :])
                            vx.inc(v.tensor_copy(cbc[:, e4 * T + 512:(e4 + 1) * T],
                                                 psS[1][:, :]))

                    for e4 in range(EGRP):
                        e = grp * EGRP + e4

                        @block.tensor
                        def _(tb, e=e, e4=e4):
                            wul.wait(tb, (e + 1) * 16)
                            k = 0
                            for j in range(2):
                                for g2 in range(2):
                                    pg = psA[0 + (k % 2)]
                                    pv = psA[2 + (k % 2)]
                                    ac.wait(tb, ac.n + k - 1)
                                    vx.wait(tb, vx.n + k - 1)
                                    last = None
                                    for f in range(NF):
                                        wslice = wup_sb[:, (e % 2) * NF * 512 + f * 512:
                                                        (e % 2) * NF * 512 + f * 512 + 512]
                                        tb.matmul(pg[:, :], wslice[:, j * 128:(j + 1) * 128],
                                                  x_fm[:, f * T + g2 * 512: f * T + (g2 + 1) * 512],
                                                  start=(f == 0), stop=(f == NF - 1))
                                        last = tb.matmul(pv[:, :],
                                                         wslice[:, 256 + j * 128: 256 + (j + 1) * 128],
                                                         x_fm[:, f * T + g2 * 512: f * T + (g2 + 1) * 512],
                                                         start=(f == 0), stop=(f == NF - 1))
                                    mm.inc(last)
                                    k += 1

                        @block.scalar
                        def _(s, e4=e4):
                            base = mm.n - 4
                            base_v = vx.n
                            for k in range(4):
                                s.wait_ge(mm.h, base + k + 1)
                                s.wait_ge(vx.h, base_v + k - 1)
                                ac.inc(s.activation(sgb2[:, (k % 2) * 512:(k % 2 + 1) * 512],
                                                    psA[0 + (k % 2)][:, :], AF.Silu))

                        @block.vector
                        def _(v, e4=e4):
                            base_a = ac.n - 4
                            k = 0
                            for j in range(2):
                                for g2 in range(2):
                                    v.wait_ge(ac.h, base_a + k + 1)
                                    v.tensor_mul(tvb[:, (k % 2) * 512:(k % 2 + 1) * 512],
                                                 sgb2[:, (k % 2) * 512:(k % 2 + 1) * 512],
                                                 cbc[:, e4 * T + g2 * 512: e4 * T + (g2 + 1) * 512])
                                    vx.inc(v.scalar_tensor_tensor(
                                        h_grp[:, (2 * e4 + j) * T + g2 * 512:
                                              (2 * e4 + j) * T + (g2 + 1) * 512],
                                        tvb[:, (k % 2) * 512:(k % 2 + 1) * 512],
                                        ones_f32[:, 0:1],
                                        psA[2 + (k % 2)][:, :],
                                        op0=ALU.mult, op1=ALU.mult))
                                    k += 1

                    @block.gpsimd
                    def _(g, grp=grp):
                        stageB_vx.append(vx.n)
                        for f in range(NF):
                            if grp * NF + f >= 2:
                                if f >= 2:
                                    g.wait_ge(vx.h, vx.n + 2 * (f - 1))
                                else:
                                    g.wait_ge(vx.h, stageB_vx[grp - 1] + 30 + 2 * f)
                            src = wdn[grp * EGRP:(grp + 1) * EGRP,
                                      :, f * 128:(f + 1) * 128]
                            src = src.rearrange("e (j p) c -> p e j c", p=128)
                            wdl.dma(g.dma_start(
                                out=wdnf[:, (f % 2) * 2048:(f % 2 + 1) * 2048],
                                in_=src))

                    @block.tensor
                    def _(tb, grp=grp):
                        vx.wait(tb)
                        for f in range(NF):
                            wdl.wait(tb, (grp * NF + f + 1) * 16)
                            for g2 in range(2):
                                i = 2 * f + g2
                                if i >= 2:
                                    vx.wait(tb, vx.n + i - 1)
                                last = None
                                kk = 0
                                for e4 in range(EGRP):
                                    for j in range(2):
                                        last = tb.matmul(
                                            psS[i % 2][:, :],
                                            wdnf[:, (f % 2) * 2048 + (2 * e4 + j) * 128:
                                                 (f % 2) * 2048 + (2 * e4 + j + 1) * 128],
                                            h_grp[:, (2 * e4 + j) * T + g2 * 512:
                                                  (2 * e4 + j) * T + (g2 + 1) * 512],
                                            start=(kk == 0), stop=(kk == 2 * EGRP - 1))
                                        kk += 1
                                mm.inc(last)

                    @block.vector
                    def _(v, grp=grp):
                        base = mm.n - 2 * NF
                        for f in range(NF):
                            for g2 in range(2):
                                i = 2 * f + g2
                                v.wait_ge(mm.h, base + i + 1)
                                vx.inc(v.tensor_add(
                                    z_bf[:, f * T + g2 * 512: f * T + (g2 + 1) * 512],
                                    z_bf[:, f * T + g2 * 512: f * T + (g2 + 1) * 512],
                                    psS[i % 2][:, :]))

            emit_layernorm()

        if _N_PHASES >= 3:
            SLOPES = [2.0 ** (-(h + 1) / 2.0) for h in range(16)]
            CEXP = 20.0
            INVSQ = 1.0 / float(np.sqrt(128.0))

            p3 = ExitStack()
            with p3:
                E3 = p3.enter_context
                bias4 = E3(nc.sbuf_tensor("bias4", [128, 4 * 16], F32))
                kcol = E3(nc.sbuf_tensor("kcol", [128, 16 * 16], F32))
                kval_f = E3(nc.sbuf_tensor("kval_f", [128, 16], F32))
                ij_f = E3(nc.sbuf_tensor("ij_f", [128, 512], F32))
                masks8 = E3(nc.sbuf_tensor("masks8", [128, 8 * 512], BF16))
                thr8 = E3(nc.sbuf_tensor("thr8", [128, 8], F32))
                thr_row = E3(nc.sbuf_tensor("thr_row", [1, 8], F32))
                thr_row2 = E3(nc.sbuf_tensor("thr_row2", [1, 16], F32))
                ci_row = E3(nc.sbuf_tensor("ci_row", [1, 8], F32))
                par_bc = E3(nc.sbuf_tensor("par_bc", [128, 2], F32))
                pen_col = E3(nc.sbuf_tensor("pen_col", [128, 4], F32))
                inv_col = E3(nc.sbuf_tensor("inv_col", [128, 2], F32))
                jb = E3(nc.sbuf_tensor("jb", [128, 512], F32))
                jbx = E3(nc.sbuf_tensor("jbx", [128, 512], F32))
                ctx_loc = E3(nc.sbuf_tensor("ctx_loc", [128, NF * T], BF16))

                p3_gate_mm = mm.n
                p3_gate_vx = vx.n
                p3_gate_ac = ac.n

                with nc.sbuf_tensor("w_st1", [128, 2 * NF * 128], BF16) as w_st, \
                     nc.sbuf_tensor("kv_stage", [128, NF * T], BF16) as kv_stage, \
                     nc.sbuf_tensor("vfm", [128, 2 * T], BF16) as vfm, \
                     nc.sbuf_tensor("biasr", [16, 4 * 128], F32) as biasr:

                    @block.gpsimd
                    def _(g):
                        g.wait_ge(mm.h, p3_gate_mm)
                        g.wait_ge(vx.h, p3_gate_vx)
                        g.wait_ge(ac.h, p3_gate_ac)
                        for bi, bb in enumerate((bq1, bk1, bv1, bo1)):
                            ld.dma(g.dma_start(out=biasr[:, bi * 128:(bi + 1) * 128],
                                               in_=bb[:, :]))

                    @block.tensor
                    def _(tb):
                        ld.wait(tb)
                        vx.wait(tb)
                        ac.wait(tb)
                        last = None
                        for bi in range(4):
                            last = tb.transpose(psS[1][:, bi * 16:(bi + 1) * 16],
                                                biasr[:, bi * 128:(bi + 1) * 128],
                                                ident_f32[0:16, 0:16])
                        mm.inc(last)

                    @block.vector
                    def _(v):
                        mm.wait(v)
                        vx.inc(v.tensor_copy(bias4[:, :], psS[1][:, 0:64]))

                    if _DEBUG_TAP == "attn":
                        @block.sync
                        def _(s):
                            vx.wait(s)
                            io.dma(s.dma_start(out=dbg[:, 14336:14400], in_=bias4[:, :]))
                            io.wait(s)

                    def emit_proj3(wsrc, bcol_i, kind, counter=[0]):
                        for m in range(NF):
                            idx = counter[0]
                            counter[0] += 1
                            slot = idx % 2

                            @block.gpsimd
                            def _(g, m=m, idx=idx, slot=slot, wsrc=wsrc):
                                if idx >= 2:
                                    g.wait_ge(ac.h, ac.n - 2)
                                srcw = wsrc[:, m * 128:(m + 1) * 128]
                                srcw = srcw.rearrange("(f p) c -> p f c", p=128)
                                wul.dma(g.dma_start(
                                    out=w_st[:, slot * NF * 128:(slot + 1) * NF * 128],
                                    in_=srcw))

                            @block.tensor
                            def _(tb, m=m, idx=idx, slot=slot):
                                wul.wait(tb)
                                if idx == 0:
                                    vx.wait(tb)
                                    ac.wait(tb)
                                last = None
                                for g2 in range(2):
                                    if 2 * idx + g2 >= 2:
                                        ac.wait(tb, ac.n + g2 - 1)
                                    for f in range(NF):
                                        last = tb.matmul(
                                            psA[(2 * idx + g2) % 2][:, :],
                                            w_st[:, slot * NF * 128 + f * 128:
                                                 slot * NF * 128 + (f + 1) * 128],
                                            x_fm[:, f * T + g2 * 512: f * T + (g2 + 1) * 512],
                                            start=(f == 0), stop=(f == NF - 1))
                                    mm.inc(last)

                            @block.scalar
                            def _(s, m=m, idx=idx, slot=slot, bcol_i=bcol_i, kind=kind):
                                base = mm.n - 2
                                if idx == 0:
                                    s.wait_ge(vx.h, vx.n)
                                for g2 in range(2):
                                    s.wait_ge(mm.h, base + g2 + 1)
                                    if kind == "k":
                                        dstap = kv_stage[:, m * T + g2 * 512:
                                                         m * T + (g2 + 1) * 512]
                                    else:
                                        dstap = vfm[:, slot * T + g2 * 512:
                                                    slot * T + (g2 + 1) * 512]
                                    ac.inc(s.add(dstap,
                                                 psA[(2 * idx + g2) % 2][:, :],
                                                 bias4[:, bcol_i * 16 + m:
                                                       bcol_i * 16 + m + 1]))

                            if kind == "v" and _DEBUG_TAP == "vfm" and m == _DEBUG_H:
                                @block.gpsimd
                                def _(g, slot=slot):
                                    g.wait_ge(ac.h, ac.n)
                                    io.dma(g.dma_start(out=dbg[:, 0:1024],
                                                       in_=vfm[:, slot * T:(slot + 1) * T]))
                                    io.wait(g)

                            if kind == "v":
                                @block.tensor
                                def _(tb, m=m, idx=idx, slot=slot):
                                    ac.wait(tb)
                                    if m >= 2:
                                        vx.wait(tb, vx.n - 1)
                                    last = None
                                    for t in range(NT):
                                        last = tb.transpose(
                                            psB[slot][:, t * 128:(t + 1) * 128],
                                            vfm[:, slot * T + t * 128:
                                                slot * T + (t + 1) * 128],
                                            ident_bf[:])
                                    mm.inc(last)

                                @block.vector
                                def _(v, m=m, slot=slot):
                                    mm.wait(v)
                                    if m == 0:
                                        v.wait_ge(io.h, io.n)
                                    last = None
                                    for t in range(NT):
                                        last = v.tensor_copy(
                                            kv_stage[:, t * 2048 + m * 128:
                                                     t * 2048 + (m + 1) * 128],
                                            psB[slot][:, t * 128:(t + 1) * 128])
                                    vx.inc(last)

                                if _DEBUG_TAP == "vfm" and m == _DEBUG_H:
                                    @block.gpsimd
                                    def _(g, m=m):
                                        g.wait_ge(vx.h, vx.n)
                                        src = kv_stage[:, :].rearrange(
                                            "p (t c) -> p t c", t=NT)
                                        io.dma(g.dma_start(
                                            out=dbg[:, 1024:2048].rearrange(
                                                "p (t c) -> p t c", t=NT),
                                            in_=src[:, :, m * 128:(m + 1) * 128]))
                                        io.wait(g)

                    emit_proj3(wk1, 1, "k")

                    @block.gpsimd
                    def _(g):
                        g.wait_ge(ac.h, ac.n)
                        par = g.partition_id() % 2
                        dstk = sh_k.ap().rearrange("(m p) (h t) -> h p m t", h=2, p=128)
                        io.dma(g.dma_start(out=dstk[ds(par, 1), :, :, :],
                                           in_=kv_stage[:, :]))

                    emit_proj3(wv1, 2, "v")

                    if _DEBUG_TAP == "vstage":
                        @block.gpsimd
                        def _(g):
                            g.wait_ge(vx.h, vx.n)
                            io.dma(g.dma_start(out=dbg[:, :], in_=kv_stage[:, :]))
                            io.wait(g)

                    @block.gpsimd
                    def _(g):
                        g.wait_ge(vx.h, vx.n)
                        par = g.partition_id() % 2
                        dstv = sh_v.ap().rearrange("(tt tp p) c -> tt p tp c", tt=2, p=128)
                        io.dma(g.dma_start(out=dstv[ds(par, 1), :, :, :],
                                           in_=kv_stage[:, :]))

                    @block.gpsimd
                    def _(g):
                        io.wait(g)
                        cc.inc(g.collective_compute(
                            "AllGather", ALU.bypass,
                            replica_groups=[list(range(NCORES))],
                            ins=[bar_in.ap()], outs=[bar_out.ap()]))

                    with nc.sbuf_tensor("kval_i", [128, 16], I32) as kval_i, \
                         nc.sbuf_tensor("ij_i", [128, 512], I32) as ij_i, \
                         nc.sbuf_tensor("ijp_i", [128, 512], I32) as ijp_i, \
                         nc.sbuf_tensor("ijp_f", [128, 512], F32) as ijp_f:

                        @block.gpsimd
                        def _(g):
                            cc.wait(g)
                            ini.inc(g.iota(kval_i[:, :], pattern=[[128, 16]], base=0,
                                           channel_multiplier=1))
                            ini.inc(g.iota(ij_i[:, :], pattern=[[1, 512]], base=0,
                                           channel_multiplier=0))
                            ini.inc(g.iota(ijp_i[:, :], pattern=[[1, 512]], base=0,
                                           channel_multiplier=-1))
                            ini.dma(g.dma_start(out=ci_row[0:1, :],
                                                in_=cinfo[0:1, :]))

                        @block.vector
                        def _(v):
                            ini.wait(v)
                            v.tensor_copy(kval_f[:, :], kval_i[:, :])
                            v.tensor_copy(ij_f[:, :], ij_i[:, :])
                            v.tensor_copy(ijp_f[:, :], ijp_i[:, :])
                            v.memset(inv_col[:, :], INVSQ)
                            for i, mi in enumerate([0, 1, 2, 3, 8, 9, 10, 11]):
                                v.memset(thr_row[0:1, i:i + 1], float(mi * 128))
                            for h in range(16):
                                v.tensor_scalar_mul(kcol[:, h * 16:(h + 1) * 16],
                                                    kval_f[:, :], SLOPES[h])
                            vx.inc(v.engine_nop())

                        @block.vector
                        def _(v):
                            vx.wait(v)
                            v.tensor_single_scalar(thr_row2[0:1, 0:8], thr_row[0:1, :],
                                                   ci_row[0:1, 2:3], ALU.subtract)
                            vx.inc(v.engine_nop())

                        @block.tensor
                        def _(tb):
                            vx.wait(tb)
                            tb.matmul(psS[0][:, 0:8], ones_f32[0:1, :],
                                      thr_row2[0:1, 0:8], start=True, stop=True)
                            mm.inc(tb.matmul(psS[0][:, 8:16], ones_f32[0:1, :],
                                             ci_row[0:1, 0:8], start=True, stop=True))

                        @block.vector
                        def _(v):
                            mm.wait(v)
                            v.tensor_copy(thr8[:, :], psS[0][:, 0:8])
                            v.tensor_copy(par_bc[:, 0:2], psS[0][:, 8:10])
                            v.tensor_copy(pen_col[:, 0:2], psS[0][:, 12:14])
                            for i in range(8):
                                v.tensor_single_scalar(masks8[:, i * 512:(i + 1) * 512],
                                                       ijp_f[:, :], thr8[:, i:i + 1],
                                                       ALU.is_ge)
                            vx.inc(v.engine_nop())

                BOUND = set([0, 1, 2, 3, 8, 9, 10, 11])
                MIDX = {0: 0, 1: 1, 2: 2, 3: 3, 8: 4, 9: 5, 10: 6, 11: 7}
                cc_bar = cc.n
                qev_ac = []

                with nc.sbuf_tensor("w_st2", [128, 2 * NF * 128], BF16) as w_st2, \
                     nc.sbuf_tensor("q_h", [128, 2 * T], BF16) as q_h, \
                     nc.sbuf_tensor("k_h", [128, 2 * 2048], BF16) as k_h, \
                     nc.sbuf_tensor("v_h", [128, 2 * 2048], BF16) as v_h, \
                     nc.sbuf_tensor("pt_sb", [128, 16 * 512], BF16) as pt_sb, \
                     nc.sbuf_tensor("argb", [128, 2 * 512], F32) as argb, \
                     nc.sbuf_tensor("sum_row", [1, 512], F32) as sum_row, \
                     nc.sbuf_tensor("rb_sb", [128, 512], F32) as rb_sb:

                    for h in range(16):
                        hs2 = h % 2

                        @block.gpsimd
                        def _(g, h=h, hs2=hs2):
                            if h >= 2:
                                g.wait_ge(ac.h, qev_ac[h - 2])
                            srcw = wq1[:, h * 128:(h + 1) * 128]
                            srcw = srcw.rearrange("(f p) c -> p f c", p=128)
                            wul.dma(g.dma_start(
                                out=w_st2[:, hs2 * NF * 128:(hs2 + 1) * NF * 128],
                                in_=srcw))

                        @block.sync
                        def _(s, h=h, hs2=hs2):
                            if h >= 2:
                                s.wait_ge(dnc.h, dnc.n - 1)
                            else:
                                s.wait_ge(cc.h, cc_bar)
                            wdl.dma(s.dma_start(out=k_h[:, hs2 * 2048:(hs2 + 1) * 2048],
                                                in_=sh_k.ap()[h * 128:(h + 1) * 128, :]))
                            srcv = sh_v.ap()[:, h * 128:(h + 1) * 128]
                            srcv = srcv.rearrange("(kt p) c -> p kt c", p=128)
                            wdl.dma(s.dma_start(out=v_h[:, hs2 * 2048:(hs2 + 1) * 2048],
                                                in_=srcv))

                        @block.tensor
                        def _(tb, h=h, hs2=hs2):
                            wul.wait(tb)
                            vx.wait(tb)
                            ac.wait(tb)
                            last = None
                            for g2 in range(2):
                                for f in range(NF):
                                    last = tb.matmul(
                                        psA[g2][:, :],
                                        w_st2[:, hs2 * NF * 128 + f * 128:
                                              hs2 * NF * 128 + (f + 1) * 128],
                                        x_fm[:, f * T + g2 * 512: f * T + (g2 + 1) * 512],
                                        start=(f == 0), stop=(f == NF - 1))
                                mm.inc(last)

                        @block.scalar
                        def _(s, h=h, hs2=hs2):
                            base = mm.n - 2
                            for g2 in range(2):
                                s.wait_ge(mm.h, base + g2 + 1)
                                if h >= 2:
                                    s.wait_ge(dnc.h, dnc.n - 1)
                                ac.inc(s.add(
                                    q_h[:, hs2 * T + g2 * 512: hs2 * T + (g2 + 1) * 512],
                                    psA[g2][:, :],
                                    bias4[:, 0 * 16 + h: 0 * 16 + h + 1]))
                            qev_ac.append(ac.n)

                        if _DEBUG_TAP == "attn" and h == _DEBUG_H:
                            @block.gpsimd
                            def _(g, hs2=hs2):
                                g.wait_ge(ac.h, ac.n)
                                g.wait_ge(wdl.h, wdl.n)
                                io.dma(g.dma_start(out=dbg[:, 11264:11264 + 1024],
                                                   in_=q_h[:, hs2 * T:hs2 * T + 1024]))
                                kvsrc = k_h if _DEBUG_KV == "k" else v_h
                                io.dma(g.dma_start(out=dbg[:, 12288:12288 + 2048],
                                                   in_=kvsrc[:, hs2 * 2048:hs2 * 2048 + 2048]))
                                io.wait(g)

                        for qg in range(2):
                            q0 = qg * 512
                            nkc = 12 if qg == 0 else 16

                            @block.vector
                            def _(v, h=h, qg=qg, q0=q0):
                                v.tensor_scalar(jb[:, :], ij_f[:, :], SLOPES[h],
                                                SLOPES[h] * q0 + CEXP,
                                                op0=ALU.mult, op1=ALU.add)
                                v.scalar_tensor_tensor(
                                    jb[:, :], par_bc[:, 0:1].to_broadcast([128, 512]),
                                    SLOPES[h] * 1024.0, jb[:, :],
                                    op0=ALU.mult, op1=ALU.add)
                                v.tensor_scalar_add(jbx[:, :], jb[:, :], pen_col[:, 0:1])
                                vx.inc(v.engine_nop())

                            stt_cnt = {}
                            exp_cnt = {}

                            @block.tensor
                            def _(tb, h=h, hs2=hs2, qg=qg, q0=q0, nkc=nkc,
                                  stt_cnt=stt_cnt):
                                wdl.wait(tb, 16 * 2 * (h + 1))
                                ac.wait(tb)
                                vx.wait(tb)
                                base_stt = vx.n
                                for kc in range(nkc):
                                    if kc >= 2:
                                        tb.wait_ge(vx.h, base_stt + kc - 1)
                                    mm.inc(tb.matmul(
                                        psA[kc % 2][:, :],
                                        k_h[:, hs2 * 2048 + kc * 128:
                                            hs2 * 2048 + (kc + 1) * 128],
                                        q_h[:, hs2 * T + q0: hs2 * T + q0 + 512],
                                        start=True, stop=True))

                            @block.vector
                            def _(v, h=h, qg=qg, nkc=nkc, stt_cnt=stt_cnt):
                                base = mm.n - nkc
                                base_ac = ac.n
                                for kc in range(nkc):
                                    mi = kc - 4 * qg
                                    v.wait_ge(mm.h, base + kc + 1)
                                    v.wait_ge(ac.h, base_ac + max(0, kc - 1))
                                    jsel = jbx if mi >= 4 else jb
                                    v.scalar_tensor_tensor(
                                        argb[:, (kc % 2) * 512:(kc % 2 + 1) * 512],
                                        psA[kc % 2][:, :], inv_col[:, 0:1], jsel[:, :],
                                        op0=ALU.mult, op1=ALU.subtract)
                                    if mi in BOUND:
                                        v.tensor_scalar(
                                            argb[:, (kc % 2) * 512:(kc % 2 + 1) * 512],
                                            argb[:, (kc % 2) * 512:(kc % 2 + 1) * 512],
                                            kcol[:, h * 16 + kc: h * 16 + kc + 1], 0.0,
                                            op0=ALU.add, op1=ALU.min)
                                    vx.inc(v.engine_nop())
                                    stt_cnt[kc] = vx.n

                            @block.scalar
                            def _(s, h=h, qg=qg, nkc=nkc, stt_cnt=stt_cnt,
                                  exp_cnt=exp_cnt):
                                for kc in range(nkc):
                                    mi = kc - 4 * qg
                                    s.wait_ge(vx.h, stt_cnt[kc])
                                    if mi in BOUND:
                                        ins = s.activation(
                                            pt_sb[:, kc * 512:(kc + 1) * 512],
                                            argb[:, (kc % 2) * 512:(kc % 2 + 1) * 512],
                                            AF.Exp)
                                    else:
                                        ins = s.activation(
                                            pt_sb[:, kc * 512:(kc + 1) * 512],
                                            argb[:, (kc % 2) * 512:(kc % 2 + 1) * 512],
                                            AF.Exp,
                                            bias=kcol[:, h * 16 + kc: h * 16 + kc + 1])
                                    ac.inc(ins)
                                    exp_cnt[kc] = ac.n

                            @block.vector
                            def _(v, qg=qg, nkc=nkc, exp_cnt=exp_cnt):
                                for kc in range(nkc):
                                    mi = kc - 4 * qg
                                    if mi in BOUND:
                                        v.wait_ge(ac.h, exp_cnt[kc])
                                        v.tensor_mul(
                                            pt_sb[:, kc * 512:(kc + 1) * 512],
                                            pt_sb[:, kc * 512:(kc + 1) * 512],
                                            masks8[:, MIDX[mi] * 512:(MIDX[mi] + 1) * 512])
                                vx.inc(v.engine_nop())

                            if _DEBUG_TAP == "attn" and h == _DEBUG_H and qg == 1:
                                @block.gpsimd
                                def _(g, nkc=nkc):
                                    ac.wait(g)
                                    g.wait_ge(vx.h, vx.n)
                                    io.dma(g.dma_start(out=dbg[:, 0:8192], in_=pt_sb[:, 0:8192]))
                                    io.dma(g.dma_start(out=dbg[:, 8192:8704], in_=jb[:, :]))
                                    io.dma(g.dma_start(out=dbg[:, 8704:9216], in_=jbx[:, :]))
                                    io.dma(g.dma_start(out=dbg[:, 9216:10240], in_=argb[:, :]))
                                    io.dma(g.dma_start(out=dbg[:, 10240:10248], in_=thr8[:, :]))
                                    io.dma(g.dma_start(out=dbg[:, 10248:10250], in_=par_bc[:, 0:2]))
                                    io.dma(g.dma_start(out=dbg[:, 10250:10252], in_=pen_col[:, 0:2]))
                                    io.dma(g.dma_start(out=dbg[:, 10252:10508], in_=kcol[:, 0:256]))
                                    io.wait(g)

                            @block.tensor
                            def _(tb, h=h, hs2=hs2, qg=qg, nkc=nkc):
                                vx.wait(tb)
                                ac.wait(tb)
                                for kc in range(nkc):
                                    tb.matmul(psS[0][0:1, :], ones_bf[:, 0:1],
                                              pt_sb[:, kc * 512:(kc + 1) * 512],
                                              start=(kc == 0), stop=(kc == nkc - 1))
                                last = None
                                for kc in range(nkc):
                                    last = tb.matmul(
                                        psA[2][:, :],
                                        v_h[:, hs2 * 2048 + kc * 128:
                                            hs2 * 2048 + (kc + 1) * 128],
                                        pt_sb[:, kc * 512:(kc + 1) * 512],
                                        start=(kc == 0), stop=(kc == nkc - 1))
                                mm.inc(last)

                            @block.vector
                            def _(v):
                                mm.wait(v)
                                vx.inc(v.tensor_copy(sum_row[0:1, :], psS[0][0:1, :]))
                                v.reciprocal(sum_row[0:1, :], sum_row[0:1, :])
                                vx.inc(v.engine_nop())

                            @block.tensor
                            def _(tb):
                                vx.wait(tb)
                                mm.inc(tb.matmul(psS[1][:, :], ones_f32[0:1, :],
                                                 sum_row[0:1, :], start=True, stop=True))

                            @block.vector
                            def _(v, h=h, qg=qg):
                                mm.wait(v)
                                v.tensor_copy(rb_sb[:, :], psS[1][:, :])
                                vx.inc(v.tensor_mul(
                                    ctx_loc[:, h * T + qg * 512: h * T + (qg + 1) * 512],
                                    psA[2][:, :], rb_sb[:, :]))

                            if _DEBUG_TAP == "attn" and h == _DEBUG_H and qg == 1:
                                @block.gpsimd
                                def _(g):
                                    g.wait_ge(vx.h, vx.n)
                                    io.dma(g.dma_start(out=dbg[:, 14400:14912],
                                                       in_=rb_sb[:, :]))
                                    io.wait(g)

                        @block.vector
                        def _(v):
                            dnc.inc(v.engine_nop())

                if _DEBUG_TAP == "ctx":
                    @block.gpsimd
                    def _(g):
                        g.wait_ge(vx.h, vx.n)
                        io.dma(g.dma_start(out=dbg[:, 0:NF * T], in_=ctx_loc[:, :]))
                        io.wait(g)

                with nc.sbuf_tensor("w_st3", [128, 2 * NF * 128], BF16) as w_st3, \
                     nc.sbuf_tensor("oproj_st", [128, 2 * T], BF16) as oproj_st:

                    o_gate_mm = mm.n
                    o_gate_vx = vx.n
                    for m in range(NF):
                        slot = m % 2

                        @block.gpsimd
                        def _(g, m=m, slot=slot):
                            if m < 2:
                                g.wait_ge(mm.h, o_gate_mm)
                                g.wait_ge(vx.h, o_gate_vx)
                            else:
                                g.wait_ge(ac.h, ac.n - 2)
                            srcw = wo1[:, m * 128:(m + 1) * 128]
                            srcw = srcw.rearrange("(f p) c -> p f c", p=128)
                            wul.dma(g.dma_start(
                                out=w_st3[:, slot * NF * 128:(slot + 1) * NF * 128],
                                in_=srcw))

                        @block.tensor
                        def _(tb, m=m, slot=slot):
                            wul.wait(tb)
                            if m == 0:
                                ac.wait(tb)
                                vx.wait(tb)
                            last = None
                            for g2 in range(2):
                                if 2 * m + g2 >= 2:
                                    ac.wait(tb, ac.n + g2 - 1)
                                for f in range(NF):
                                    last = tb.matmul(
                                        psA[(2 * m + g2) % 2][:, :],
                                        w_st3[:, slot * NF * 128 + f * 128:
                                              slot * NF * 128 + (f + 1) * 128],
                                        ctx_loc[:, f * T + g2 * 512: f * T + (g2 + 1) * 512],
                                        start=(f == 0), stop=(f == NF - 1))
                                mm.inc(last)

                        @block.scalar
                        def _(s, m=m, slot=slot):
                            base = mm.n - 2
                            base_v = vx.n
                            for g2 in range(2):
                                s.wait_ge(mm.h, base + g2 + 1)
                                if m >= 2:
                                    s.wait_ge(vx.h, base_v + g2 - 3)
                                ac.inc(s.add(
                                    oproj_st[:, slot * T + g2 * 512:
                                             slot * T + (g2 + 1) * 512],
                                    psA[(2 * m + g2) % 2][:, :],
                                    bias4[:, 3 * 16 + m: 3 * 16 + m + 1]))

                        @block.vector
                        def _(v, m=m, slot=slot):
                            base = ac.n - 2
                            for g2 in range(2):
                                v.wait_ge(ac.h, base + g2 + 1)
                                vx.inc(v.tensor_add(
                                    z_bf[:, m * T + g2 * 512: m * T + (g2 + 1) * 512],
                                    x_fm[:, m * T + g2 * 512: m * T + (g2 + 1) * 512],
                                    oproj_st[:, slot * T + g2 * 512:
                                             slot * T + (g2 + 1) * 512]))

            emit_layernorm()

        if _N_PHASES >= 4:
            NH = 44
            p4 = ExitStack()
            with p4:
                E4 = p4.enter_context
                h_act = E4(nc.sbuf_tensor("h_act", [128, NH * T], BF16))
                scl_sb = E4(nc.sbuf_tensor("scl_sb", [128, 144], F32))
                m128 = E4(nc.sbuf_tensor("m128", [1, 128], F32))
                sr_row = E4(nc.sbuf_tensor("sr_row", [1, T], F32))

                p4_mm = mm.n
                p4_vx = vx.n
                p4_ac = ac.n

                with nc.sbuf_tensor("sgr", [44, 256], F32) as sgr, \
                     nc.sbuf_tensor("sdr", [16, 128], F32) as sdr, \
                     nc.sbuf_tensor("xneg", [128, T], F32) as xneg:

                    @block.gpsimd
                    def _(g):
                        g.wait_ge(mm.h, p4_mm)
                        g.wait_ge(vx.h, p4_vx)
                        g.wait_ge(ac.h, p4_ac)
                        ld.dma(g.dma_start(out=sgr[:, 0:128], in_=sg_in[:, :]))
                        ld.dma(g.dma_start(out=sgr[:, 128:256], in_=su_in[:, :]))
                        ld.dma(g.dma_start(out=sdr[:, :], in_=sd_in[:, :]))

                    @block.vector
                    def _(v):
                        v.wait_ge(mm.h, p4_mm)
                        vx.inc(v.memset(m128[0:1, :], -128.0))

                    @block.tensor
                    def _(tb):
                        ld.wait(tb)
                        vx.wait(tb)
                        ac.wait(tb)
                        tb.transpose(psS[0][:, 0:44], sgr[:, 0:128],
                                     ident_f32[0:44, 0:44])
                        tb.transpose(psS[0][:, 64:108], sgr[:, 128:256],
                                     ident_f32[0:44, 0:44])
                        tb.transpose(psS[0][:, 128:144], sdr[:, :],
                                     ident_f32[0:16, 0:16])
                        last = None
                        for f in range(NF):
                            tb.matmul(psS[1][0:1, :], ones_bf[:, 0:1],
                                      x_fm[:, f * T: f * T + 512],
                                      start=(f == 0), stop=(f == NF - 1))
                            last = tb.matmul(psA[0][0:1, :], ones_bf[:, 0:1],
                                             x_fm[:, f * T + 512:(f + 1) * T],
                                             start=(f == 0), stop=(f == NF - 1))
                        mm.inc(last)

                    @block.vector
                    def _(v):
                        mm.wait(v)
                        v.tensor_copy(scl_sb[:, :], psS[0][:, 0:144])
                        v.tensor_copy(sr_row[0:1, 0:512], psS[1][0:1, :])
                        vx.inc(v.tensor_copy(sr_row[0:1, 512:1024],
                                             psA[0][0:1, :]))

                    @block.tensor
                    def _(tb):
                        vx.wait(tb)
                        tb.matmul(psS[0][:, :], m128[0:1, :], sr_row[0:1, 0:512],
                                  start=True, stop=True)
                        mm.inc(tb.matmul(psS[1][:, :], m128[0:1, :],
                                         sr_row[0:1, 512:1024],
                                         start=True, stop=True))

                    @block.vector
                    def _(v):
                        mm.wait(v)
                        v.tensor_copy(xneg[:, 0:512], psS[0][:, :])
                        vx.inc(v.tensor_copy(xneg[:, 512:1024], psS[1][:, :]))

                    with nc.sbuf_tensor("wT4", [128, 2 * D], BF16) as wT4, \
                         nc.sbuf_tensor("sgb4", [128, T], BF16) as sgb4, \
                         nc.sbuf_tensor("t1g", [128, T], F32) as t1g:

                        ldg_wul = []
                        gm_mm = {}
                        um_mm = {}
                        qm_mm = []
                        evg_vx = {}
                        evu_vx = {}
                        sil_ac = {}

                        for hc in range(NH):
                            for mi, wsrc in ((0, wqgT), (1, wquT)):
                                seq = 2 * hc + mi
                                slot = seq % 2

                                @block.gpsimd
                                def _(g, hc=hc, seq=seq, slot=slot, wsrc=wsrc):
                                    if seq == 0:
                                        g.wait_ge(mm.h, mm.n)
                                        g.wait_ge(vx.h, vx.n)
                                        g.wait_ge(ac.h, ac.n)
                                    if seq >= 2:
                                        g.wait_ge(mm.h, qm_mm[seq - 2])
                                    srcw = wsrc[:, hc * 128:(hc + 1) * 128]
                                    srcw = srcw.rearrange("(f p) c -> p f c", p=128)
                                    wul.dma(g.dma_start(
                                        out=wT4[:, slot * D:(slot + 1) * D],
                                        in_=srcw))
                                    ldg_wul.append(wul.n)

                                @block.tensor
                                def _(tb, hc=hc, mi=mi, seq=seq, slot=slot):
                                    tb.wait_ge(wul.h, ldg_wul[seq])
                                    if seq == 0:
                                        vx.wait(tb)
                                        ac.wait(tb)
                                    pbase = 2 * mi
                                    for g2 in range(2):
                                        if hc > 0:
                                            ev = evg_vx if mi == 0 else evu_vx
                                            tb.wait_ge(vx.h, ev[(hc - 1, g2)])
                                        last = None
                                        for f in range(NF):
                                            last = tb.matmul(
                                                psA[pbase + g2][:, :],
                                                wT4[:, slot * D + f * 128:
                                                    slot * D + (f + 1) * 128],
                                                x_fm[:, f * T + g2 * 512:
                                                     f * T + (g2 + 1) * 512],
                                                start=(f == 0), stop=(f == NF - 1))
                                        mm.inc(last)
                                        if mi == 0:
                                            gm_mm[(hc, g2)] = mm.n
                                        else:
                                            um_mm[(hc, g2)] = mm.n
                                    qm_mm.append(mm.n)

                            @block.vector
                            def _(v, hc=hc):
                                for g2 in range(2):
                                    v.wait_ge(mm.h, gm_mm[(hc, g2)])
                                    vx.inc(v.tensor_add(
                                        t1g[:, g2 * 512:(g2 + 1) * 512],
                                        psA[g2][:, :],
                                        xneg[:, g2 * 512:(g2 + 1) * 512]))
                                    evg_vx[(hc, g2)] = vx.n

                            @block.scalar
                            def _(s, hc=hc):
                                for g2 in range(2):
                                    s.wait_ge(vx.h, evg_vx[(hc, g2)])
                                    ac.inc(s.activation(
                                        sgb4[:, g2 * 512:(g2 + 1) * 512],
                                        t1g[:, g2 * 512:(g2 + 1) * 512],
                                        AF.Silu, scale=scl_sb[:, hc:hc + 1]))
                                    sil_ac[(hc, g2)] = ac.n

                            @block.vector
                            def _(v, hc=hc):
                                for g2 in range(2):
                                    v.wait_ge(ac.h, sil_ac[(hc, g2)])
                                    v.wait_ge(mm.h, um_mm[(hc, g2)])
                                    vx.inc(v.tensor_add(
                                        t1g[:, g2 * 512:(g2 + 1) * 512],
                                        psA[2 + g2][:, :],
                                        xneg[:, g2 * 512:(g2 + 1) * 512]))
                                    evu_vx[(hc, g2)] = vx.n
                                    vx.inc(v.scalar_tensor_tensor(
                                        h_act[:, hc * T + g2 * 512:
                                              hc * T + (g2 + 1) * 512],
                                        t1g[:, g2 * 512:(g2 + 1) * 512],
                                        scl_sb[:, 64 + hc: 64 + hc + 1],
                                        sgb4[:, g2 * 512:(g2 + 1) * 512],
                                        op0=ALU.mult, op1=ALU.mult))

                with nc.sbuf_tensor("hneg", [128, T], F32) as hneg, \
                     nc.sbuf_tensor("wdT", [128, 5632], BF16) as wdT, \
                     nc.sbuf_tensor("t1d", [128, 512], F32) as t1d:

                    @block.tensor
                    def _(tb):
                        vx.wait(tb)
                        last = None
                        for j in range(NH):
                            tb.matmul(psS[0][0:1, :], ones_bf[:, 0:1],
                                      h_act[:, j * T: j * T + 512],
                                      start=(j == 0), stop=(j == NH - 1))
                            last = tb.matmul(psS[1][0:1, :], ones_bf[:, 0:1],
                                             h_act[:, j * T + 512:(j + 1) * T],
                                             start=(j == 0), stop=(j == NH - 1))
                        mm.inc(last)

                    @block.vector
                    def _(v):
                        mm.wait(v)
                        v.tensor_copy(sr_row[0:1, 0:512], psS[0][0:1, :])
                        vx.inc(v.tensor_copy(sr_row[0:1, 512:1024],
                                             psS[1][0:1, :]))

                    @block.tensor
                    def _(tb):
                        vx.wait(tb)
                        tb.matmul(psS[0][:, :], m128[0:1, :], sr_row[0:1, 0:512],
                                  start=True, stop=True)
                        mm.inc(tb.matmul(psS[1][:, :], m128[0:1, :],
                                         sr_row[0:1, 512:1024],
                                         start=True, stop=True))

                    @block.vector
                    def _(v):
                        mm.wait(v)
                        v.tensor_copy(hneg[:, 0:512], psS[0][:, :])
                        vx.inc(v.tensor_copy(hneg[:, 512:1024], psS[1][:, :]))

                    ldd_wul = []
                    dnm_mm = {}
                    evd_vx = {}

                    for dc in range(NF):
                        @block.gpsimd
                        def _(g, dc=dc):
                            if dc == 0:
                                g.wait_ge(vx.h, vx.n)
                                g.wait_ge(mm.h, mm.n)
                            else:
                                g.wait_ge(mm.h, dnm_mm[(dc - 1, 1)])
                            srcw = wqdT[:, dc * 128:(dc + 1) * 128]
                            srcw = srcw.rearrange("(j p) c -> p j c", p=128)
                            wul.dma(g.dma_start(out=wdT[:, :], in_=srcw))
                            ldd_wul.append(wul.n)

                        @block.tensor
                        def _(tb, dc=dc):
                            tb.wait_ge(wul.h, ldd_wul[dc])
                            for g2 in range(2):
                                if dc > 0:
                                    tb.wait_ge(vx.h, evd_vx[(dc - 1, g2)])
                                else:
                                    vx.wait(tb)
                                last = None
                                for j in range(NH):
                                    last = tb.matmul(
                                        psA[g2][:, :],
                                        wdT[:, j * 128:(j + 1) * 128],
                                        h_act[:, j * T + g2 * 512:
                                              j * T + (g2 + 1) * 512],
                                        start=(j == 0), stop=(j == NH - 1))
                                mm.inc(last)
                                dnm_mm[(dc, g2)] = mm.n

                        @block.vector
                        def _(v, dc=dc):
                            for g2 in range(2):
                                v.wait_ge(mm.h, dnm_mm[(dc, g2)])
                                v.tensor_add(t1d[:, :], psA[g2][:, :],
                                             hneg[:, g2 * 512:(g2 + 1) * 512])
                                vx.inc(v.scalar_tensor_tensor(
                                    z_bf[:, dc * T + g2 * 512:
                                         dc * T + (g2 + 1) * 512],
                                    t1d[:, :],
                                    scl_sb[:, 128 + dc: 128 + dc + 1],
                                    x_fm[:, dc * T + g2 * 512:
                                         dc * T + (g2 + 1) * 512],
                                    op0=ALU.mult, op1=ALU.add))
                                evd_vx[(dc, g2)] = vx.n

            emit_layernorm()

        with nc.sbuf_tensor("out_sb", [128, 2 * D], F32) as out_sb:
            @block.tensor
            def _(tb):
                vx.wait(tb)
                for t in range(NT):
                    for half in range(2):
                        i = 2 * t + half
                        if i >= 2:
                            vx.wait(tb, vx.n + i - 1)
                        pb = psB[half]
                        last = None
                        for f4 in range(4):
                            f = half * 8 + 2 * f4
                            last = tb.transpose(pb[:, f4 * 256:f4 * 256 + 128],
                                                x_fm[:, f * T + t * 128: f * T + (t + 1) * 128],
                                                ident_bf[:])
                            last = tb.transpose(pb[:, f4 * 256 + 128:(f4 + 1) * 256],
                                                x_fm[:, (f + 1) * T + t * 128:(f + 1) * T + (t + 1) * 128],
                                                ident_bf[:])
                        mm.inc(last)

            @block.vector
            def _(v):
                base = mm.n - 2 * NT
                io_base = io.n
                for t in range(NT):
                    if t >= 2:
                        v.wait_ge(io.h, io_base + (t - 1) * 16)
                    for half in range(2):
                        i = 2 * t + half
                        v.wait_ge(mm.h, base + i + 1)
                        vx.inc(v.tensor_copy(
                            out_sb[:, (t % 2) * D + half * 1024:(t % 2) * D + (half + 1) * 1024],
                            psB[half][:, :]))

            @block.sync
            def _(s):
                base = vx.n - 2 * NT
                for t in range(NT):
                    s.wait_ge(vx.h, base + 2 * (t + 1))
                    io.dma(s.dma_start(out=out[t * 128:(t + 1) * 128, :],
                                       in_=out_sb[:, (t % 2) * D:(t % 2 + 1) * D]))
                io.wait(s)

    return nc


def kernel(**inputs):
    u_full = np.ascontiguousarray(inputs["L0_attn_u"].reshape(8192, D))
    afac = np.ascontiguousarray(inputs["L0_attn_A_factors"].reshape(1, 32))
    dvec = np.ascontiguousarray(inputs["L0_attn_D_vec"].reshape(NF, 128))
    wqgT_h = np.ascontiguousarray(inputs["L1_ffn_wq_gate"].T)
    wquT_h = np.ascontiguousarray(inputs["L1_ffn_wq_up"].T)
    wqdT_h = np.ascontiguousarray(inputs["L1_ffn_wq_down"].T)

    nc = build_program()
    in_maps = []
    for c in range(NCORES):
        m = {
            "u": u_full[c * T:(c + 1) * T],
            "wbr": inputs["L0_attn_W_b_rank"],
            "wbs": inputs["L0_attn_W_b_state"],
            "wcr": inputs["L0_attn_W_c_rank"],
            "wcm": inputs["L0_attn_W_c_model"],
            "afac": afac,
            "dvec": dvec,
        }
        m.update({
            "wg": inputs["L0_ffn_w_gate"],
            "wsu": inputs["L0_ffn_w_shared_up"],
            "wsd": inputs["L0_ffn_w_shared_down"],
            "wup": inputs["L0_ffn_w_up"],
            "wdn": inputs["L0_ffn_w_down"],
            "wq1": inputs["L1_attn_Wq"],
            "wk1": inputs["L1_attn_Wk"],
            "wv1": inputs["L1_attn_Wv"],
            "wo1": inputs["L1_attn_Wo"],
            "bq1": np.ascontiguousarray(inputs["L1_attn_bq"].reshape(16, 128)),
            "bk1": np.ascontiguousarray(inputs["L1_attn_bk"].reshape(16, 128)),
            "bv1": np.ascontiguousarray(inputs["L1_attn_bv"].reshape(16, 128)),
            "bo1": np.ascontiguousarray(inputs["L1_attn_bo"].reshape(16, 128)),
        })
        m.update({
            "wqgT": wqgT_h,
            "wquT": wquT_h,
            "wqdT": wqdT_h,
            "sg_in": np.ascontiguousarray(
                inputs["L1_ffn_s_gate"].reshape(44, 128)),
            "su_in": np.ascontiguousarray(
                inputs["L1_ffn_s_up"].reshape(44, 128)),
            "sd_in": np.ascontiguousarray(
                inputs["L1_ffn_s_down"].reshape(16, 128)),
        })
        par = c % 2
        pen = (1 - par) * 1.0e4
        m["cinfo"] = np.array([[par, par, 1024.0 * par, 1024.0 * par,
                                pen, pen, float(c), float(c)]], dtype=np.float32)
        in_maps.append(m)
    res = run_bass_kernel_spmd(nc, in_maps, core_ids=list(range(NCORES)))
    global _LAST_RES
    _LAST_RES = res
    out = np.concatenate([res.results[c]["out"] for c in range(NCORES)], axis=0)
    return out.reshape(4, 2048, 2048)


# version 05
# speedup vs baseline: 1.1102x; 1.0451x over previous
import numpy as np
from contextlib import ExitStack

import concourse.bass as bass
import concourse.mybir as mybir
from concourse.bass import ds
from concourse.expressions import RuntimeValue as RV
from concourse.bass_utils import run_bass_kernel_spmd

F32 = mybir.dt.float32
BF16 = mybir.dt.bfloat16
I32 = mybir.dt.int32
AF = mybir.ActivationFunctionType
ALU = mybir.AluOpType

NCORES = 8
T = 1024
D = 2048
NF = 16
NT = 8
EPS = 1e-5
NEXP = 64
EGRP = 8
CAP = 1024

_N_PHASES = 99
_DEBUG_TAP = None
_DEBUG_H = 0
_DEBUG_KV = "k"
_LAST_RES = None


class Sem:

    def __init__(self, ctx, nc, name):
        self.h = ctx.enter_context(nc.semaphore(name))
        self.n = 0

    def inc(self, inst, k=1):
        inst.then_inc(self.h, k)
        self.n += k
        return inst

    def dma(self, inst):
        return self.inc(inst, 16)

    def wait(self, eng, n=None):
        eng.wait_ge(self.h, self.n if n is None else n)


def build_program():
    nc = bass.Bass("TRN2", target_bir_lowering=False, num_devices=NCORES)

    u = nc.dram_tensor("u", [T, D], F32, kind="ExternalInput").ap()
    wbr = nc.dram_tensor("wbr", [D, 64], F32, kind="ExternalInput").ap()
    wbs = nc.dram_tensor("wbs", [64, 256], F32, kind="ExternalInput").ap()
    wcr = nc.dram_tensor("wcr", [256, 64], F32, kind="ExternalInput").ap()
    wcm = nc.dram_tensor("wcm", [64, D], F32, kind="ExternalInput").ap()
    afac = nc.dram_tensor("afac", [1, 32], F32, kind="ExternalInput").ap()
    dvec = nc.dram_tensor("dvec", [NF, 128], F32, kind="ExternalInput").ap()
    wg = nc.dram_tensor("wg", [D, 64], F32, kind="ExternalInput").ap()
    wsu = nc.dram_tensor("wsu", [D, 1024], F32, kind="ExternalInput").ap()
    wsd = nc.dram_tensor("wsd", [512, D], F32, kind="ExternalInput").ap()
    wup = nc.dram_tensor("wup", [NEXP, D, 512], F32, kind="ExternalInput").ap()
    wdn = nc.dram_tensor("wdn", [NEXP, 256, D], F32, kind="ExternalInput").ap()

    wq1 = nc.dram_tensor("wq1", [D, D], F32, kind="ExternalInput").ap()
    wk1 = nc.dram_tensor("wk1", [D, D], F32, kind="ExternalInput").ap()
    wv1 = nc.dram_tensor("wv1", [D, D], F32, kind="ExternalInput").ap()
    wo1 = nc.dram_tensor("wo1", [D, D], F32, kind="ExternalInput").ap()
    bq1 = nc.dram_tensor("bq1", [16, 128], F32, kind="ExternalInput").ap()
    bk1 = nc.dram_tensor("bk1", [16, 128], F32, kind="ExternalInput").ap()
    bv1 = nc.dram_tensor("bv1", [16, 128], F32, kind="ExternalInput").ap()
    bo1 = nc.dram_tensor("bo1", [16, 128], F32, kind="ExternalInput").ap()

    cnt_in = nc.dram_tensor("cnt_in", [1, 64], F32)
    cnt_out = nc.dram_tensor("cnt_out", [NCORES, 64], F32)
    offs_dram = nc.dram_tensor("offs_dram", [NCORES, 64], F32)
    bar_in = nc.dram_tensor("bar_in", [1, 2], F32)
    bar_out = nc.dram_tensor("bar_out", [NCORES, 2], F32)
    cinfo = nc.dram_tensor("cinfo", [1, 8], F32, kind="ExternalInput").ap()
    sh_q = nc.dram_tensor("sh_q", [D, D], BF16, addr_space="Shared")
    sh_k = nc.dram_tensor("sh_k", [D, D], BF16, addr_space="Shared")
    sh_v = nc.dram_tensor("sh_v", [D, D], BF16, addr_space="Shared")
    sh_ctx = nc.dram_tensor("sh_ctx", [D, D], BF16, addr_space="Shared")

    wqgT = nc.dram_tensor("wqgT", [D, 5632], I32, kind="ExternalInput").ap()
    wquT = nc.dram_tensor("wquT", [D, 5632], I32, kind="ExternalInput").ap()
    wqdT = nc.dram_tensor("wqdT", [5632, D], I32, kind="ExternalInput").ap()
    sg_in = nc.dram_tensor("sg_in", [44, 128], F32, kind="ExternalInput").ap()
    su_in = nc.dram_tensor("su_in", [44, 128], F32, kind="ExternalInput").ap()
    sd_in = nc.dram_tensor("sd_in", [16, 128], F32, kind="ExternalInput").ap()

    out = nc.dram_tensor("out", [T, D], F32, kind="ExternalOutput").ap()
    dbg = (nc.dram_tensor("dbg", [128, NF * T], F32, kind="ExternalOutput").ap()
           if _DEBUG_TAP else None)

    ctx = ExitStack()
    with ctx:
        E = ctx.enter_context

        x_fm = E(nc.sbuf_tensor("x_fm", [128, NF * T], BF16))
        z_bf = E(nc.sbuf_tensor("z_bf", [128, NF * T], BF16))
        ones_bf = E(nc.sbuf_tensor("ones_bf", [128, 128], BF16))
        ones_f32 = E(nc.sbuf_tensor("ones_f32", [128, 128], F32))
        ident_bf = E(nc.sbuf_tensor("ident_bf", [128, 128], BF16))
        ident_f32 = E(nc.sbuf_tensor("ident_f32", [128, 128], F32))
        iota_jp = E(nc.sbuf_tensor("iota_jp", [128, 128], I32))
        ln_rs = E(nc.sbuf_tensor("ln_rs", [128, T], F32))
        ln_mr = E(nc.sbuf_tensor("ln_mr", [128, T], F32))
        ln_rs_bf = E(nc.sbuf_tensor("ln_rs_bf", [128, T], BF16))
        ln_mr_bf = E(nc.sbuf_tensor("ln_mr_bf", [128, T], BF16))
        stat = E(nc.sbuf_tensor("stat", [128, 1024], F32))
        eps_col = E(nc.sbuf_tensor("eps_col", [128, 1], F32))

        psA = [E(nc.psum_tensor(f"psA{i}", [128, 512], F32)) for i in range(4)]
        psB = [E(nc.psum_tensor(f"psB{i}", [128, 1024], BF16)) for i in range(2)]
        psS = [E(nc.psum_tensor(f"psS{i}", [128, 512], F32)) for i in range(2)]

        ld = Sem(ctx, nc, "s_ld")
        mm = Sem(ctx, nc, "s_mm")
        vx = Sem(ctx, nc, "s_vx")
        ac = Sem(ctx, nc, "s_ac")
        io = Sem(ctx, nc, "s_io")
        ini = Sem(ctx, nc, "s_ini")
        wul = Sem(ctx, nc, "s_wul")
        wdl = Sem(ctx, nc, "s_wdl")
        upc = Sem(ctx, nc, "s_upc")
        dnc = Sem(ctx, nc, "s_dnc")
        cc = Sem(ctx, nc, "s_cc")

        block = E(nc.Block())

        r_pid = nc.gpsimd.alloc_register("r_pid")
        r_tmp = nc.gpsimd.alloc_register("r_tmp")

        ln_seq = [0]

        def emit_layernorm():
            ln_seq[0] += 1
            with nc.sbuf_tensor(f"zsq_ln{ln_seq[0]}", [128, NF * T], BF16) as zsq:
                @block.scalar
                def _(s):
                    vx.wait(s)
                    last = None
                    for f in range(NF):
                        last = s.activation(zsq[:, f * T:(f + 1) * T],
                                            z_bf[:, f * T:(f + 1) * T], AF.Square)
                    ac.inc(last)

                @block.tensor
                def _(tb):
                    vx.wait(tb)
                    ac.wait(tb)
                    last = None
                    for g in range(2):
                        for f in range(NF):
                            last = tb.matmul(
                                psS[g][:, :], ones_bf[:],
                                z_bf[:, f * T + g * 512: f * T + (g + 1) * 512],
                                start=(f == 0), stop=(f == NF - 1))
                    for g in range(2):
                        for f in range(NF):
                            last = tb.matmul(
                                psA[2 + g][:, :], ones_bf[:],
                                zsq[:, f * T + g * 512: f * T + (g + 1) * 512],
                                start=(f == 0), stop=(f == NF - 1))
                    mm.inc(last)

                @block.vector
                def _(v):
                    mm.wait(v)
                    for g in range(2):
                        sl = slice(g * 512, (g + 1) * 512)
                        v.tensor_scalar_mul(ln_mr[:, sl], psS[g][:, :], 1.0 / D)
                        v.tensor_scalar_mul(stat[:, sl], psA[2 + g][:, :], 1.0 / D)
                        v.tensor_mul(ln_rs[:, sl], ln_mr[:, sl], ln_mr[:, sl])
                        v.tensor_sub(stat[:, sl], stat[:, sl], ln_rs[:, sl])
                    vx.inc(v.engine_nop())

                @block.scalar
                def _(s):
                    vx.wait(s)
                    ac.inc(s.activation(ln_rs[:, :], stat[:, :], AF.Sqrt,
                                        bias=eps_col[:, 0:1]))

                @block.vector
                def _(v):
                    ac.wait(v)
                    v.reciprocal(ln_rs[:, :], ln_rs[:, :])
                    v.tensor_mul(ln_mr[:, :], ln_mr[:, :], ln_rs[:, :])
                    v.tensor_copy(ln_rs_bf[:, :], ln_rs[:, :])
                    v.tensor_copy(ln_mr_bf[:, :], ln_mr[:, :])
                    for f in range(NF):
                        v.tensor_mul(x_fm[:, f * T:(f + 1) * T],
                                     z_bf[:, f * T:(f + 1) * T], ln_rs_bf[:, :])
                        v.tensor_sub(x_fm[:, f * T:(f + 1) * T],
                                     x_fm[:, f * T:(f + 1) * T], ln_mr_bf[:, :])
                    vx.inc(v.engine_nop())

        with nc.sbuf_tensor("x_tm", [128, NT * D], BF16) as x_tm:
            @block.gpsimd
            def _(g):
                pid = g.compute_val(g.partition_id())
                g.reg_mov(r_pid, pid)
                for t in range(NT):
                    ld.dma(g.dma_start(out=x_tm[:, t * D:(t + 1) * D],
                                       in_=u[t * 128:(t + 1) * 128, :]))
                ini.inc(g.iota(iota_jp[:], pattern=[[1, 128]], base=0,
                               channel_multiplier=-1))

            @block.vector
            def _(v):
                v.memset(ones_bf[:], 1.0)
                v.memset(ones_f32[:], 1.0)
                v.memset(eps_col[:], float(EPS))
                ini.wait(v)
                v.tensor_single_scalar(ident_bf[:], iota_jp[:], 0, ALU.is_equal)
                vx.inc(v.tensor_single_scalar(ident_f32[:], iota_jp[:], 0, ALU.is_equal))

            @block.tensor
            def _(tb):
                ld.wait(tb)
                vx.wait(tb)
                for f in range(NF):
                    if f >= 2:
                        vx.wait(tb, vx.n + f - 1)
                    pb = psB[f % 2]
                    last = None
                    for t in range(NT):
                        last = tb.transpose(pb[:, t * 128:(t + 1) * 128],
                                            x_tm[:, t * D + f * 128: t * D + (f + 1) * 128],
                                            ident_bf[:])
                    mm.inc(last)

            @block.vector
            def _(v):
                base = mm.n - NF
                for f in range(NF):
                    v.wait_ge(mm.h, base + f + 1)
                    vx.inc(v.tensor_copy(x_fm[:, f * T:(f + 1) * T], psB[f % 2][:, :]))

        if _N_PHASES >= 1:
            p1 = ExitStack()
            with p1:
                E1 = p1.enter_context
                wbr_sb = E1(nc.sbuf_tensor("wbr_sb", [128, NF * 64], BF16))
                wbs_sb = E1(nc.sbuf_tensor("wbs_sb", [64, 256], BF16))
                wcr_sb = E1(nc.sbuf_tensor("wcr_sb", [128, 2 * 64], BF16))
                wcm_sb = E1(nc.sbuf_tensor("wcm_sb", [64, D], BF16))
                afac_sb = E1(nc.sbuf_tensor("afac_sb", [1, 32], F32))
                abc = E1(nc.sbuf_tensor("abc", [128, 32], F32))
                dvecr = E1(nc.sbuf_tensor("dvecr", [16, 128], F32))
                dv_sb = E1(nc.sbuf_tensor("dv_sb", [128, 16], F32))
                t1_bf = E1(nc.sbuf_tensor("t1_bf", [64, T], BF16))
                st_a = E1(nc.sbuf_tensor("st_a", [128, NT * 256], F32))
                st_b = E1(nc.sbuf_tensor("st_b", [128, NT * 256], F32))
                stT_bf = E1(nc.sbuf_tensor("stT_bf", [128, 2 * T], BF16))
                y1_bf = E1(nc.sbuf_tensor("y1_bf", [64, T], BF16))

                gate_mm = mm.n

                @block.gpsimd
                def _(g):
                    g.wait_ge(mm.h, gate_mm)
                    for f in range(NF):
                        ld.dma(g.dma_start(out=wbr_sb[:, f * 64:(f + 1) * 64],
                                           in_=wbr[f * 128:(f + 1) * 128, :]))
                    ld.dma(g.dma_start(out=wbs_sb[:, :], in_=wbs[:, :]))
                    for j in range(2):
                        ld.dma(g.dma_start(out=wcr_sb[:, j * 64:(j + 1) * 64],
                                           in_=wcr[j * 128:(j + 1) * 128, :]))
                    ld.dma(g.dma_start(out=wcm_sb[:, :], in_=wcm[:, :]))
                    ld.dma(g.dma_start(out=afac_sb[:, :], in_=afac[:, :]))
                    ld.dma(g.dma_start(out=dvecr[:, :], in_=dvec[:, :]))

                @block.tensor
                def _(tb):
                    ld.wait(tb)
                    vx.wait(tb)
                    tb.matmul(psS[0][:, 0:32], ones_f32[0:1, :], afac_sb[:, :],
                              start=True, stop=True)
                    mm.inc(tb.transpose(psS[1][:, 0:16], dvecr[:, :],
                                        ident_f32[0:16, 0:16]))

                @block.vector
                def _(v):
                    mm.wait(v)
                    v.tensor_copy(abc[:, :], psS[0][:, 0:32])
                    vx.inc(v.tensor_scalar_add(dv_sb[:, :], psS[1][:, 0:16], 1.0))

                @block.tensor
                def _(tb):
                    vx.wait(tb)
                    for g in range(2):
                        if g == 1:
                            vx.wait(tb, vx.n + 1)
                        last = None
                        for f in range(NF):
                            last = tb.matmul(psA[0][0:64, :],
                                             wbr_sb[:, f * 64:(f + 1) * 64],
                                             x_fm[:, f * T + g * 512: f * T + (g + 1) * 512],
                                             start=(f == 0), stop=(f == NF - 1))
                        mm.inc(last)

                @block.vector
                def _(v):
                    base = mm.n - 2
                    for g in range(2):
                        v.wait_ge(mm.h, base + g + 1)
                        vx.inc(v.tensor_copy(t1_bf[0:64, g * 512:(g + 1) * 512],
                                             psA[0][0:64, :]))

                @block.tensor
                def _(tb):
                    vx.wait(tb)
                    for i in range(NT):
                        if i >= 2:
                            vx.wait(tb, vx.n + i - 1)
                        mm.inc(tb.matmul(psA[i % 2][:, 0:256],
                                         t1_bf[0:64, i * 128:(i + 1) * 128],
                                         wbs_sb[0:64, :], start=True, stop=True))

                @block.vector
                def _(v):
                    base = mm.n - NT
                    for i in range(NT):
                        v.wait_ge(mm.h, base + i + 1)
                        vx.inc(v.tensor_copy(st_a[:, i * 256:(i + 1) * 256],
                                             psA[i % 2][:, 0:256]))

                @block.vector
                def _(v):
                    vx.wait(v)
                    src, dst = st_a, st_b
                    for lvl in range(8):
                        s = 1 << (7 - lvl)
                        a00 = abc[:, lvl * 4 + 0: lvl * 4 + 1]
                        a01 = abc[:, lvl * 4 + 1: lvl * 4 + 2]
                        a10 = abc[:, lvl * 4 + 2: lvl * 4 + 3]
                        a11 = abc[:, lvl * 4 + 3: lvl * 4 + 4]

                        def view(buf, bit, s=s):
                            ap = buf[:, :].rearrange("p (t g i) -> p t g i",
                                                     t=NT, g=256 // (2 * s), i=2 * s)
                            return ap[:, :, :, bit * s:(bit + 1) * s]

                        x0s, x1s = view(src, 0), view(src, 1)
                        y0s, y1s = view(dst, 0), view(dst, 1)
                        v.tensor_scalar_mul(y0s, x1s, a10)
                        v.scalar_tensor_tensor(y0s, x0s, a00, y0s, op0=ALU.mult, op1=ALU.add)
                        v.tensor_scalar_mul(y1s, x1s, a11)
                        v.scalar_tensor_tensor(y1s, x0s, a01, y1s, op0=ALU.mult, op1=ALU.add)
                        src, dst = dst, src
                    vx.inc(v.engine_nop())

                @block.tensor
                def _(tb):
                    vx.wait(tb)
                    k = 0
                    for j in range(2):
                        for q in range(2):
                            if k >= 2:
                                vx.wait(tb, vx.n + k - 1)
                            last = None
                            for t4 in range(4):
                                i = q * 4 + t4
                                last = tb.transpose(
                                    psA[q][:, t4 * 128:(t4 + 1) * 128],
                                    st_a[:, i * 256 + j * 128: i * 256 + (j + 1) * 128],
                                    ident_f32[:])
                            mm.inc(last)
                            k += 1

                @block.vector
                def _(v):
                    base = mm.n - 4
                    k = 0
                    for j in range(2):
                        for q in range(2):
                            k += 1
                            v.wait_ge(mm.h, base + k)
                            vx.inc(v.tensor_copy(
                                stT_bf[:, j * T + q * 512: j * T + (q + 1) * 512],
                                psA[q][:, :]))

                @block.tensor
                def _(tb):
                    vx.wait(tb)
                    for g in range(2):
                        if g == 1:
                            vx.wait(tb, vx.n + 1)
                        last = None
                        for j in range(2):
                            last = tb.matmul(psA[2][0:64, :],
                                             wcr_sb[:, j * 64:(j + 1) * 64],
                                             stT_bf[:, j * T + g * 512: j * T + (g + 1) * 512],
                                             start=(j == 0), stop=(j == 1))
                        mm.inc(last)

                @block.vector
                def _(v):
                    base = mm.n - 2
                    for g in range(2):
                        v.wait_ge(mm.h, base + g + 1)
                        vx.inc(v.tensor_copy(y1_bf[0:64, g * 512:(g + 1) * 512],
                                             psA[2][0:64, :]))

                @block.tensor
                def _(tb):
                    vx.wait(tb)
                    for f in range(NF):
                        for g in range(2):
                            i = 2 * f + g
                            if i >= 2:
                                vx.wait(tb, vx.n + i - 1)
                            mm.inc(tb.matmul(psA[i % 2][:, :],
                                             wcm_sb[0:64, f * 128:(f + 1) * 128],
                                             y1_bf[0:64, g * 512:(g + 1) * 512],
                                             start=True, stop=True))

                @block.vector
                def _(v):
                    base = mm.n - 2 * NF
                    for f in range(NF):
                        for g in range(2):
                            i = 2 * f + g
                            v.wait_ge(mm.h, base + i + 1)
                            vx.inc(v.scalar_tensor_tensor(
                                z_bf[:, f * T + g * 512: f * T + (g + 1) * 512],
                                x_fm[:, f * T + g * 512: f * T + (g + 1) * 512],
                                dv_sb[:, f:f + 1],
                                psA[i % 2][:, :],
                                op0=ALU.mult, op1=ALU.add))

            emit_layernorm()

        if _N_PHASES >= 2:
            coeff_bf = E(nc.sbuf_tensor("coeff_bf", [64, T], BF16))
            p2_gate_mm = mm.n
            p2_gate_vx = vx.n
            p2_gate_ac = ac.n

            p2a = ExitStack()
            with p2a:
                E2 = p2a.enter_context
                wg_sb = E2(nc.sbuf_tensor("wg_sb", [128, NF * 64], BF16))
                wsu_sb = E2(nc.sbuf_tensor("wsu_sb", [128, NF * 1024], BF16))
                wsd_sb = E2(nc.sbuf_tensor("wsd_sb", [128, 4 * D], BF16))
                h_s = E2(nc.sbuf_tensor("h_s", [128, 4 * T], BF16))
                sgb = E2(nc.sbuf_tensor("sgb", [128, 2 * 512], BF16))
                sc_fm = E2(nc.sbuf_tensor("sc_fm", [64, T], F32))
                sc_tm = E2(nc.sbuf_tensor("sc_tm", [128, NT * 64], F32))
                g_tm = E2(nc.sbuf_tensor("g_tm", [128, NT * 64], F32))
                t8 = E2(nc.sbuf_tensor("t8", [128, 8], F32))
                ts_col = E2(nc.sbuf_tensor("ts_col", [128, NT], F32))
                coeff_fm = E2(nc.sbuf_tensor("coeff_fm", [64, T], F32))
                m_fm = E2(nc.sbuf_tensor("m_fm", [64, T], F32))
                ltm = E2(nc.sbuf_tensor("ltm", [8, 8], F32))
                stat2 = E2(nc.sbuf_tensor("stat2", [128, 1024], F32))
                cnts_sb = E2(nc.sbuf_tensor("cnts_sb", [8, 64], F32))
                offs_sb = E2(nc.sbuf_tensor("offs_sb", [8, 64], F32))
                offc = E2(nc.sbuf_tensor("offc", [64, 1], F32))

                @block.gpsimd
                def _(g):
                    g.wait_ge(mm.h, p2_gate_mm)
                    g.wait_ge(vx.h, p2_gate_vx)
                    for f in range(NF):
                        ld.dma(g.dma_start(out=wg_sb[:, f * 64:(f + 1) * 64],
                                           in_=wg[f * 128:(f + 1) * 128, :]))
                    for f in range(NF):
                        ld.dma(g.dma_start(out=wsu_sb[:, f * 1024:(f + 1) * 1024],
                                           in_=wsu[f * 128:(f + 1) * 128, :]))
                    for j in range(4):
                        ld.dma(g.dma_start(out=wsd_sb[:, j * D:(j + 1) * D],
                                           in_=wsd[j * 128:(j + 1) * 128, :]))

                @block.tensor
                def _(tb):
                    ld.wait(tb, ld.n - 20 * 16)
                    vx.wait(tb)
                    for g2 in range(2):
                        if g2 == 1:
                            ac.wait(tb, ac.n + 1)
                        last = None
                        for f in range(NF):
                            last = tb.matmul(psS[0][0:64, :],
                                             wg_sb[:, f * 64:(f + 1) * 64],
                                             x_fm[:, f * T + g2 * 512: f * T + (g2 + 1) * 512],
                                             start=(f == 0), stop=(f == NF - 1))
                        mm.inc(last)

                @block.scalar
                def _(s):
                    base = mm.n - 2
                    for g2 in range(2):
                        s.wait_ge(mm.h, base + g2 + 1)
                        ac.inc(s.activation(sc_fm[0:64, g2 * 512:(g2 + 1) * 512],
                                            psS[0][0:64, :], AF.Sigmoid))

                @block.tensor
                def _(tb):
                    ac.wait(tb)
                    last = None
                    for t in range(NT):
                        last = tb.transpose(psS[1][:, t * 64:(t + 1) * 64],
                                            sc_fm[0:64, t * 128:(t + 1) * 128],
                                            ident_f32[0:64, 0:64])
                    mm.inc(last)

                @block.vector
                def _(v):
                    mm.wait(v)
                    v.tensor_copy(sc_tm[:, :], psS[1][:, :])
                    for t in range(NT):
                        scs = sc_tm[:, t * 64:(t + 1) * 64]
                        gts = g_tm[:, t * 64:(t + 1) * 64]
                        v.max(out=t8[:], in_=scs)
                        v.memset(t8[:, 6:8], -1e30)
                        v.tensor_copy(gts, scs)
                        v.match_replace(out=gts, in_to_replace=t8[:, 0:8],
                                        in_values=gts, imm_value=0.0)
                        v.tensor_sub(gts, scs, gts)
                    vx.inc(v.engine_nop())

                @block.tensor
                def _(tb):
                    vx.wait(tb)
                    last = None
                    for t in range(NT):
                        last = tb.transpose(psA[t // 4][0:64, (t % 4) * 128:(t % 4 + 1) * 128],
                                            g_tm[:, t * 64:(t + 1) * 64],
                                            ident_f32[:])
                    mm.inc(last)

                @block.vector
                def _(v):
                    mm.wait(v)
                    for q in range(2):
                        v.tensor_copy(coeff_fm[0:64, q * 512:(q + 1) * 512],
                                      psA[q][0:64, :])
                    v.tensor_single_scalar(m_fm[0:64, :], coeff_fm[0:64, :], 0.0, ALU.is_gt)
                    v.tensor_tensor_scan(stat[0:64, :], m_fm[0:64, :], m_fm[0:64, :],
                                         0.0, ALU.add, ALU.bypass)
                    vx.inc(v.engine_nop())

                @block.tensor
                def _(tb):
                    vx.wait(tb)
                    last = None
                    for q in range(2):
                        last = tb.matmul(psA[2 + q][:, :], ones_f32[0:64, :],
                                         coeff_fm[0:64, q * 512:(q + 1) * 512],
                                         start=True, stop=True)
                    mm.inc(last)

                @block.vector
                def _(v):
                    mm.wait(v)
                    for q in range(2):
                        sl = slice(q * 512, (q + 1) * 512)
                        v.reciprocal(stat2[:, sl], psA[2 + q][:, :])
                    v.tensor_mul(coeff_fm[0:64, :], coeff_fm[0:64, :], stat2[0:64, :])
                    vx.inc(v.engine_nop())

                @block.sync
                def _(s):
                    vx.wait(s)
                    io.dma(s.dma_start(out=cnt_in.ap()[0:1, :],
                                       in_=stat[0:64, (T - 1):T]))

                @block.gpsimd
                def _(g):
                    io.wait(g)
                    cc.inc(g.collective_compute(
                        "AllGather", ALU.bypass,
                        replica_groups=[list(range(NCORES))],
                        ins=[cnt_in.ap()], outs=[cnt_out.ap()]))
                    cc.wait(g)
                    ld.dma(g.dma_start(out=cnts_sb[:, :], in_=cnt_out.ap()[:, :]))

                shared_ld = ld.n - 16

                @block.tensor
                def _(tb):
                    ld.wait(tb, shared_ld)
                    vx.wait(tb)
                    k = 0
                    for j in range(4):
                        for g2 in range(2):
                            py = psA[0 + (k % 2)]
                            pg = psA[2 + (k % 2)]
                            if k >= 2:
                                vx.wait(tb, vx.n + k - 1)
                            last = None
                            for f in range(NF):
                                tb.matmul(py[:, :],
                                          wsu_sb[:, f * 1024 + j * 128: f * 1024 + (j + 1) * 128],
                                          x_fm[:, f * T + g2 * 512: f * T + (g2 + 1) * 512],
                                          start=(f == 0), stop=(f == NF - 1))
                                last = tb.matmul(pg[:, :],
                                                 wsu_sb[:, f * 1024 + 512 + j * 128: f * 1024 + 512 + (j + 1) * 128],
                                                 x_fm[:, f * T + g2 * 512: f * T + (g2 + 1) * 512],
                                                 start=(f == 0), stop=(f == NF - 1))
                            mm.inc(last)
                            k += 1

                @block.scalar
                def _(s):
                    base = mm.n - 8
                    base_v = vx.n
                    for k in range(8):
                        s.wait_ge(mm.h, base + k + 1)
                        if k >= 2:
                            s.wait_ge(vx.h, base_v + k - 1)
                        ac.inc(s.activation(sgb[:, (k % 2) * 512:(k % 2 + 1) * 512],
                                            psA[2 + (k % 2)][:, :], AF.Silu))

                @block.vector
                def _(v):
                    base_a = ac.n - 8
                    k = 0
                    for j in range(4):
                        for g2 in range(2):
                            v.wait_ge(ac.h, base_a + k + 1)
                            vx.inc(v.tensor_mul(
                                h_s[:, j * T + g2 * 512: j * T + (g2 + 1) * 512],
                                psA[0 + (k % 2)][:, :],
                                sgb[:, (k % 2) * 512:(k % 2 + 1) * 512]))
                            k += 1

                @block.tensor
                def _(tb):
                    vx.wait(tb)
                    for f in range(NF):
                        for g2 in range(2):
                            i = 2 * f + g2
                            if i >= 2:
                                vx.wait(tb, vx.n + i - 1)
                            last = None
                            for j in range(4):
                                last = tb.matmul(
                                    psA[i % 2][:, :],
                                    wsd_sb[:, j * D + f * 128: j * D + (f + 1) * 128],
                                    h_s[:, j * T + g2 * 512: j * T + (g2 + 1) * 512],
                                    start=(j == 0), stop=(j == 3))
                            mm.inc(last)

                @block.vector
                def _(v):
                    base = mm.n - 2 * NF
                    for f in range(NF):
                        for g2 in range(2):
                            i = 2 * f + g2
                            v.wait_ge(mm.h, base + i + 1)
                            vx.inc(v.tensor_add(
                                z_bf[:, f * T + g2 * 512: f * T + (g2 + 1) * 512],
                                x_fm[:, f * T + g2 * 512: f * T + (g2 + 1) * 512],
                                psA[i % 2][:, :]))

                @block.vector
                def _(v):
                    ld.wait(v)
                    vx.inc(v.tensor_single_scalar(ltm[:, :], iota_jp[0:8, 0:8],
                                                  0, ALU.is_gt))

                @block.tensor
                def _(tb):
                    vx.wait(tb)
                    ac.wait(tb)
                    mm.inc(tb.matmul(psS[0][0:8, 0:64], ltm[:, :], cnts_sb[:, :],
                                     start=True, stop=True))

                @block.vector
                def _(v):
                    mm.wait(v)
                    vx.inc(v.tensor_copy(offs_sb[:, :], psS[0][0:8, 0:64]))

                @block.sync
                def _(s):
                    vx.wait(s)
                    io.dma(s.dma_start(out=offs_dram.ap()[:, :], in_=offs_sb[:, :]))

                @block.gpsimd
                def _(g):
                    io.wait(g)
                    ld.dma(g.dma_start(out=offc[:, :],
                                       in_=offs_dram.ap()[ds(RV(r_pid, 0, 7), 1), :]))

                @block.vector
                def _(v):
                    ld.wait(v)
                    v.tensor_scalar_add(stat[0:64, :], stat[0:64, :], offc[0:64, 0:1])
                    v.tensor_single_scalar(m_fm[0:64, :], stat[0:64, :], float(CAP),
                                           ALU.is_le)
                    v.tensor_mul(coeff_fm[0:64, :], coeff_fm[0:64, :], m_fm[0:64, :])
                    vx.inc(v.tensor_copy(coeff_bf[0:64, :], coeff_fm[0:64, :]))

                if _DEBUG_TAP == "coeff":
                    @block.sync
                    def _(s):
                        vx.wait(s)
                        io.dma(s.dma_start(out=dbg[0:64, 0:T], in_=coeff_fm[0:64, :]))
                        io.dma(s.dma_start(out=dbg[64:128, 0:T], in_=stat[0:64, :]))
                        io.dma(s.dma_start(out=dbg[0:8, T:T + 64], in_=cnts_sb[:, :]))
                        io.dma(s.dma_start(out=dbg[0:64, T + 64:T + 65], in_=offc[:, :], allow_slow_non_contiguous=True))
                        io.dma(s.dma_start(out=dbg[0:64, T + 65:T + 65 + 512],
                                           in_=sc_fm[0:64, 0:512]))
                        io.wait(s)

            p2b = ExitStack()
            with p2b:
                E2 = p2b.enter_context
                wup_sb = E2(nc.sbuf_tensor("wup_sb", [128, 2 * NF * 512], BF16))
                wdnf = E2(nc.sbuf_tensor("wdnf", [128, 2 * 16 * 128], BF16))
                h_grp = E2(nc.sbuf_tensor("h_grp", [128, 2 * EGRP * T], BF16))
                cbc = E2(nc.sbuf_tensor("cbc", [128, EGRP * T], BF16))
                sgb2 = E2(nc.sbuf_tensor("sgb2", [128, 2 * 512], BF16))
                tvb = E2(nc.sbuf_tensor("tvb", [128, 2 * 512], BF16))
                oh_i32 = E2(nc.sbuf_tensor("oh_i32", [64, EGRP * 128], I32))
                oh_bf = E2(nc.sbuf_tensor("oh_bf", [64, EGRP * 128], BF16))

                gate2_mm = mm.n
                gate2_vx = vx.n
                cmp_vx = []
                stageA_ac = []
                stageB_vx = []

                for grp in range(NEXP // EGRP):
                    @block.gpsimd
                    def _(g, grp=grp):
                        if grp == 0:
                            g.wait_ge(mm.h, gate2_mm)
                            g.wait_ge(vx.h, gate2_vx)
                        else:
                            g.wait_ge(vx.h, cmp_vx[grp - 1])
                        ini.inc(g.iota(oh_i32[:, :], pattern=[[-1, EGRP], [0, 128]],
                                       base=-(grp * EGRP), channel_multiplier=1))
                        stageA_ac.append(ac.n)
                        for e4 in range(EGRP):
                            e = grp * EGRP + e4
                            if e >= 2:
                                if e4 >= 2:
                                    g.wait_ge(ac.h, ac.n + 4 * (e4 - 1))
                                else:
                                    g.wait_ge(ac.h, stageA_ac[grp - 1] + 4 * (e4 + 7))
                            src = wup[e].rearrange("(f p) c -> p f c", p=128)
                            wul.dma(g.dma_start(
                                out=wup_sb[:, (e % 2) * NF * 512:(e % 2 + 1) * NF * 512],
                                in_=src))

                    @block.vector
                    def _(v, grp=grp):
                        ini.wait(v)
                        vx.inc(v.tensor_single_scalar(oh_bf[:, :], oh_i32[:, :],
                                                      0, ALU.is_equal))
                        cmp_vx.append(vx.n)
                    @block.tensor
                    def _(tb, grp=grp):
                        vx.wait(tb)
                        last = None
                        for e4 in range(EGRP):
                            for g2 in range(2):
                                last = tb.matmul(
                                    psS[g2][:, :],
                                    oh_bf[:, e4 * 128:(e4 + 1) * 128],
                                    coeff_bf[0:64, g2 * 512:(g2 + 1) * 512],
                                    start=True, stop=True)
                            mm.inc(last)
                            vx.wait(tb, vx.n + e4 + 1)

                    @block.vector
                    def _(v, grp=grp):
                        base = mm.n - EGRP
                        for e4 in range(EGRP):
                            v.wait_ge(mm.h, base + e4 + 1)
                            v.tensor_copy(cbc[:, e4 * T: e4 * T + 512], psS[0][:, :])
                            vx.inc(v.tensor_copy(cbc[:, e4 * T + 512:(e4 + 1) * T],
                                                 psS[1][:, :]))

                    for e4 in range(EGRP):
                        e = grp * EGRP + e4

                        @block.tensor
                        def _(tb, e=e, e4=e4):
                            wul.wait(tb, (e + 1) * 16)
                            k = 0
                            for j in range(2):
                                for g2 in range(2):
                                    pg = psA[0 + (k % 2)]
                                    pv = psA[2 + (k % 2)]
                                    ac.wait(tb, ac.n + k - 1)
                                    vx.wait(tb, vx.n + k - 1)
                                    last = None
                                    for f in range(NF):
                                        wslice = wup_sb[:, (e % 2) * NF * 512 + f * 512:
                                                        (e % 2) * NF * 512 + f * 512 + 512]
                                        tb.matmul(pg[:, :], wslice[:, j * 128:(j + 1) * 128],
                                                  x_fm[:, f * T + g2 * 512: f * T + (g2 + 1) * 512],
                                                  start=(f == 0), stop=(f == NF - 1))
                                        last = tb.matmul(pv[:, :],
                                                         wslice[:, 256 + j * 128: 256 + (j + 1) * 128],
                                                         x_fm[:, f * T + g2 * 512: f * T + (g2 + 1) * 512],
                                                         start=(f == 0), stop=(f == NF - 1))
                                    mm.inc(last)
                                    k += 1

                        @block.scalar
                        def _(s, e4=e4):
                            base = mm.n - 4
                            base_v = vx.n
                            for k in range(4):
                                s.wait_ge(mm.h, base + k + 1)
                                s.wait_ge(vx.h, base_v + k - 1)
                                ac.inc(s.activation(sgb2[:, (k % 2) * 512:(k % 2 + 1) * 512],
                                                    psA[0 + (k % 2)][:, :], AF.Silu))

                        @block.vector
                        def _(v, e4=e4):
                            base_a = ac.n - 4
                            k = 0
                            for j in range(2):
                                for g2 in range(2):
                                    v.wait_ge(ac.h, base_a + k + 1)
                                    v.tensor_mul(tvb[:, (k % 2) * 512:(k % 2 + 1) * 512],
                                                 sgb2[:, (k % 2) * 512:(k % 2 + 1) * 512],
                                                 cbc[:, e4 * T + g2 * 512: e4 * T + (g2 + 1) * 512])
                                    vx.inc(v.scalar_tensor_tensor(
                                        h_grp[:, (2 * e4 + j) * T + g2 * 512:
                                              (2 * e4 + j) * T + (g2 + 1) * 512],
                                        tvb[:, (k % 2) * 512:(k % 2 + 1) * 512],
                                        ones_f32[:, 0:1],
                                        psA[2 + (k % 2)][:, :],
                                        op0=ALU.mult, op1=ALU.mult))
                                    k += 1

                    @block.gpsimd
                    def _(g, grp=grp):
                        stageB_vx.append(vx.n)
                        for f in range(NF):
                            if grp * NF + f >= 2:
                                if f >= 2:
                                    g.wait_ge(vx.h, vx.n + 2 * (f - 1))
                                else:
                                    g.wait_ge(vx.h, stageB_vx[grp - 1] + 30 + 2 * f)
                            src = wdn[grp * EGRP:(grp + 1) * EGRP,
                                      :, f * 128:(f + 1) * 128]
                            src = src.rearrange("e (j p) c -> p e j c", p=128)
                            wdl.dma(g.dma_start(
                                out=wdnf[:, (f % 2) * 2048:(f % 2 + 1) * 2048],
                                in_=src))

                    @block.tensor
                    def _(tb, grp=grp):
                        vx.wait(tb)
                        for f in range(NF):
                            wdl.wait(tb, (grp * NF + f + 1) * 16)
                            for g2 in range(2):
                                i = 2 * f + g2
                                if i >= 2:
                                    vx.wait(tb, vx.n + i - 1)
                                last = None
                                kk = 0
                                for e4 in range(EGRP):
                                    for j in range(2):
                                        last = tb.matmul(
                                            psS[i % 2][:, :],
                                            wdnf[:, (f % 2) * 2048 + (2 * e4 + j) * 128:
                                                 (f % 2) * 2048 + (2 * e4 + j + 1) * 128],
                                            h_grp[:, (2 * e4 + j) * T + g2 * 512:
                                                  (2 * e4 + j) * T + (g2 + 1) * 512],
                                            start=(kk == 0), stop=(kk == 2 * EGRP - 1))
                                        kk += 1
                                mm.inc(last)

                    @block.vector
                    def _(v, grp=grp):
                        base = mm.n - 2 * NF
                        for f in range(NF):
                            for g2 in range(2):
                                i = 2 * f + g2
                                v.wait_ge(mm.h, base + i + 1)
                                vx.inc(v.tensor_add(
                                    z_bf[:, f * T + g2 * 512: f * T + (g2 + 1) * 512],
                                    z_bf[:, f * T + g2 * 512: f * T + (g2 + 1) * 512],
                                    psS[i % 2][:, :]))

            emit_layernorm()

        if _N_PHASES >= 3:
            SLOPES = [2.0 ** (-(h + 1) / 2.0) for h in range(16)]
            CEXP = 20.0
            INVSQ = 1.0 / float(np.sqrt(128.0))

            p3 = ExitStack()
            with p3:
                E3 = p3.enter_context
                bias4 = E3(nc.sbuf_tensor("bias4", [128, 4 * 16], F32))
                kcol = E3(nc.sbuf_tensor("kcol", [128, 16 * 16], F32))
                kval_f = E3(nc.sbuf_tensor("kval_f", [128, 16], F32))
                ij_f = E3(nc.sbuf_tensor("ij_f", [128, 512], F32))
                masks8 = E3(nc.sbuf_tensor("masks8", [128, 8 * 512], BF16))
                thr8 = E3(nc.sbuf_tensor("thr8", [128, 8], F32))
                thr_row = E3(nc.sbuf_tensor("thr_row", [1, 8], F32))
                thr_row2 = E3(nc.sbuf_tensor("thr_row2", [1, 16], F32))
                ci_row = E3(nc.sbuf_tensor("ci_row", [1, 8], F32))
                par_bc = E3(nc.sbuf_tensor("par_bc", [128, 2], F32))
                pen_col = E3(nc.sbuf_tensor("pen_col", [128, 4], F32))
                inv_col = E3(nc.sbuf_tensor("inv_col", [128, 2], F32))
                jb = E3(nc.sbuf_tensor("jb", [128, 512], F32))
                jbx = E3(nc.sbuf_tensor("jbx", [128, 512], F32))
                ctx_loc = E3(nc.sbuf_tensor("ctx_loc", [128, NF * T], BF16))

                p3_gate_mm = mm.n
                p3_gate_vx = vx.n
                p3_gate_ac = ac.n

                with nc.sbuf_tensor("w_st1", [128, 2 * NF * 128], BF16) as w_st, \
                     nc.sbuf_tensor("kv_stage", [128, NF * T], BF16) as kv_stage, \
                     nc.sbuf_tensor("vfm", [128, 2 * T], BF16) as vfm, \
                     nc.sbuf_tensor("biasr", [16, 4 * 128], F32) as biasr:

                    @block.gpsimd
                    def _(g):
                        g.wait_ge(mm.h, p3_gate_mm)
                        g.wait_ge(vx.h, p3_gate_vx)
                        g.wait_ge(ac.h, p3_gate_ac)
                        for bi, bb in enumerate((bq1, bk1, bv1, bo1)):
                            ld.dma(g.dma_start(out=biasr[:, bi * 128:(bi + 1) * 128],
                                               in_=bb[:, :]))

                    @block.tensor
                    def _(tb):
                        ld.wait(tb)
                        vx.wait(tb)
                        ac.wait(tb)
                        last = None
                        for bi in range(4):
                            last = tb.transpose(psS[1][:, bi * 16:(bi + 1) * 16],
                                                biasr[:, bi * 128:(bi + 1) * 128],
                                                ident_f32[0:16, 0:16])
                        mm.inc(last)

                    @block.vector
                    def _(v):
                        mm.wait(v)
                        vx.inc(v.tensor_copy(bias4[:, :], psS[1][:, 0:64]))

                    if _DEBUG_TAP == "attn":
                        @block.sync
                        def _(s):
                            vx.wait(s)
                            io.dma(s.dma_start(out=dbg[:, 14336:14400], in_=bias4[:, :]))
                            io.wait(s)

                    def emit_proj3(wsrc, bcol_i, kind, counter=[0]):
                        for m in range(NF):
                            idx = counter[0]
                            counter[0] += 1
                            slot = idx % 2

                            @block.gpsimd
                            def _(g, m=m, idx=idx, slot=slot, wsrc=wsrc):
                                if idx >= 2:
                                    g.wait_ge(ac.h, ac.n - 2)
                                srcw = wsrc[:, m * 128:(m + 1) * 128]
                                srcw = srcw.rearrange("(f p) c -> p f c", p=128)
                                wul.dma(g.dma_start(
                                    out=w_st[:, slot * NF * 128:(slot + 1) * NF * 128],
                                    in_=srcw))

                            @block.tensor
                            def _(tb, m=m, idx=idx, slot=slot):
                                wul.wait(tb)
                                if idx == 0:
                                    vx.wait(tb)
                                    ac.wait(tb)
                                last = None
                                for g2 in range(2):
                                    if 2 * idx + g2 >= 2:
                                        ac.wait(tb, ac.n + g2 - 1)
                                    for f in range(NF):
                                        last = tb.matmul(
                                            psA[(2 * idx + g2) % 2][:, :],
                                            w_st[:, slot * NF * 128 + f * 128:
                                                 slot * NF * 128 + (f + 1) * 128],
                                            x_fm[:, f * T + g2 * 512: f * T + (g2 + 1) * 512],
                                            start=(f == 0), stop=(f == NF - 1))
                                    mm.inc(last)

                            @block.scalar
                            def _(s, m=m, idx=idx, slot=slot, bcol_i=bcol_i, kind=kind):
                                base = mm.n - 2
                                if idx == 0:
                                    s.wait_ge(vx.h, vx.n)
                                for g2 in range(2):
                                    s.wait_ge(mm.h, base + g2 + 1)
                                    if kind == "k":
                                        dstap = kv_stage[:, m * T + g2 * 512:
                                                         m * T + (g2 + 1) * 512]
                                    else:
                                        dstap = vfm[:, slot * T + g2 * 512:
                                                    slot * T + (g2 + 1) * 512]
                                    ac.inc(s.add(dstap,
                                                 psA[(2 * idx + g2) % 2][:, :],
                                                 bias4[:, bcol_i * 16 + m:
                                                       bcol_i * 16 + m + 1]))

                            if kind == "v" and _DEBUG_TAP == "vfm" and m == _DEBUG_H:
                                @block.gpsimd
                                def _(g, slot=slot):
                                    g.wait_ge(ac.h, ac.n)
                                    io.dma(g.dma_start(out=dbg[:, 0:1024],
                                                       in_=vfm[:, slot * T:(slot + 1) * T]))
                                    io.wait(g)

                            if kind == "v":
                                @block.tensor
                                def _(tb, m=m, idx=idx, slot=slot):
                                    ac.wait(tb)
                                    if m >= 2:
                                        vx.wait(tb, vx.n - 1)
                                    last = None
                                    for t in range(NT):
                                        last = tb.transpose(
                                            psB[slot][:, t * 128:(t + 1) * 128],
                                            vfm[:, slot * T + t * 128:
                                                slot * T + (t + 1) * 128],
                                            ident_bf[:])
                                    mm.inc(last)

                                @block.vector
                                def _(v, m=m, slot=slot):
                                    mm.wait(v)
                                    if m == 0:
                                        v.wait_ge(io.h, io.n)
                                    last = None
                                    for t in range(NT):
                                        last = v.tensor_copy(
                                            kv_stage[:, t * 2048 + m * 128:
                                                     t * 2048 + (m + 1) * 128],
                                            psB[slot][:, t * 128:(t + 1) * 128])
                                    vx.inc(last)

                                if _DEBUG_TAP == "vfm" and m == _DEBUG_H:
                                    @block.gpsimd
                                    def _(g, m=m):
                                        g.wait_ge(vx.h, vx.n)
                                        src = kv_stage[:, :].rearrange(
                                            "p (t c) -> p t c", t=NT)
                                        io.dma(g.dma_start(
                                            out=dbg[:, 1024:2048].rearrange(
                                                "p (t c) -> p t c", t=NT),
                                            in_=src[:, :, m * 128:(m + 1) * 128]))
                                        io.wait(g)

                    emit_proj3(wk1, 1, "k")

                    @block.gpsimd
                    def _(g):
                        g.wait_ge(ac.h, ac.n)
                        par = g.partition_id() % 2
                        dstk = sh_k.ap().rearrange("(m p) (h t) -> h p m t", h=2, p=128)
                        io.dma(g.dma_start(out=dstk[ds(par, 1), :, :, :],
                                           in_=kv_stage[:, :]))

                    emit_proj3(wv1, 2, "v")

                    if _DEBUG_TAP == "vstage":
                        @block.gpsimd
                        def _(g):
                            g.wait_ge(vx.h, vx.n)
                            io.dma(g.dma_start(out=dbg[:, :], in_=kv_stage[:, :]))
                            io.wait(g)

                    @block.gpsimd
                    def _(g):
                        g.wait_ge(vx.h, vx.n)
                        par = g.partition_id() % 2
                        dstv = sh_v.ap().rearrange("(tt tp p) c -> tt p tp c", tt=2, p=128)
                        io.dma(g.dma_start(out=dstv[ds(par, 1), :, :, :],
                                           in_=kv_stage[:, :]))

                    @block.gpsimd
                    def _(g):
                        io.wait(g)
                        cc.inc(g.collective_compute(
                            "AllGather", ALU.bypass,
                            replica_groups=[list(range(NCORES))],
                            ins=[bar_in.ap()], outs=[bar_out.ap()]))

                    with nc.sbuf_tensor("kval_i", [128, 16], I32) as kval_i, \
                         nc.sbuf_tensor("ij_i", [128, 512], I32) as ij_i, \
                         nc.sbuf_tensor("ijp_i", [128, 512], I32) as ijp_i, \
                         nc.sbuf_tensor("ijp_f", [128, 512], F32) as ijp_f:

                        @block.gpsimd
                        def _(g):
                            cc.wait(g)
                            ini.inc(g.iota(kval_i[:, :], pattern=[[128, 16]], base=0,
                                           channel_multiplier=1))
                            ini.inc(g.iota(ij_i[:, :], pattern=[[1, 512]], base=0,
                                           channel_multiplier=0))
                            ini.inc(g.iota(ijp_i[:, :], pattern=[[1, 512]], base=0,
                                           channel_multiplier=-1))
                            ini.dma(g.dma_start(out=ci_row[0:1, :],
                                                in_=cinfo[0:1, :]))

                        @block.vector
                        def _(v):
                            ini.wait(v)
                            v.tensor_copy(kval_f[:, :], kval_i[:, :])
                            v.tensor_copy(ij_f[:, :], ij_i[:, :])
                            v.tensor_copy(ijp_f[:, :], ijp_i[:, :])
                            v.memset(inv_col[:, :], INVSQ)
                            for i, mi in enumerate([0, 1, 2, 3, 8, 9, 10, 11]):
                                v.memset(thr_row[0:1, i:i + 1], float(mi * 128))
                            for h in range(16):
                                v.tensor_scalar_mul(kcol[:, h * 16:(h + 1) * 16],
                                                    kval_f[:, :], SLOPES[h])
                            vx.inc(v.engine_nop())

                        @block.vector
                        def _(v):
                            vx.wait(v)
                            v.tensor_single_scalar(thr_row2[0:1, 0:8], thr_row[0:1, :],
                                                   ci_row[0:1, 2:3], ALU.subtract)
                            vx.inc(v.engine_nop())

                        @block.tensor
                        def _(tb):
                            vx.wait(tb)
                            tb.matmul(psS[0][:, 0:8], ones_f32[0:1, :],
                                      thr_row2[0:1, 0:8], start=True, stop=True)
                            mm.inc(tb.matmul(psS[0][:, 8:16], ones_f32[0:1, :],
                                             ci_row[0:1, 0:8], start=True, stop=True))

                        @block.vector
                        def _(v):
                            mm.wait(v)
                            v.tensor_copy(thr8[:, :], psS[0][:, 0:8])
                            v.tensor_copy(par_bc[:, 0:2], psS[0][:, 8:10])
                            v.tensor_copy(pen_col[:, 0:2], psS[0][:, 12:14])
                            for i in range(8):
                                v.tensor_single_scalar(masks8[:, i * 512:(i + 1) * 512],
                                                       ijp_f[:, :], thr8[:, i:i + 1],
                                                       ALU.is_ge)
                            vx.inc(v.engine_nop())

                BOUND = set([0, 1, 2, 3, 8, 9, 10, 11])
                MIDX = {0: 0, 1: 1, 2: 2, 3: 3, 8: 4, 9: 5, 10: 6, 11: 7}
                cc_bar = cc.n
                qev_ac = []

                with nc.sbuf_tensor("w_st2", [128, 2 * NF * 128], BF16) as w_st2, \
                     nc.sbuf_tensor("q_h", [128, 2 * T], BF16) as q_h, \
                     nc.sbuf_tensor("k_h", [128, 2 * 2048], BF16) as k_h, \
                     nc.sbuf_tensor("v_h", [128, 2 * 2048], BF16) as v_h, \
                     nc.sbuf_tensor("pt_sb", [128, 16 * 512], BF16) as pt_sb, \
                     nc.sbuf_tensor("argb", [128, 2 * 512], F32) as argb, \
                     nc.sbuf_tensor("sum_row", [1, 512], F32) as sum_row, \
                     nc.sbuf_tensor("rb_sb", [128, 512], F32) as rb_sb:

                    for h in range(16):
                        hs2 = h % 2

                        @block.gpsimd
                        def _(g, h=h, hs2=hs2):
                            if h >= 2:
                                g.wait_ge(ac.h, qev_ac[h - 2])
                            srcw = wq1[:, h * 128:(h + 1) * 128]
                            srcw = srcw.rearrange("(f p) c -> p f c", p=128)
                            wul.dma(g.dma_start(
                                out=w_st2[:, hs2 * NF * 128:(hs2 + 1) * NF * 128],
                                in_=srcw))

                        @block.sync
                        def _(s, h=h, hs2=hs2):
                            if h >= 2:
                                s.wait_ge(dnc.h, dnc.n - 1)
                            else:
                                s.wait_ge(cc.h, cc_bar)
                            wdl.dma(s.dma_start(out=k_h[:, hs2 * 2048:(hs2 + 1) * 2048],
                                                in_=sh_k.ap()[h * 128:(h + 1) * 128, :]))
                            srcv = sh_v.ap()[:, h * 128:(h + 1) * 128]
                            srcv = srcv.rearrange("(kt p) c -> p kt c", p=128)
                            wdl.dma(s.dma_start(out=v_h[:, hs2 * 2048:(hs2 + 1) * 2048],
                                                in_=srcv))

                        @block.tensor
                        def _(tb, h=h, hs2=hs2):
                            wul.wait(tb)
                            vx.wait(tb)
                            ac.wait(tb)
                            last = None
                            for g2 in range(2):
                                for f in range(NF):
                                    last = tb.matmul(
                                        psA[g2][:, :],
                                        w_st2[:, hs2 * NF * 128 + f * 128:
                                              hs2 * NF * 128 + (f + 1) * 128],
                                        x_fm[:, f * T + g2 * 512: f * T + (g2 + 1) * 512],
                                        start=(f == 0), stop=(f == NF - 1))
                                mm.inc(last)

                        @block.scalar
                        def _(s, h=h, hs2=hs2):
                            base = mm.n - 2
                            for g2 in range(2):
                                s.wait_ge(mm.h, base + g2 + 1)
                                if h >= 2:
                                    s.wait_ge(dnc.h, dnc.n - 1)
                                ac.inc(s.add(
                                    q_h[:, hs2 * T + g2 * 512: hs2 * T + (g2 + 1) * 512],
                                    psA[g2][:, :],
                                    bias4[:, 0 * 16 + h: 0 * 16 + h + 1]))
                            qev_ac.append(ac.n)

                        if _DEBUG_TAP == "attn" and h == _DEBUG_H:
                            @block.gpsimd
                            def _(g, hs2=hs2):
                                g.wait_ge(ac.h, ac.n)
                                g.wait_ge(wdl.h, wdl.n)
                                io.dma(g.dma_start(out=dbg[:, 11264:11264 + 1024],
                                                   in_=q_h[:, hs2 * T:hs2 * T + 1024]))
                                kvsrc = k_h if _DEBUG_KV == "k" else v_h
                                io.dma(g.dma_start(out=dbg[:, 12288:12288 + 2048],
                                                   in_=kvsrc[:, hs2 * 2048:hs2 * 2048 + 2048]))
                                io.wait(g)

                        for qg in range(2):
                            q0 = qg * 512
                            nkc = 12 if qg == 0 else 16

                            @block.vector
                            def _(v, h=h, qg=qg, q0=q0):
                                v.tensor_scalar(jb[:, :], ij_f[:, :], SLOPES[h],
                                                SLOPES[h] * q0 + CEXP,
                                                op0=ALU.mult, op1=ALU.add)
                                v.scalar_tensor_tensor(
                                    jb[:, :], par_bc[:, 0:1].to_broadcast([128, 512]),
                                    SLOPES[h] * 1024.0, jb[:, :],
                                    op0=ALU.mult, op1=ALU.add)
                                v.tensor_scalar_add(jbx[:, :], jb[:, :], pen_col[:, 0:1])
                                vx.inc(v.engine_nop())

                            stt_cnt = {}
                            exp_cnt = {}

                            @block.tensor
                            def _(tb, h=h, hs2=hs2, qg=qg, q0=q0, nkc=nkc,
                                  stt_cnt=stt_cnt):
                                wdl.wait(tb, 16 * 2 * (h + 1))
                                ac.wait(tb)
                                vx.wait(tb)
                                base_stt = vx.n
                                for kc in range(nkc):
                                    if kc >= 2:
                                        tb.wait_ge(vx.h, base_stt + kc - 1)
                                    mm.inc(tb.matmul(
                                        psA[kc % 2][:, :],
                                        k_h[:, hs2 * 2048 + kc * 128:
                                            hs2 * 2048 + (kc + 1) * 128],
                                        q_h[:, hs2 * T + q0: hs2 * T + q0 + 512],
                                        start=True, stop=True))

                            @block.vector
                            def _(v, h=h, qg=qg, nkc=nkc, stt_cnt=stt_cnt):
                                base = mm.n - nkc
                                base_ac = ac.n
                                for kc in range(nkc):
                                    mi = kc - 4 * qg
                                    v.wait_ge(mm.h, base + kc + 1)
                                    v.wait_ge(ac.h, base_ac + max(0, kc - 1))
                                    jsel = jbx if mi >= 4 else jb
                                    v.scalar_tensor_tensor(
                                        argb[:, (kc % 2) * 512:(kc % 2 + 1) * 512],
                                        psA[kc % 2][:, :], inv_col[:, 0:1], jsel[:, :],
                                        op0=ALU.mult, op1=ALU.subtract)
                                    if mi in BOUND:
                                        v.tensor_scalar(
                                            argb[:, (kc % 2) * 512:(kc % 2 + 1) * 512],
                                            argb[:, (kc % 2) * 512:(kc % 2 + 1) * 512],
                                            kcol[:, h * 16 + kc: h * 16 + kc + 1], 0.0,
                                            op0=ALU.add, op1=ALU.min)
                                    vx.inc(v.engine_nop())
                                    stt_cnt[kc] = vx.n

                            @block.scalar
                            def _(s, h=h, qg=qg, nkc=nkc, stt_cnt=stt_cnt,
                                  exp_cnt=exp_cnt):
                                for kc in range(nkc):
                                    mi = kc - 4 * qg
                                    s.wait_ge(vx.h, stt_cnt[kc])
                                    if mi in BOUND:
                                        ins = s.activation(
                                            pt_sb[:, kc * 512:(kc + 1) * 512],
                                            argb[:, (kc % 2) * 512:(kc % 2 + 1) * 512],
                                            AF.Exp)
                                    else:
                                        ins = s.activation(
                                            pt_sb[:, kc * 512:(kc + 1) * 512],
                                            argb[:, (kc % 2) * 512:(kc % 2 + 1) * 512],
                                            AF.Exp,
                                            bias=kcol[:, h * 16 + kc: h * 16 + kc + 1])
                                    ac.inc(ins)
                                    exp_cnt[kc] = ac.n

                            @block.vector
                            def _(v, qg=qg, nkc=nkc, exp_cnt=exp_cnt):
                                for kc in range(nkc):
                                    mi = kc - 4 * qg
                                    if mi in BOUND:
                                        v.wait_ge(ac.h, exp_cnt[kc])
                                        v.tensor_mul(
                                            pt_sb[:, kc * 512:(kc + 1) * 512],
                                            pt_sb[:, kc * 512:(kc + 1) * 512],
                                            masks8[:, MIDX[mi] * 512:(MIDX[mi] + 1) * 512])
                                vx.inc(v.engine_nop())

                            if _DEBUG_TAP == "attn" and h == _DEBUG_H and qg == 1:
                                @block.gpsimd
                                def _(g, nkc=nkc):
                                    ac.wait(g)
                                    g.wait_ge(vx.h, vx.n)
                                    io.dma(g.dma_start(out=dbg[:, 0:8192], in_=pt_sb[:, 0:8192]))
                                    io.dma(g.dma_start(out=dbg[:, 8192:8704], in_=jb[:, :]))
                                    io.dma(g.dma_start(out=dbg[:, 8704:9216], in_=jbx[:, :]))
                                    io.dma(g.dma_start(out=dbg[:, 9216:10240], in_=argb[:, :]))
                                    io.dma(g.dma_start(out=dbg[:, 10240:10248], in_=thr8[:, :]))
                                    io.dma(g.dma_start(out=dbg[:, 10248:10250], in_=par_bc[:, 0:2]))
                                    io.dma(g.dma_start(out=dbg[:, 10250:10252], in_=pen_col[:, 0:2]))
                                    io.dma(g.dma_start(out=dbg[:, 10252:10508], in_=kcol[:, 0:256]))
                                    io.wait(g)

                            @block.tensor
                            def _(tb, h=h, hs2=hs2, qg=qg, nkc=nkc):
                                vx.wait(tb)
                                ac.wait(tb)
                                for kc in range(nkc):
                                    tb.matmul(psS[0][0:1, :], ones_bf[:, 0:1],
                                              pt_sb[:, kc * 512:(kc + 1) * 512],
                                              start=(kc == 0), stop=(kc == nkc - 1))
                                last = None
                                for kc in range(nkc):
                                    last = tb.matmul(
                                        psA[2][:, :],
                                        v_h[:, hs2 * 2048 + kc * 128:
                                            hs2 * 2048 + (kc + 1) * 128],
                                        pt_sb[:, kc * 512:(kc + 1) * 512],
                                        start=(kc == 0), stop=(kc == nkc - 1))
                                mm.inc(last)

                            @block.vector
                            def _(v):
                                mm.wait(v)
                                vx.inc(v.tensor_copy(sum_row[0:1, :], psS[0][0:1, :]))
                                v.reciprocal(sum_row[0:1, :], sum_row[0:1, :])
                                vx.inc(v.engine_nop())

                            @block.tensor
                            def _(tb):
                                vx.wait(tb)
                                mm.inc(tb.matmul(psS[1][:, :], ones_f32[0:1, :],
                                                 sum_row[0:1, :], start=True, stop=True))

                            @block.vector
                            def _(v, h=h, qg=qg):
                                mm.wait(v)
                                v.tensor_copy(rb_sb[:, :], psS[1][:, :])
                                vx.inc(v.tensor_mul(
                                    ctx_loc[:, h * T + qg * 512: h * T + (qg + 1) * 512],
                                    psA[2][:, :], rb_sb[:, :]))

                            if _DEBUG_TAP == "attn" and h == _DEBUG_H and qg == 1:
                                @block.gpsimd
                                def _(g):
                                    g.wait_ge(vx.h, vx.n)
                                    io.dma(g.dma_start(out=dbg[:, 14400:14912],
                                                       in_=rb_sb[:, :]))
                                    io.wait(g)

                        @block.vector
                        def _(v):
                            dnc.inc(v.engine_nop())

                if _DEBUG_TAP == "ctx":
                    @block.gpsimd
                    def _(g):
                        g.wait_ge(vx.h, vx.n)
                        io.dma(g.dma_start(out=dbg[:, 0:NF * T], in_=ctx_loc[:, :]))
                        io.wait(g)

                with nc.sbuf_tensor("w_st3", [128, 2 * NF * 128], BF16) as w_st3, \
                     nc.sbuf_tensor("oproj_st", [128, 2 * T], BF16) as oproj_st:

                    o_gate_mm = mm.n
                    o_gate_vx = vx.n
                    for m in range(NF):
                        slot = m % 2

                        @block.gpsimd
                        def _(g, m=m, slot=slot):
                            if m < 2:
                                g.wait_ge(mm.h, o_gate_mm)
                                g.wait_ge(vx.h, o_gate_vx)
                            else:
                                g.wait_ge(ac.h, ac.n - 2)
                            srcw = wo1[:, m * 128:(m + 1) * 128]
                            srcw = srcw.rearrange("(f p) c -> p f c", p=128)
                            wul.dma(g.dma_start(
                                out=w_st3[:, slot * NF * 128:(slot + 1) * NF * 128],
                                in_=srcw))

                        @block.tensor
                        def _(tb, m=m, slot=slot):
                            wul.wait(tb)
                            if m == 0:
                                ac.wait(tb)
                                vx.wait(tb)
                            last = None
                            for g2 in range(2):
                                if 2 * m + g2 >= 2:
                                    ac.wait(tb, ac.n + g2 - 1)
                                for f in range(NF):
                                    last = tb.matmul(
                                        psA[(2 * m + g2) % 2][:, :],
                                        w_st3[:, slot * NF * 128 + f * 128:
                                              slot * NF * 128 + (f + 1) * 128],
                                        ctx_loc[:, f * T + g2 * 512: f * T + (g2 + 1) * 512],
                                        start=(f == 0), stop=(f == NF - 1))
                                mm.inc(last)

                        @block.scalar
                        def _(s, m=m, slot=slot):
                            base = mm.n - 2
                            base_v = vx.n
                            for g2 in range(2):
                                s.wait_ge(mm.h, base + g2 + 1)
                                if m >= 2:
                                    s.wait_ge(vx.h, base_v + g2 - 3)
                                ac.inc(s.add(
                                    oproj_st[:, slot * T + g2 * 512:
                                             slot * T + (g2 + 1) * 512],
                                    psA[(2 * m + g2) % 2][:, :],
                                    bias4[:, 3 * 16 + m: 3 * 16 + m + 1]))

                        @block.vector
                        def _(v, m=m, slot=slot):
                            base = ac.n - 2
                            for g2 in range(2):
                                v.wait_ge(ac.h, base + g2 + 1)
                                vx.inc(v.tensor_add(
                                    z_bf[:, m * T + g2 * 512: m * T + (g2 + 1) * 512],
                                    x_fm[:, m * T + g2 * 512: m * T + (g2 + 1) * 512],
                                    oproj_st[:, slot * T + g2 * 512:
                                             slot * T + (g2 + 1) * 512]))

            emit_layernorm()

        if _N_PHASES >= 4:
            NH = 44
            p4 = ExitStack()
            with p4:
                E4 = p4.enter_context
                h_act = E4(nc.sbuf_tensor("h_act", [128, NH * T], BF16))
                scl_sb = E4(nc.sbuf_tensor("scl_sb", [128, 144], F32))
                m128 = E4(nc.sbuf_tensor("m128", [1, 128], F32))
                sr_row = E4(nc.sbuf_tensor("sr_row", [1, T], F32))

                p4_mm = mm.n
                p4_vx = vx.n
                p4_ac = ac.n

                with nc.sbuf_tensor("sgr", [44, 256], F32) as sgr, \
                     nc.sbuf_tensor("sdr", [16, 128], F32) as sdr, \
                     nc.sbuf_tensor("xneg", [128, T], F32) as xneg:

                    @block.gpsimd
                    def _(g):
                        g.wait_ge(mm.h, p4_mm)
                        g.wait_ge(vx.h, p4_vx)
                        g.wait_ge(ac.h, p4_ac)
                        ld.dma(g.dma_start(out=sgr[:, 0:128], in_=sg_in[:, :]))
                        ld.dma(g.dma_start(out=sgr[:, 128:256], in_=su_in[:, :]))
                        ld.dma(g.dma_start(out=sdr[:, :], in_=sd_in[:, :]))

                    @block.vector
                    def _(v):
                        v.wait_ge(mm.h, p4_mm)
                        vx.inc(v.memset(m128[0:1, :], -128.0))

                    @block.tensor
                    def _(tb):
                        ld.wait(tb)
                        vx.wait(tb)
                        ac.wait(tb)
                        tb.transpose(psS[0][:, 0:44], sgr[:, 0:128],
                                     ident_f32[0:44, 0:44])
                        tb.transpose(psS[0][:, 64:108], sgr[:, 128:256],
                                     ident_f32[0:44, 0:44])
                        tb.transpose(psS[0][:, 128:144], sdr[:, :],
                                     ident_f32[0:16, 0:16])
                        last = None
                        for f in range(NF):
                            tb.matmul(psS[1][0:1, :], ones_bf[:, 0:1],
                                      x_fm[:, f * T: f * T + 512],
                                      start=(f == 0), stop=(f == NF - 1))
                            last = tb.matmul(psA[0][0:1, :], ones_bf[:, 0:1],
                                             x_fm[:, f * T + 512:(f + 1) * T],
                                             start=(f == 0), stop=(f == NF - 1))
                        mm.inc(last)

                    @block.vector
                    def _(v):
                        mm.wait(v)
                        v.tensor_copy(scl_sb[:, :], psS[0][:, 0:144])
                        v.tensor_copy(sr_row[0:1, 0:512], psS[1][0:1, :])
                        vx.inc(v.tensor_copy(sr_row[0:1, 512:1024],
                                             psA[0][0:1, :]))

                    @block.tensor
                    def _(tb):
                        vx.wait(tb)
                        tb.matmul(psS[0][:, :], m128[0:1, :], sr_row[0:1, 0:512],
                                  start=True, stop=True)
                        mm.inc(tb.matmul(psS[1][:, :], m128[0:1, :],
                                         sr_row[0:1, 512:1024],
                                         start=True, stop=True))

                    @block.vector
                    def _(v):
                        mm.wait(v)
                        v.tensor_copy(xneg[:, 0:512], psS[0][:, :])
                        vx.inc(v.tensor_copy(xneg[:, 512:1024], psS[1][:, :]))

                    with nc.sbuf_tensor("wT4", [128, 2 * D], BF16) as wT4, \
                         nc.sbuf_tensor("sgb4", [128, T], BF16) as sgb4, \
                         nc.sbuf_tensor("t1g", [128, T], F32) as t1g:

                        ldg_wul = []
                        gm_mm = {}
                        um_mm = {}
                        qm_mm = []
                        evg_vx = {}
                        evu_vx = {}
                        sil_ac = {}

                        for hc in range(NH):
                            for mi, wsrc in ((0, wqgT), (1, wquT)):
                                seq = 2 * hc + mi
                                slot = seq % 2

                                @block.gpsimd
                                def _(g, hc=hc, seq=seq, slot=slot, wsrc=wsrc):
                                    if seq == 0:
                                        g.wait_ge(mm.h, mm.n)
                                        g.wait_ge(vx.h, vx.n)
                                        g.wait_ge(ac.h, ac.n)
                                    if seq >= 2:
                                        g.wait_ge(mm.h, qm_mm[seq - 2])
                                    srcw = wsrc[:, hc * 128:(hc + 1) * 128]
                                    srcw = srcw.rearrange("(f p) c -> p f c", p=128)
                                    wul.dma(g.dma_start(
                                        out=wT4[:, slot * D:(slot + 1) * D],
                                        in_=srcw))
                                    ldg_wul.append(wul.n)

                                @block.tensor
                                def _(tb, hc=hc, mi=mi, seq=seq, slot=slot):
                                    tb.wait_ge(wul.h, ldg_wul[seq])
                                    if seq == 0:
                                        vx.wait(tb)
                                        ac.wait(tb)
                                    pbase = 2 * mi
                                    for g2 in range(2):
                                        if hc > 0:
                                            ev = evg_vx if mi == 0 else evu_vx
                                            tb.wait_ge(vx.h, ev[(hc - 1, g2)])
                                        last = None
                                        for f in range(NF):
                                            last = tb.matmul(
                                                psA[pbase + g2][:, :],
                                                wT4[:, slot * D + f * 128:
                                                    slot * D + (f + 1) * 128],
                                                x_fm[:, f * T + g2 * 512:
                                                     f * T + (g2 + 1) * 512],
                                                start=(f == 0), stop=(f == NF - 1))
                                        mm.inc(last)
                                        if mi == 0:
                                            gm_mm[(hc, g2)] = mm.n
                                        else:
                                            um_mm[(hc, g2)] = mm.n
                                    qm_mm.append(mm.n)

                            @block.vector
                            def _(v, hc=hc):
                                for g2 in range(2):
                                    v.wait_ge(mm.h, gm_mm[(hc, g2)])
                                    vx.inc(v.tensor_add(
                                        t1g[:, g2 * 512:(g2 + 1) * 512],
                                        psA[g2][:, :],
                                        xneg[:, g2 * 512:(g2 + 1) * 512]))
                                    evg_vx[(hc, g2)] = vx.n

                            @block.scalar
                            def _(s, hc=hc):
                                for g2 in range(2):
                                    s.wait_ge(vx.h, evg_vx[(hc, g2)])
                                    ac.inc(s.activation(
                                        sgb4[:, g2 * 512:(g2 + 1) * 512],
                                        t1g[:, g2 * 512:(g2 + 1) * 512],
                                        AF.Silu, scale=scl_sb[:, hc:hc + 1]))
                                    sil_ac[(hc, g2)] = ac.n

                            @block.vector
                            def _(v, hc=hc):
                                for g2 in range(2):
                                    v.wait_ge(ac.h, sil_ac[(hc, g2)])
                                    v.wait_ge(mm.h, um_mm[(hc, g2)])
                                    vx.inc(v.tensor_add(
                                        t1g[:, g2 * 512:(g2 + 1) * 512],
                                        psA[2 + g2][:, :],
                                        xneg[:, g2 * 512:(g2 + 1) * 512]))
                                    evu_vx[(hc, g2)] = vx.n
                                    vx.inc(v.scalar_tensor_tensor(
                                        h_act[:, hc * T + g2 * 512:
                                              hc * T + (g2 + 1) * 512],
                                        t1g[:, g2 * 512:(g2 + 1) * 512],
                                        scl_sb[:, 64 + hc: 64 + hc + 1],
                                        sgb4[:, g2 * 512:(g2 + 1) * 512],
                                        op0=ALU.mult, op1=ALU.mult))

                with nc.sbuf_tensor("hneg", [128, T], F32) as hneg, \
                     nc.sbuf_tensor("wdT", [128, 2 * 5632], BF16) as wdT, \
                     nc.sbuf_tensor("t1d", [128, 512], F32) as t1d:

                    @block.tensor
                    def _(tb):
                        vx.wait(tb)
                        last = None
                        for j in range(NH):
                            tb.matmul(psS[0][0:1, :], ones_bf[:, 0:1],
                                      h_act[:, j * T: j * T + 512],
                                      start=(j == 0), stop=(j == NH - 1))
                            last = tb.matmul(psS[1][0:1, :], ones_bf[:, 0:1],
                                             h_act[:, j * T + 512:(j + 1) * T],
                                             start=(j == 0), stop=(j == NH - 1))
                        mm.inc(last)

                    @block.vector
                    def _(v):
                        mm.wait(v)
                        v.tensor_copy(sr_row[0:1, 0:512], psS[0][0:1, :])
                        vx.inc(v.tensor_copy(sr_row[0:1, 512:1024],
                                             psS[1][0:1, :]))

                    @block.tensor
                    def _(tb):
                        vx.wait(tb)
                        tb.matmul(psS[0][:, :], m128[0:1, :], sr_row[0:1, 0:512],
                                  start=True, stop=True)
                        mm.inc(tb.matmul(psS[1][:, :], m128[0:1, :],
                                         sr_row[0:1, 512:1024],
                                         start=True, stop=True))

                    @block.vector
                    def _(v):
                        mm.wait(v)
                        v.tensor_copy(hneg[:, 0:512], psS[0][:, :])
                        vx.inc(v.tensor_copy(hneg[:, 512:1024], psS[1][:, :]))

                    ldd_wul = []
                    dnm_mm = {}
                    evd_vx = {}

                    for dc in range(NF):
                        @block.gpsimd
                        def _(g, dc=dc):
                            if dc <= 1:
                                g.wait_ge(vx.h, vx.n)
                                g.wait_ge(mm.h, mm.n)
                            else:
                                g.wait_ge(mm.h, dnm_mm[(dc - 2, 1)])
                            srcw = wqdT[:, dc * 128:(dc + 1) * 128]
                            srcw = srcw.rearrange("(j p) c -> p j c", p=128)
                            wul.dma(g.dma_start(
                                out=wdT[:, (dc % 2) * 5632:(dc % 2 + 1) * 5632],
                                in_=srcw))
                            ldd_wul.append(wul.n)

                        @block.tensor
                        def _(tb, dc=dc):
                            tb.wait_ge(wul.h, ldd_wul[dc])
                            for g2 in range(2):
                                if dc > 0:
                                    tb.wait_ge(vx.h, evd_vx[(dc - 1, g2)])
                                else:
                                    vx.wait(tb)
                                last = None
                                for j in range(NH):
                                    last = tb.matmul(
                                        psA[g2][:, :],
                                        wdT[:, (dc % 2) * 5632 + j * 128:
                                            (dc % 2) * 5632 + (j + 1) * 128],
                                        h_act[:, j * T + g2 * 512:
                                              j * T + (g2 + 1) * 512],
                                        start=(j == 0), stop=(j == NH - 1))
                                mm.inc(last)
                                dnm_mm[(dc, g2)] = mm.n

                        @block.vector
                        def _(v, dc=dc):
                            for g2 in range(2):
                                v.wait_ge(mm.h, dnm_mm[(dc, g2)])
                                v.tensor_add(t1d[:, :], psA[g2][:, :],
                                             hneg[:, g2 * 512:(g2 + 1) * 512])
                                vx.inc(v.scalar_tensor_tensor(
                                    z_bf[:, dc * T + g2 * 512:
                                         dc * T + (g2 + 1) * 512],
                                    t1d[:, :],
                                    scl_sb[:, 128 + dc: 128 + dc + 1],
                                    x_fm[:, dc * T + g2 * 512:
                                         dc * T + (g2 + 1) * 512],
                                    op0=ALU.mult, op1=ALU.add))
                                evd_vx[(dc, g2)] = vx.n

            emit_layernorm()

        with nc.sbuf_tensor("out_sb", [128, 2 * D], F32) as out_sb:
            @block.tensor
            def _(tb):
                vx.wait(tb)
                for t in range(NT):
                    for half in range(2):
                        i = 2 * t + half
                        if i >= 2:
                            vx.wait(tb, vx.n + i - 1)
                        pb = psB[half]
                        last = None
                        for f4 in range(4):
                            f = half * 8 + 2 * f4
                            last = tb.transpose(pb[:, f4 * 256:f4 * 256 + 128],
                                                x_fm[:, f * T + t * 128: f * T + (t + 1) * 128],
                                                ident_bf[:])
                            last = tb.transpose(pb[:, f4 * 256 + 128:(f4 + 1) * 256],
                                                x_fm[:, (f + 1) * T + t * 128:(f + 1) * T + (t + 1) * 128],
                                                ident_bf[:])
                        mm.inc(last)

            @block.vector
            def _(v):
                base = mm.n - 2 * NT
                io_base = io.n
                for t in range(NT):
                    if t >= 2:
                        v.wait_ge(io.h, io_base + (t - 1) * 16)
                    for half in range(2):
                        i = 2 * t + half
                        v.wait_ge(mm.h, base + i + 1)
                        vx.inc(v.tensor_copy(
                            out_sb[:, (t % 2) * D + half * 1024:(t % 2) * D + (half + 1) * 1024],
                            psB[half][:, :]))

            @block.sync
            def _(s):
                base = vx.n - 2 * NT
                for t in range(NT):
                    s.wait_ge(vx.h, base + 2 * (t + 1))
                    io.dma(s.dma_start(out=out[t * 128:(t + 1) * 128, :],
                                       in_=out_sb[:, (t % 2) * D:(t % 2 + 1) * D]))
                io.wait(s)

    return nc


def kernel(**inputs):
    u_full = np.ascontiguousarray(inputs["L0_attn_u"].reshape(8192, D))
    afac = np.ascontiguousarray(inputs["L0_attn_A_factors"].reshape(1, 32))
    dvec = np.ascontiguousarray(inputs["L0_attn_D_vec"].reshape(NF, 128))
    wqgT_h = np.ascontiguousarray(inputs["L1_ffn_wq_gate"].T)
    wquT_h = np.ascontiguousarray(inputs["L1_ffn_wq_up"].T)
    wqdT_h = np.ascontiguousarray(inputs["L1_ffn_wq_down"].T)

    nc = build_program()
    in_maps = []
    for c in range(NCORES):
        m = {
            "u": u_full[c * T:(c + 1) * T],
            "wbr": inputs["L0_attn_W_b_rank"],
            "wbs": inputs["L0_attn_W_b_state"],
            "wcr": inputs["L0_attn_W_c_rank"],
            "wcm": inputs["L0_attn_W_c_model"],
            "afac": afac,
            "dvec": dvec,
        }
        m.update({
            "wg": inputs["L0_ffn_w_gate"],
            "wsu": inputs["L0_ffn_w_shared_up"],
            "wsd": inputs["L0_ffn_w_shared_down"],
            "wup": inputs["L0_ffn_w_up"],
            "wdn": inputs["L0_ffn_w_down"],
            "wq1": inputs["L1_attn_Wq"],
            "wk1": inputs["L1_attn_Wk"],
            "wv1": inputs["L1_attn_Wv"],
            "wo1": inputs["L1_attn_Wo"],
            "bq1": np.ascontiguousarray(inputs["L1_attn_bq"].reshape(16, 128)),
            "bk1": np.ascontiguousarray(inputs["L1_attn_bk"].reshape(16, 128)),
            "bv1": np.ascontiguousarray(inputs["L1_attn_bv"].reshape(16, 128)),
            "bo1": np.ascontiguousarray(inputs["L1_attn_bo"].reshape(16, 128)),
        })
        m.update({
            "wqgT": wqgT_h,
            "wquT": wquT_h,
            "wqdT": wqdT_h,
            "sg_in": np.ascontiguousarray(
                inputs["L1_ffn_s_gate"].reshape(44, 128)),
            "su_in": np.ascontiguousarray(
                inputs["L1_ffn_s_up"].reshape(44, 128)),
            "sd_in": np.ascontiguousarray(
                inputs["L1_ffn_s_down"].reshape(16, 128)),
        })
        par = c % 2
        pen = (1 - par) * 1.0e4
        m["cinfo"] = np.array([[par, par, 1024.0 * par, 1024.0 * par,
                                pen, pen, float(c), float(c)]], dtype=np.float32)
        in_maps.append(m)
    res = run_bass_kernel_spmd(nc, in_maps, core_ids=list(range(NCORES)))
    global _LAST_RES
    _LAST_RES = res
    out = np.concatenate([res.results[c]["out"] for c in range(NCORES)], axis=0)
    return out.reshape(4, 2048, 2048)
